# version 01
# baseline (speedup 1.0000x reference)
import numpy as np
from contextlib import ExitStack

import concourse.bass as bass
import concourse.mybir as mybir
from concourse.bass_utils import run_bass_kernel_spmd

F32 = mybir.dt.float32
BF16 = mybir.dt.bfloat16
U32 = mybir.dt.uint32
AF = mybir.ActivationFunctionType
ALU = mybir.AluOpType
AX = mybir.AxisListType

NCORE = 8
B, T, D = 4, 8192, 1024
TH = T // 2
NCH = D // 128
LN_EPS = 1e-5

ENGS = ("sync", "gpsimd", "vector", "scalar", "tensor")


class Prog:

    def __init__(self, race=True):
        self.nc = bass.Bass("TRN2", target_bir_lowering=False,
                            detect_race_conditions=race)
        self.q = {e: [] for e in ENGS}
        self.stack = ExitStack()
        self._semcnt = {}
        self._sems = {}
        self._names = set()

    def dram(self, name, shape, dtype=F32, kind="Internal", **kw):
        return self.nc.dram_tensor(name, list(shape), dtype, kind=kind, **kw).ap()

    def sb(self, name, shape, dtype=F32):
        assert name not in self._names, name
        self._names.add(name)
        return self.stack.enter_context(
            self.nc.sbuf_tensor(name, list(shape), dtype))

    def ps(self, name, shape, dtype=F32):
        assert name not in self._names, name
        self._names.add(name)
        return self.stack.enter_context(
            self.nc.psum_tensor(name, list(shape), dtype))

    def sem(self, name):
        if name not in self._sems:
            self._sems[name] = self.stack.enter_context(self.nc.semaphore(name))
            self._semcnt[name] = 0
        return self._sems[name]

    def op(self, eng, fn, *, inc=None, incn=1, wait=None):
        waits = []
        if wait is not None:
            waits = [wait] if isinstance(wait, tuple) else [w for w in wait if w]
        for wn, wv in waits:
            self.sem(wn)
        semh = None
        if inc is not None:
            semh = self.sem(inc)
            self._semcnt[inc] += incn

        def run(e, fn=fn, waits=waits, semh=semh, incn=incn):
            for (wn, wv) in waits:
                e.wait_ge(self._sems[wn], wv)
            ins = fn(e)
            if semh is not None:
                ins.then_inc(semh, incn)
            return ins
        self.q[eng].append(run)

    def count(self, name):
        self.sem(name)
        return self._semcnt[name]

    def after(self, name):
        return (name, self.count(name))

    def wait(self, eng, name, value=None):
        self.sem(name)
        v = self._semcnt[name] if value is None else value
        self.q[eng].append(lambda e, n=name, v=v: e.wait_ge(self._sems[n], v))

    def sop(self, eng, fn, dma=False):
        w = self.after("seq")
        self.op(eng, fn, inc="seq", incn=16 if dma else 1,
                wait=[w] if w[1] else None)

    def build(self):
        with self.nc.Block() as block:
            for e in ENGS:
                ops = self.q[e]
                if not ops:
                    continue
                getattr(block, e)(lambda eng, ops=ops: [f(eng) for f in ops])
        self.stack.close()
        return self.nc


def cdiv(a, b):
    return -(-a // b)


class PsumPool:

    def __init__(self, p, name, n, shape, dtype=F32):
        self.p = p
        self.tiles = [p.ps(f"{name}{i}", shape, dtype) for i in range(n)]
        self.n = n
        self.k = 0
        self.evsem = name + "_ev"
        p.sem(self.evsem)

    def next(self):
        t = self.tiles[self.k % self.n]
        war = (self.evsem, self.k - self.n + 1) if self.k >= self.n else None
        self.k += 1
        return t, war



def ld_weight(p, dst_sb, w_ap):
    din, dout = w_ap.shape
    full = (din // 128) * 128
    if full:
        src = w_ap[0:full, :].rearrange("(c p) n -> p c n", p=128)
        nch = din // 128
        p.op("gpsimd", lambda e: e.dma_start(
            out=dst_sb[:, 0:nch * dout].rearrange("p (c n) -> p c n", c=nch),
            in_=src), inc="ld", incn=16)
    rem = din - full
    if rem:
        off = (din // 128) * dout
        p.op("gpsimd", lambda e: e.dma_start(
            out=dst_sb[0:rem, off:off + dout], in_=w_ap[full:din, :]),
            inc="ld", incn=16)


def ld_col(p, dst_sb, vec_ap, n=None, poff=0):
    n = vec_ap.shape[0] if n is None else n
    p.op("gpsimd", lambda e: e.dma_start(out=dst_sb[poff:poff + n, 0:1],
                                         in_=vec_ap[0:n][:, None]),
         inc="ld", incn=16)


def ld_row(p, dst_sb, vec_ap, n=None, roff=0):
    n = vec_ap.shape[0] if n is None else n
    p.op("gpsimd", lambda e: e.dma_start(out=dst_sb[0:1, roff:roff + n],
                                         in_=vec_ap[0:n][None, :]),
         inc="ld", incn=16)


def common_tiles(p):
    p._ones_f = p.sb("c_ones_f", [128, 8], F32)
    p._ones_b = p.sb("c_ones_b", [1, 128], BF16)
    p._onesrow_f = p.sb("c_onesrow_f", [1, 512], F32)
    p._eps_col = p.sb("c_eps_col", [1, 1], F32)
    p._identb = p.sb("c_identb", [128, 128], BF16)
    p._onescol_b = p.sb("c_onescol_b", [128, 1], BF16)
    p._rtmp = p.sb("c_rtmp", [1, TH], F32)
    p._iotarow = p.sb("c_iotarow", [1, 128], mybir.dt.int32)
    p._lntmp = p.sb("c_lntmp", [128, 512], F32)
    p.op("vector", lambda e: e.memset(p._ones_f[:, :], 1.0), inc="const")
    p.op("vector", lambda e: e.memset(p._ones_b[0:1, :], 1.0), inc="const")
    p.op("vector", lambda e: e.memset(p._onesrow_f[:, :], 1.0), inc="const")
    p.op("vector", lambda e: e.memset(p._eps_col[:, :], LN_EPS), inc="const")
    p.op("vector", lambda e: e.memset(p._onescol_b[:, :], 1.0), inc="const")
    p.op("gpsimd", lambda e: e.iota(p._iotarow[0:1, 0:128],
                                    pattern=[[1, 128]], base=0,
                                    channel_multiplier=0), inc="const")
    p.wait("tensor", "const")
    p.wait("scalar", "const")


def stats_rows(p, name, x_f, ntok, nch, sq_tmp, sum_row, sq_row, pp, wait=()):
    mmsem = "mm_" + name
    sqsem = "sq_" + name
    nblk = 512
    for tb in range(cdiv(ntok, nblk)):
        n = min(nblk, ntok - tb * nblk)
        for ci in range(nch):
            p.op("scalar", lambda e, ci=ci, tb=tb, n=n: e.activation(
                sq_tmp[:, ci * nblk:ci * nblk + n],
                x_f[:, ci * ntok + tb * nblk:ci * ntok + tb * nblk + n],
                AF.Square),
                inc=sqsem, wait=list(wait) + [p.after("sq_done_" + name)]
                if ci == 0 and tb > 0 else (wait if ci == 0 else None))
        sqv = p.count(sqsem)
        for which in range(2):
            pt, war = pp.next()
            for ci in range(nch):
                w = []
                if ci == 0:
                    w = list(wait) if which == 0 else [(sqsem, sqv)]
                    if war:
                        w.append(war)

                def mmfn(t, pt=pt, ci=ci, tb=tb, n=n, which=which):
                    src = (x_f[:, ci * ntok + tb * nblk:
                               ci * ntok + tb * nblk + n] if which == 0
                           else sq_tmp[:, ci * nblk:ci * nblk + n])
                    return t.matmul(pt[0:1, 0:n], p._ones_f[:, 0:1], src,
                                    start=(ci == 0), stop=(ci == nch - 1))
                p.op("tensor", mmfn, wait=w or None,
                     inc=mmsem if ci == nch - 1 else None)
            dst = sum_row if which == 0 else sq_row
            p.op("vector",
                 lambda e, pt=pt, dst=dst, tb=tb, n=n:
                 e.tensor_copy(dst[0:1, tb * nblk:tb * nblk + n], pt[0:1, 0:n]),
                 wait=(mmsem, p.count(mmsem)),
                 inc=pp.evsem)
        p.op("vector", lambda e: e.engine_nop(), inc="sq_done_" + name)


def rows_meanvar_rstd(p, name, sum_row, sq_row, ntok, mean_row, var_row,
                      rstd_row, negms_row, wait):
    sname = name + "_r"
    p.op("vector", lambda e: e.tensor_scalar_mul(
        mean_row[0:1, 0:ntok], sum_row[0:1, 0:ntok], 1.0 / D), wait=wait)
    p.op("vector", lambda e: e.tensor_scalar_mul(
        var_row[0:1, 0:ntok], sq_row[0:1, 0:ntok], 1.0 / D))
    p.op("vector", lambda e: e.tensor_mul(
        p._rtmp[0:1, 0:ntok], mean_row[0:1, 0:ntok], mean_row[0:1, 0:ntok]))
    p.op("vector", lambda e: e.tensor_sub(
        var_row[0:1, 0:ntok], var_row[0:1, 0:ntok], p._rtmp[0:1, 0:ntok]),
        inc=sname)
    p.op("scalar", lambda e: e.activation(
        p._rtmp[0:1, 0:ntok], var_row[0:1, 0:ntok], AF.Sqrt,
        bias=p._eps_col[0:1, 0:1]), wait=(sname, p.count(sname)), inc=sname)
    p.op("vector", lambda e: e.reciprocal(
        rstd_row[0:1, 0:ntok], p._rtmp[0:1, 0:ntok]),
        wait=(sname, p.count(sname)))
    p.op("vector", lambda e: e.tensor_mul(
        negms_row[0:1, 0:ntok], mean_row[0:1, 0:ntok], rstd_row[0:1, 0:ntok]))
    p.op("vector", lambda e: e.tensor_scalar_mul(
        negms_row[0:1, 0:ntok], negms_row[0:1, 0:ntok], -1.0), inc=sname)
    return (sname, p.count(sname))


def ln_apply(p, name, x_f, ntok, nch, scale_row, negms_row, g_row, b_row,
             out_sb, out_ntok, pp, wait=(), xoff=0, out_off=0, x_is_psumfree=True):
    mmsem = "mm_" + name
    nblk = 512
    for tb in range(cdiv(ntok, nblk)):
        n = min(nblk, ntok - tb * nblk)
        for ci in range(nch):
            pt, war = pp.next()
            w = list(wait)
            if war:
                w.append(war)
            grow = (g_row[0:1, ci * 128:(ci + 1) * 128] if g_row is not None
                    else p._onesrow_f[0:1, 0:128])
            p.op("tensor",
                 lambda t, pt=pt, grow=grow, tb=tb, n=n:
                 t.matmul(pt[:, 0:n], grow,
                          scale_row[0:1, tb * nblk:tb * nblk + n],
                          start=True, stop=True),
                 wait=w or None)
            if negms_row is not None:
                p.op("tensor",
                     lambda t, pt=pt, grow=grow, tb=tb, n=n:
                     t.matmul(pt[:, 512:512 + n], grow,
                              negms_row[0:1, tb * nblk:tb * nblk + n],
                              start=True, stop=False))
            p.op("tensor",
                 lambda t, pt=pt, ci=ci, tb=tb, n=n:
                 t.matmul(pt[:, 512:512 + n],
                          b_row[0:1, ci * 128:(ci + 1) * 128],
                          p._onesrow_f[0:1, 0:n],
                          start=(negms_row is None), stop=True),
                 inc=mmsem)
            mmv = p.count(mmsem)
            p.op("vector",
                 lambda e, pt=pt, ci=ci, tb=tb, n=n:
                 e.tensor_mul(p._lntmp[:, 0:n],
                              x_f[:, ci * ntok + xoff + tb * nblk:
                                  ci * ntok + xoff + tb * nblk + n],
                              pt[:, 0:n]),
                 wait=(mmsem, mmv))
            p.op("vector",
                 lambda e, pt=pt, ci=ci, tb=tb, n=n:
                 e.tensor_add(out_sb[:, ci * out_ntok + out_off + tb * nblk:
                                     ci * out_ntok + out_off + tb * nblk + n],
                              p._lntmp[:, 0:n], pt[:, 512:512 + n]),
                 inc=pp.evsem)


def proj_feat(p, name, wsb, d_in, d_out, xsb, ntok, pp, evac,
              xdt_ntok=None, wait=(), nblk=512):
    XN = ntok if xdt_ntok is None else xdt_ntok
    nchin = cdiv(d_in, 128)
    mmsem = "mm_" + name
    for co in range(cdiv(d_out, 128)):
        cod = min(128, d_out - co * 128)
        for tb in range(cdiv(ntok, nblk)):
            n = min(nblk, ntok - tb * nblk)
            pt, war = pp.next()
            for ci in range(nchin):
                cid = min(128, d_in - ci * 128)
                w = list(wait) if ci == 0 else []
                if ci == 0 and war:
                    w.append(war)
                p.op("tensor",
                     lambda t, pt=pt, ci=ci, co=co, cod=cod, cid=cid, tb=tb, n=n:
                     t.matmul(pt[0:cod, 0:n],
                              wsb[0:cid, ci * d_out + co * 128:
                                  ci * d_out + co * 128 + cod],
                              xsb[0:cid, ci * XN + tb * nblk:
                                  ci * XN + tb * nblk + n],
                              start=(ci == 0), stop=(ci == nchin - 1)),
                     wait=w or None,
                     inc=mmsem if ci == nchin - 1 else None)
            evac(co, tb, cod, n, pt, p.count(mmsem), pp.evsem)


def proj_tok(p, name, xsb, d_in, wsb, d_out, ntok, pp, evac, bias_row,
             xdt_ntok=None, wait=(), nblk=512):
    XN = ntok if xdt_ntok is None else xdt_ntok
    nchin = cdiv(d_in, 128)
    mmsem = "mm_" + name
    for tt in range(cdiv(ntok, 128)):
        tn = min(128, ntok - tt * 128)
        for ob in range(cdiv(d_out, nblk)):
            n = min(nblk, d_out - ob * nblk)
            pt, war = pp.next()
            for ci in range(nchin):
                cid = min(128, d_in - ci * 128)
                w = list(wait) if ci == 0 else []
                if ci == 0 and war:
                    w.append(war)
                p.op("tensor",
                     lambda t, pt=pt, ci=ci, cid=cid, tt=tt, tn=tn, ob=ob, n=n:
                     t.matmul(pt[0:tn, 0:n],
                              xsb[0:cid, ci * XN + tt * 128:
                                  ci * XN + tt * 128 + tn],
                              wsb[0:cid, ci * d_out + ob * nblk:
                                  ci * d_out + ob * nblk + n],
                              start=(ci == 0), stop=False),
                     wait=w or None)
            p.op("tensor",
                 lambda t, pt=pt, tt=tt, tn=tn, ob=ob, n=n:
                 t.matmul(pt[0:tn, 0:n], p._ones_b[0:1, 0:tn],
                          bias_row[0:1, ob * nblk:ob * nblk + n],
                          start=False, stop=True),
                 inc=mmsem)
            evac(tt, ob, tn, n, pt, p.count(mmsem), pp.evsem)



def stream_blocks(p, name, src_featmajor, ntok, nch, bufs, consume,
                  width=512, dst_dtype=None, first_wait=None):
    csem = name + "_c"
    p.sem(csem)
    nb = len(bufs)
    for tb in range(cdiv(ntok, width)):
        n = min(width, ntok - tb * width)
        buf = bufs[tb % nb]
        ldsem = f"{name}_ld{tb % nb}"
        w = [(csem, tb - nb + 1)] if tb >= nb else (list(first_wait) if first_wait else None)
        src = src_featmajor.rearrange("(c p) n -> p c n", p=128)[
            :, :, tb * width:tb * width + n]
        p.op("sync", lambda e, buf=buf, src=src, n=n, nch=nch:
             e.dma_start(out=buf[:, 0:nch * n].rearrange(
                 "p (c n) -> p c n", c=nch), in_=src),
             wait=w, inc=ldsem, incn=16)
        consume(tb, n, buf, (ldsem, p.count(ldsem)))


def build_pre():
    p = Prog()
    xT = p.dram("xT", [D, TH], F32, kind="ExternalInput")
    xn0 = p.dram("xn0", [D, TH], F32, kind="ExternalOutput")
    vy = p.dram("vy", [1, TH], F32, kind="ExternalOutput")

    common_tiles(p)
    xbufs = [p.sb(f"xb{i}", [128, NCH * 512], F32) for i in range(2)]
    sq_tmp = p.sb("sq_tmp", [128, NCH * 512], F32)
    sum_row = p.sb("sum_row", [1, TH], F32)
    sq_row = p.sb("sq_row", [1, TH], F32)
    rstd_row = p.sb("rstd_row", [1, TH], F32)
    negms_row = p.sb("negms_row", [1, TH], F32)
    out_f = p.sb("out_f", [128, 512], F32)

    pp1 = PsumPool(p, "pp1", 2, [1, 512])
    pp2 = PsumPool(p, "pp2", 3, [128, 1024])

    def statsA(tb, n, buf, ldw):
        sqsem = "sqA"
        for ci in range(NCH):
            w = None
            if ci == 0:
                w = [ldw]
                if tb > 0:
                    w.append(("mmA", p.count("mmA")))
            p.op("scalar", lambda e, ci=ci, n=n, buf=buf: e.activation(
                sq_tmp[:, ci * 512:ci * 512 + n],
                buf[:, ci * n:ci * n + n], AF.Square),
                wait=w, inc=sqsem)
        sqv = p.count(sqsem)
        for which in range(2):
            pt, war = pp1.next()
            for ci in range(NCH):
                w = []
                if ci == 0:
                    w = [ldw] if which == 0 else [(sqsem, sqv)]
                    if war:
                        w.append(war)

                def mmfn(t, pt=pt, ci=ci, n=n, which=which, buf=buf):
                    src = (buf[:, ci * n:ci * n + n] if which == 0
                           else sq_tmp[:, ci * 512:ci * 512 + n])
                    return t.matmul(pt[0:1, 0:n], p._ones_f[:, 0:1], src,
                                    start=(ci == 0), stop=(ci == NCH - 1))
                p.op("tensor", mmfn, wait=w or None,
                     inc="mmA" if ci == NCH - 1 else None)
            dst = sum_row if which == 0 else sq_row
            p.op("vector",
                 lambda e, pt=pt, dst=dst, tb=tb, n=n:
                 e.tensor_copy(dst[0:1, tb * 512:tb * 512 + n], pt[0:1, 0:n]),
                 wait=("mmA", p.count("mmA")),
                 inc=pp1.evsem if True else None)
            if which == 1:
                p.op("vector", lambda e: e.engine_nop(), inc="xA_c")

    stream_blocks(p, "xA", xT, TH, NCH, xbufs, statsA)

    rw0 = [p.after("mmA"), (pp1.evsem, pp1.k)]
    p.op("vector", lambda e: e.tensor_scalar_mul(
        sum_row[0:1, 0:TH], sum_row[0:1, 0:TH], 1.0 / D), wait=rw0)
    p.op("vector", lambda e: e.tensor_scalar_mul(
        sq_row[0:1, 0:TH], sq_row[0:1, 0:TH], 1.0 / D))
    p.op("vector", lambda e: e.tensor_mul(
        p._rtmp[0:1, 0:TH], sum_row[0:1, 0:TH], sum_row[0:1, 0:TH]))
    p.op("vector", lambda e: e.tensor_sub(
        sq_row[0:1, 0:TH], sq_row[0:1, 0:TH], p._rtmp[0:1, 0:TH]), inc="mv")
    p.op("scalar", lambda e: e.activation(
        p._rtmp[0:1, 0:TH], sq_row[0:1, 0:TH], AF.Sqrt,
        bias=p._eps_col[0:1, 0:1]), wait=p.after("mv"), inc="mv")
    p.op("vector", lambda e: e.reciprocal(
        rstd_row[0:1, 0:TH], p._rtmp[0:1, 0:TH]), wait=p.after("mv"))
    p.op("vector", lambda e: e.tensor_mul(
        negms_row[0:1, 0:TH], sq_row[0:1, 0:TH], rstd_row[0:1, 0:TH]))
    p.op("vector", lambda e: e.tensor_mul(
        negms_row[0:1, 0:TH], negms_row[0:1, 0:TH], rstd_row[0:1, 0:TH]),
        inc="vyr")
    p.op("sync", lambda e: e.dma_start(out=vy, in_=negms_row[0:1, 0:TH]),
         wait=p.after("vyr"), inc="st", incn=16)
    p.op("vector", lambda e: e.tensor_mul(
        negms_row[0:1, 0:TH], sum_row[0:1, 0:TH], rstd_row[0:1, 0:TH]),
        wait=p.after("st"))
    p.op("vector", lambda e: e.tensor_scalar_mul(
        negms_row[0:1, 0:TH], negms_row[0:1, 0:TH], -1.0), inc="mv")
    rw = p.after("mv")

    def applyB(tb, n, buf, ldw):
        for ci in range(NCH):
            pt, war = pp2.next()
            w = [rw, ldw]
            if war:
                w.append(war)
            p.op("tensor", lambda t, pt=pt, tb=tb, n=n:
                 t.matmul(pt[:, 0:n], p._onesrow_f[0:1, 0:128],
                          rstd_row[0:1, tb * 512:tb * 512 + n],
                          start=True, stop=True), wait=w)
            p.op("tensor", lambda t, pt=pt, tb=tb, n=n:
                 t.matmul(pt[:, 512:512 + n], p._onesrow_f[0:1, 0:128],
                          negms_row[0:1, tb * 512:tb * 512 + n],
                          start=True, stop=True), inc="mmB")
            mmv = p.count("mmB")
            p.op("vector", lambda e, pt=pt, ci=ci, n=n, buf=buf:
                 e.tensor_mul(out_f[:, 0:n], buf[:, ci * n:ci * n + n],
                              pt[:, 0:n]),
                 wait=[("mmB", mmv), ldw, p.after("st2")])
            p.op("vector", lambda e, pt=pt, n=n:
                 e.tensor_add(out_f[:, 0:n], out_f[:, 0:n], pt[:, 512:512 + n]),
                 inc=pp2.evsem)
            if ci == NCH - 1:
                p.op("vector", lambda e: e.engine_nop(), inc="xB_c")
            p.op("sync", lambda e, ci=ci, tb=tb, n=n:
                 e.dma_start(out=xn0[ci * 128:(ci + 1) * 128,
                                     tb * 512:tb * 512 + n],
                             in_=out_f[:, 0:n]),
                 wait=p.after(pp2.evsem), inc="st2", incn=16)

    stream_blocks(p, "xB", xT, TH, NCH, xbufs, applyB,
                  first_wait=[p.after("xA_c")])
    p.wait("sync", "st2")
    p.wait("sync", "st")
    return p.build()



def drv_scale_row(p, vy_sb, out_row, ntok, wait, eps=LN_EPS):
    p.op("scalar", lambda e: e.activation(
        p._rtmp[0:1, 0:ntok], vy_sb[0:1, 0:ntok], AF.Sqrt,
        bias=p._eps_col[0:1, 0:1]), wait=wait, inc="drv")
    p.op("vector", lambda e: e.reciprocal(
        out_row[0:1, 0:ntok], p._rtmp[0:1, 0:ntok]),
        wait=p.after("drv"), inc="drv")
    return p.after("drv")


def ln_stream(p, name, srcT, ntok, nch, bufs, scale_row, negms_row,
              g_row, b_row, out_sb, out_ntok, pp, rw, first_wait=None):
    mmsem = "mm_" + name

    def consume(tb, n, buf, ldw):
        for ci in range(nch):
            pt, war = pp.next()
            w = [rw, ldw]
            if war:
                w.append(war)
            grow = (g_row[0:1, ci * 128:(ci + 1) * 128] if g_row is not None
                    else p._onesrow_f[0:1, 0:128])
            p.op("tensor",
                 lambda t, pt=pt, grow=grow, tb=tb, n=n:
                 t.matmul(pt[:, 0:n], grow,
                          scale_row[0:1, tb * 512:tb * 512 + n],
                          start=True, stop=True), wait=w)
            if negms_row is not None:
                p.op("tensor",
                     lambda t, pt=pt, grow=grow, tb=tb, n=n:
                     t.matmul(pt[:, 512:512 + n], grow,
                              negms_row[0:1, tb * 512:tb * 512 + n],
                              start=True, stop=False))
            p.op("tensor",
                 lambda t, pt=pt, ci=ci, tb=tb, n=n:
                 t.matmul(pt[:, 512:512 + n],
                          b_row[0:1, ci * 128:(ci + 1) * 128],
                          p._onesrow_f[0:1, 0:n],
                          start=(negms_row is None), stop=True),
                 inc=mmsem)
            mmv = p.count(mmsem)
            p.op("vector",
                 lambda e, pt=pt, ci=ci, n=n, buf=buf:
                 e.tensor_mul(p._lntmp[:, 0:n], buf[:, ci * n:ci * n + n],
                              pt[:, 0:n]),
                 wait=[(mmsem, mmv), ldw])
            p.op("vector",
                 lambda e, pt=pt, ci=ci, tb=tb, n=n:
                 e.tensor_add(out_sb[:, ci * out_ntok + tb * 512:
                                     ci * out_ntok + tb * 512 + n],
                              p._lntmp[:, 0:n], pt[:, 512:512 + n]),
                 inc=pp.evsem)
            if ci == nch - 1:
                p.op("vector", lambda e: e.engine_nop(), inc=name + "_c")

    stream_blocks(p, name, srcT, ntok, nch, bufs, consume,
                  first_wait=first_wait)
    return p.after(name + "_c")


class OutTiles:

    def __init__(self, p, name, n, shape, dtype):
        self.p = p
        self.tiles = [p.sb(f"{name}{i}", shape, dtype) for i in range(n)]
        self.n = n
        self.k = 0
        self.stsem = name + "_st"
        p.sem(self.stsem)

    def next(self):
        t = self.tiles[self.k % self.n]
        war = (self.stsem, 16 * (self.k - self.n + 1)) if self.k >= self.n \
            else None
        self.k += 1
        return t, war


def build_aqkv():
    p = Prog()
    xn0 = p.dram("xn0", [D, TH], F32, kind="ExternalInput")
    vyd = p.dram("vy", [1, TH], F32, kind="ExternalInput")
    wq = p.dram("q_w", [D, D], F32, kind="ExternalInput")
    wk = p.dram("k_w", [D, D], F32, kind="ExternalInput")
    wv = p.dram("v_w", [D, D], F32, kind="ExternalInput")
    wdec = p.dram("dec_w", [D, 16], F32, kind="ExternalInput")
    qb = p.dram("q_b", [D], F32, kind="ExternalInput")
    kb = p.dram("k_b", [D], F32, kind="ExternalInput")
    vb = p.dram("v_b", [D], F32, kind="ExternalInput")
    decb = p.dram("dec_b", [16], F32, kind="ExternalInput")
    ln1w = p.dram("ln1_w", [D], F32, kind="ExternalInput")
    ln1b = p.dram("ln1_b", [D], F32, kind="ExternalInput")
    phiq = p.dram("phiq", [D, TH], BF16, kind="ExternalOutput")
    phik = p.dram("phik", [D, TH], BF16, kind="ExternalOutput")
    vtok = p.dram("vtok", [TH, D], BF16, kind="ExternalOutput")
    loglam = p.dram("loglam", [16, TH], F32, kind="ExternalOutput")

    common_tiles(p)
    wq_sb = p.sb("wq_sb", [128, NCH * D], BF16)
    wk_sb = p.sb("wk_sb", [128, NCH * D], BF16)
    wv_sb = p.sb("wv_sb", [128, NCH * D], BF16)
    wd_sb = p.sb("wd_sb", [128, NCH * 16], BF16)
    qb_c = p.sb("qb_c", [128, NCH], F32)
    kb_c = p.sb("kb_c", [128, NCH], F32)
    db_c = p.sb("db_c", [16, 1], F32)
    vb_r = p.sb("vb_r", [1, D], BF16)
    ln1w_r = p.sb("ln1w_r", [1, D], F32)
    ln1b_r = p.sb("ln1b_r", [1, D], F32)
    vy_r = p.sb("vy_r", [1, TH], F32)
    s_row = vy_r
    ha = p.sb("ha", [128, NCH * TH], BF16)
    xbufs = [p.sb(f"xb{i}", [128, NCH * 512], F32) for i in range(2)]
    t1 = p.sb("t1", [128, 512], F32)
    t2 = p.sb("t2", [128, 512], F32)
    t3 = p.sb("t3", [128, 512], F32)

    ld_weight(p, wq_sb, wq)
    ld_weight(p, wk_sb, wk)
    ld_weight(p, wv_sb, wv)
    ld_weight(p, wd_sb, wdec)
    for ci in range(NCH):
        ld_col(p, qb_c[:, ci:ci + 1].tensor[:, ci:ci + 1] if False else qb_c,
               qb[ci * 128:(ci + 1) * 128], n=128) if False else None
    p.op("gpsimd", lambda e: e.dma_start(
        out=qb_c[:, :], in_=qb.rearrange("(c p) -> p c", p=128),
        allow_slow_non_contiguous=True), inc="ld", incn=16)
    p.op("gpsimd", lambda e: e.dma_start(
        out=kb_c[:, :], in_=kb.rearrange("(c p) -> p c", p=128),
        allow_slow_non_contiguous=True), inc="ld", incn=16)
    ld_col(p, db_c, decb)
    ld_row(p, vb_r, vb)
    ld_row(p, ln1w_r, ln1w)
    ld_row(p, ln1b_r, ln1b)
    ld_row(p, vy_r, vyd[0, :])
    ldall = p.after("ld")

    pp = PsumPool(p, "pp", 3, [128, 1024])
    ppv = PsumPool(p, "ppv", 2, [128, 512])
    ot = OutTiles(p, "ot", 3, [128, 512], BF16)
    otl = OutTiles(p, "otl", 2, [16, 512], F32)

    rw = drv_scale_row(p, vy_r, s_row, TH, [ldall])
    hadone = ln_stream(p, "ha", xn0, TH, NCH, xbufs, s_row, None,
                       ln1w_r, ln1b_r, ha, TH, pp, rw, first_wait=[ldall])

    def phi_evac(which, qbc):
        def evac(co, tb, cod, n, pt, mmv, evsem):
            otile, war = ot.next()
            w = [("mm_" + which, mmv)]
            if war:
                w.append(war)
            p.op("vector", lambda e, pt=pt, co=co, n=n:
                 e.tensor_scalar(t1[:, 0:n], pt[:, 0:n],
                                 qbc[:, co:co + 1], 0.0, ALU.add, ALU.min),
                 wait=w)
            p.op("vector", lambda e, pt=pt, co=co, n=n:
                 e.tensor_scalar(t2[:, 0:n], pt[:, 0:n],
                                 qbc[:, co:co + 1], 0.0, ALU.add, ALU.max),
                 inc=evsem)
            p.op("scalar", lambda e, n=n: e.activation(
                t3[:, 0:n], t1[:, 0:n], AF.Exp),
                wait=p.after(evsem), inc="phiexp")
            p.op("vector", lambda e, otile=otile, n=n:
                 e.tensor_add(otile[:, 0:n], t3[:, 0:n], t2[:, 0:n]),
                 wait=("phiexp", p.count("phiexp")), inc="phiadd")
            dst = phiq if which == "q" else phik
            p.op("sync", lambda e, otile=otile, co=co, tb=tb, cod=cod, n=n,
                 dst=dst:
                 e.dma_start(out=dst[co * 128:co * 128 + cod,
                                     tb * 512:tb * 512 + n],
                             in_=otile[0:cod, 0:n]),
                 wait=p.after("phiadd"), inc=ot.stsem, incn=16)
        return evac

    proj_feat(p, "q", wq_sb, D, D, ha, TH, pp, phi_evac("q", qb_c),
              wait=[hadone])
    proj_feat(p, "k", wk_sb, D, D, ha, TH, pp, phi_evac("k", kb_c),
              wait=[hadone])

    def dec_evac(co, tb, cod, n, pt, mmv, evsem):
        otile, war = otl.next()
        w = [("mm_dec", mmv)]
        if war:
            w.append(war)
        p.op("scalar", lambda e, pt=pt, n=n, cod=cod: e.activation(
            t1[0:cod, 0:n], pt[0:cod, 0:n], AF.Sigmoid,
            bias=db_c[0:16, 0:1]), wait=w, inc="decs")
        p.op("vector", lambda e, n=n, cod=cod:
             e.tensor_scalar_max(t2[0:cod, 0:n], t1[0:cod, 0:n], 1e-8),
             wait=p.after("decs"), inc=evsem)
        p.op("scalar", lambda e, otile=otile, n=n, cod=cod: e.activation(
            otile[0:cod, 0:n], t2[0:cod, 0:n], AF.Ln),
            wait=p.after(evsem), inc="decl")
        p.op("sync", lambda e, otile=otile, tb=tb, cod=cod, n=n:
             e.dma_start(out=loglam[0:cod, tb * 512:tb * 512 + n],
                         in_=otile[0:cod, 0:n]),
             wait=p.after("decl"), inc=otl.stsem, incn=16)

    proj_feat(p, "dec", wd_sb, D, 16, ha, TH, pp, dec_evac, wait=[hadone])

    def v_evac(tt, ob, tn, n, pt, mmv, evsem):
        otile, war = ot.next()
        w = [("mm_v", mmv)]
        if war:
            w.append(war)
        p.op("scalar", lambda e, pt=pt, otile=otile, tn=tn, n=n:
             e.activation(otile[0:tn, 0:n], pt[0:tn, 0:n], AF.Copy),
             wait=w, inc=evsem)
        p.op("sync", lambda e, otile=otile, tt=tt, ob=ob, tn=tn, n=n:
             e.dma_start(out=vtok[tt * 128:tt * 128 + tn,
                                  ob * 512:ob * 512 + n],
                         in_=otile[0:tn, 0:n]),
             wait=p.after(evsem), inc=ot.stsem, incn=16)

    proj_tok(p, "v", ha, D, wv_sb, D, TH, ppv, v_evac, vb_r, wait=[hadone])

    p.wait("sync", ot.stsem)
    p.wait("sync", otl.stsem)
    return p.build()



def build_lion(NP=8, TT=T, CH=128, dbg=False):
    p = Prog()
    nchk = TT // CH
    phiq = p.dram("phiq", [NP * 64, TT], BF16, kind="ExternalInput")
    phik = p.dram("phik", [NP * 64, TT], BF16, kind="ExternalInput")
    vtok = p.dram("vtok", [NP * TT, 64], BF16, kind="ExternalInput")
    loglam = p.dram("loglam", [NP, TT], F32, kind="ExternalInput")
    maskL = p.dram("maskL", [128, 128], F32, kind="ExternalInput")
    ident = p.dram("ident", [128, 128], F32, kind="ExternalInput")
    attn = p.dram("attn", [NP * 64, TT], BF16, kind="ExternalOutput")
    if dbg:
        dres = p.dram("dres", [65, TT], F32, kind="ExternalOutput")
        dE = p.dram("dE", [4 * 128, 128], F32, kind="ExternalOutput")
        dst8 = p.dram("dst8", [64, 65 * 8], BF16, kind="ExternalOutput")
        dG = p.dram("dG", [128, 128], F32, kind="ExternalOutput")

    common_tiles(p)
    mask_sb = p.sb("mask_sb", [128, 128], F32)
    pq = p.sb("pq", [64, TT], BF16)
    pk = p.sb("pk", [64, TT], BF16)
    vt = p.sb("vt", [128, (TT // 128) * 64], BF16)
    ll = p.sb("ll", [1, TT], F32)
    res = p.sb("res", [65, TT], F32)
    rstore = p.sb("rstore", [64, 65 * nchk], BF16)
    pkt = p.sb("pkt", [128, 64 * nchk], BF16)
    llc = p.sb("llc", [1, CH], F32)
    lt_col = p.sb("lt_col", [128, 1], F32)
    nlt_col = p.sb("nlt_col", [128, 1], F32)
    lc128 = p.sb("lc128", [128, 1], F32)
    acol = p.sb("acol", [128, 1], F32)
    wcol = p.sb("wcol", [128, 1], F32)
    ecall = p.sb("ecall", [128, 1], F32)
    bcsb = p.sb("bcsb", [128, 128], F32)
    e1in = p.sb("e1in", [128, 128], F32)
    e2in = p.sb("e2in", [128, 128], F32)
    E1 = p.sb("E1", [128, 128], BF16)
    E2 = p.sb("E2", [128, 128], BF16)
    Etot = p.sb("Etot", [128, 128], BF16)
    Asb = p.sb("Asb", [128, 128], BF16)
    expb = p.sb("expb", [64, 128], BF16)
    wrow = p.sb("wrow", [64, 128], BF16)
    qs = p.sb("qs", [64, 128], BF16)
    qsr = p.sb("qsr", [64, 128], BF16)
    Vsc = p.sb("Vsc", [128, 65], BF16)
    S_f = p.sb("S_f", [64, 65], F32)
    R_f = p.sb("R_f", [64, 65], F32)
    Sb = p.sb("Sb", [64, 65], BF16)
    oattn = p.sb("oattn", [64, 512], BF16)

    psb = p.ps("psb", [128, 128], F32)
    ps2c = p.ps("ps2c", [128, 2], F32)
    psG = p.ps("psG", [128, 128], F32)
    psO = p.ps("psO", [128, 128], F32)
    psO2 = p.ps("psO2", [128, 128], F32)
    psPT = p.ps("psPT", [128, 64], BF16)
    psS = p.ps("psS", [64, 65], F32)
    psN = p.ps("psN", [64, 512], F32)

    identf = p.sb("identf", [128, 128], F32)
    p.sop("sync", lambda e: e.dma_start(out=mask_sb[:, :], in_=maskL), dma=True)
    p.sop("sync", lambda e: e.dma_start(out=identf[:, :], in_=ident), dma=True)
    p.sop("vector", lambda e: e.tensor_copy(p._identb[:, :], identf[:, :]))

    def chunk_common(pr, ch, need_bcast):
        p.sop("vector", lambda e, ch=ch: e.tensor_tensor_scan(
            llc[0:1, :], ll[0:1, ch * CH:(ch + 1) * CH],
            p._onesrow_f[0:1, 0:CH], 0.0, ALU.add, ALU.mult))
        p.sop("tensor", lambda e: e.transpose(
            ps2c[:, 0:1], llc[0:1, :], p._onesrow_f[0:1, 0:1]))
        p.sop("vector", lambda e: e.tensor_copy(lt_col[:, :], ps2c[:, 0:1]))
        p.sop("vector", lambda e: e.tensor_scalar_mul(
            nlt_col[:, :], lt_col[:, :], -1.0))
        p.sop("tensor", lambda e: e.matmul(
            ps2c[:, 1:2], p._onesrow_f[0:1, 0:128], llc[0:1, 127:128],
            start=True, stop=True, skip_group_check=True))
        p.sop("vector", lambda e: e.tensor_copy(lc128[:, :], ps2c[:, 1:2]))
        p.sop("scalar", lambda e: e.activation(acol[:, :], lt_col[:, :], AF.Exp))
        p.sop("scalar", lambda e: e.activation(
            wcol[:, :], lt_col[:, :], AF.Exp, bias=lc128[:, 0:1], scale=-1.0))
        p.sop("scalar", lambda e: e.activation(ecall[:, :], lc128[:, :], AF.Exp))
        if need_bcast:
            p.sop("tensor", lambda e: e.matmul(
                psb[:, :], p._onesrow_f[0:1, 0:128], llc[0:1, :],
                start=True, stop=True))
            p.sop("vector", lambda e: e.tensor_copy(bcsb[:, :], psb[:, :]))
            p.sop("vector", lambda e: e.tensor_add(
                e1in[:, :], bcsb[:, :], mask_sb[:, :]))
            p.sop("vector", lambda e: e.tensor_scalar_add(
                e2in[:, :], e1in[:, :], 1e4))

    for pr in range(NP):
        p.sop("sync", lambda e, pr=pr: e.dma_start(
            out=pq[:, :], in_=phiq[pr * 64:(pr + 1) * 64, :]), dma=True)
        p.sop("sync", lambda e, pr=pr: e.dma_start(
            out=pk[:, :], in_=phik[pr * 64:(pr + 1) * 64, :]), dma=True)
        p.sop("sync", lambda e, pr=pr: e.dma_start(
            out=vt[:, :].rearrange("p (c n) -> p c n", c=TT // 128),
            in_=vtok[pr * TT:(pr + 1) * TT, :].rearrange(
                "(c p) n -> p c n", p=128)), dma=True)
        p.sop("sync", lambda e, pr=pr: e.dma_start(
            out=ll[0:1, :], in_=loglam[pr:pr + 1, :]), dma=True)

        p.sop("vector", lambda e: e.memset(R_f[:, :], 0.0))
        p.sop("vector", lambda e: e.memset(S_f[:, :], 0.0))

        for ch in range(nchk - 1, -1, -1):
            chunk_common(pr, ch, need_bcast=False)
            p.sop("vector", lambda e, ch=ch: e.tensor_copy(
                rstore[:, ch * 65:(ch + 1) * 65], R_f[:, :]))
            p.sop("tensor", lambda e, ch=ch: e.transpose(
                psPT[:, :], pk[:, ch * CH:(ch + 1) * CH], p._identb[0:64, 0:64]))
            p.sop("vector", lambda e, ch=ch: e.tensor_copy(
                pkt[:, ch * 64:(ch + 1) * 64], psPT[:, :]))
            p.sop("vector", lambda e, ch=ch: e.tensor_scalar_mul(
                Vsc[:, 0:64], vt[:, ch * 64:(ch + 1) * 64], acol[:, 0:1]))
            p.sop("vector", lambda e: e.tensor_copy(Vsc[:, 64:65], acol[:, :]))
            p.sop("tensor", lambda e, ch=ch: e.matmul(
                psS[:, :], pkt[:, ch * 64:(ch + 1) * 64], Vsc[:, :],
                start=True, stop=True))
            p.sop("vector", lambda e: e.tensor_scalar_mul(
                R_f[:, :], R_f[:, :], ecall[0:64, 0:1]))
            p.sop("vector", lambda e: e.tensor_add(R_f[:, :], R_f[:, :],
                                                   psS[:, :]))

        for ch in range(nchk):
            chunk_common(pr, ch, need_bcast=True)
            p.sop("scalar", lambda e: e.activation(
                E1[:, :], e1in[:, :], AF.Exp, bias=nlt_col[:, 0:1]))
            p.sop("scalar", lambda e: e.activation(
                E2[:, :], e2in[:, :], AF.Exp, bias=lt_col[:, 0:1], scale=-1.0))
            p.sop("vector", lambda e: e.tensor_add(Etot[:, :], E1[:, :],
                                                   E2[:, :]))
            p.sop("scalar", lambda e: e.activation(
                expb[:, :], bcsb[0:64, :], AF.Exp))
            p.sop("scalar", lambda e: e.activation(
                wrow[:, :], bcsb[0:64, :], AF.Exp, bias=lc128[0:64, 0:1],
                scale=-1.0))
            p.sop("tensor", lambda e, ch=ch: e.matmul(
                psG[:, :], pk[:, ch * CH:(ch + 1) * CH],
                pq[:, ch * CH:(ch + 1) * CH], start=True, stop=True))
            if dbg and pr == 0 and ch == nchk - 1:
                p.sop("vector", lambda e: e.tensor_copy(e1in[:, :], psG[:, :]))
                p.sop("sync", lambda e: e.dma_start(out=dG, in_=e1in[:, :]),
                      dma=True)
            p.sop("vector", lambda e: e.tensor_tensor(
                Asb[:, :], psG[:, :], Etot[:, :], ALU.mult))
            p.sop("vector", lambda e, ch=ch: e.tensor_mul(
                qs[:, :], pq[:, ch * CH:(ch + 1) * CH], expb[:, :]))
            p.sop("vector", lambda e, ch=ch: e.tensor_mul(
                qsr[:, :], pq[:, ch * CH:(ch + 1) * CH], wrow[:, :]))
            p.sop("vector", lambda e: e.tensor_copy(Sb[:, :], S_f[:, :]))
            p.sop("tensor", lambda e, ch=ch: e.matmul(
                psO[0:64, :], vt[:, ch * 64:(ch + 1) * 64], Asb[:, :],
                start=True, stop=True, skip_group_check=True))
            p.sop("tensor", lambda e: e.matmul(
                psO[64:65, :], p._onescol_b[:, 0:1], Asb[:, :],
                start=True, stop=True, skip_group_check=True))
            p.sop("tensor", lambda e: e.matmul(
                psO2[0:65, :], Sb[:, :], qs[:, :],
                start=True, stop=False, skip_group_check=True))
            p.sop("tensor", lambda e, ch=ch: e.matmul(
                psO2[0:65, :], rstore[:, ch * 65:(ch + 1) * 65], qsr[:, :],
                start=False, stop=True, skip_group_check=True))
            p.sop("vector", lambda e, ch=ch: e.tensor_copy(
                res[:, ch * CH:(ch + 1) * CH], psO[0:65, :]))
            p.sop("vector", lambda e, ch=ch: e.tensor_add(
                res[:, ch * CH:(ch + 1) * CH],
                res[:, ch * CH:(ch + 1) * CH], psO2[0:65, :]))
            p.sop("vector", lambda e, ch=ch: e.tensor_scalar_mul(
                Vsc[:, 0:64], vt[:, ch * 64:(ch + 1) * 64], wcol[:, 0:1]))
            p.sop("vector", lambda e: e.tensor_copy(Vsc[:, 64:65], wcol[:, :]))
            p.sop("tensor", lambda e, ch=ch: e.matmul(
                psS[:, :], pkt[:, ch * 64:(ch + 1) * 64], Vsc[:, :],
                start=True, stop=True))
            p.sop("vector", lambda e: e.tensor_scalar_mul(
                S_f[:, :], S_f[:, :], ecall[0:64, 0:1]))
            p.sop("vector", lambda e: e.tensor_add(S_f[:, :], S_f[:, :],
                                                   psS[:, :]))

        if dbg and pr == 0:
            p.sop("sync", lambda e: e.dma_start(out=dres, in_=res[:, :]),
                  dma=True)
            p.sop("vector", lambda e: e.tensor_copy(e1in[:, :], E1[:, :]))
            p.sop("sync", lambda e: e.dma_start(out=dE[0:128, :],
                                                in_=e1in[:, :]), dma=True)
            p.sop("vector", lambda e: e.tensor_copy(e1in[:, :], E2[:, :]))
            p.sop("sync", lambda e: e.dma_start(out=dE[128:256, :],
                                                in_=e1in[:, :]), dma=True)
            p.sop("vector", lambda e: e.tensor_copy(e1in[:, :], Etot[:, :]))
            p.sop("sync", lambda e: e.dma_start(out=dE[256:384, :],
                                                in_=e1in[:, :]), dma=True)
            p.sop("vector", lambda e: e.tensor_copy(e1in[:, :], Asb[:, :]))
            p.sop("sync", lambda e: e.dma_start(out=dE[384:512, :],
                                                in_=e1in[:, :]), dma=True)
            for cc in range(min(8, nchk)):
                p.sop("vector", lambda e, cc=cc: e.tensor_copy(
                    Vsc[:, :] if False else S_f[:, :], S_f[:, :]))
                p.sop("sync", lambda e, cc=cc: e.dma_start(
                    out=dst8[:, cc * 65:(cc + 1) * 65],
                    in_=rstore[:, cc * 65:(cc + 1) * 65]), dma=True)

        p.sop("vector", lambda e: e.tensor_scalar_max(
            ll[0:1, :], res[64:65, :], 1e-6))
        p.sop("vector", lambda e: e.reciprocal(ll[0:1, :], ll[0:1, :]))
        for tb in range(TT // 512):
            p.sop("tensor", lambda e, tb=tb: e.matmul(
                psN[:, :], p._onesrow_f[0:1, 0:64],
                ll[0:1, tb * 512:(tb + 1) * 512], start=True, stop=True))
            p.sop("vector", lambda e, tb=tb: e.tensor_mul(
                oattn[:, :], res[0:64, tb * 512:(tb + 1) * 512], psN[:, :]))
            p.sop("sync", lambda e, pr=pr, tb=tb: e.dma_start(
                out=attn[pr * 64:(pr + 1) * 64, tb * 512:(tb + 1) * 512],
                in_=oattn[:, :]), dma=True)

    p.wait("sync", "seq")
    return p.build()



def ld_bias_cols(p, dst_sb, vec_ap, nch):
    p.op("gpsimd", lambda e: e.dma_start(
        out=dst_sb[:, 0:nch], in_=vec_ap.rearrange("(c p) -> p c", p=128),
        allow_slow_non_contiguous=True), inc="ld", incn=16)


def srows_meanvar_rstd(p, sum_row, sq_row, ntok, nd, rstd_row, negms_row,
                      eps=LN_EPS):
    p.sop("vector", lambda e: e.tensor_scalar_mul(
        sum_row[0:1, 0:ntok], sum_row[0:1, 0:ntok], 1.0 / nd))
    p.sop("vector", lambda e: e.tensor_scalar_mul(
        sq_row[0:1, 0:ntok], sq_row[0:1, 0:ntok], 1.0 / nd))
    p.sop("vector", lambda e: e.tensor_mul(
        p._rtmp[0:1, 0:ntok], sum_row[0:1, 0:ntok], sum_row[0:1, 0:ntok]))
    p.sop("vector", lambda e: e.tensor_sub(
        sq_row[0:1, 0:ntok], sq_row[0:1, 0:ntok], p._rtmp[0:1, 0:ntok]))
    p.sop("scalar", lambda e: e.activation(
        p._rtmp[0:1, 0:ntok], sq_row[0:1, 0:ntok], AF.Sqrt,
        bias=p._eps_col[0:1, 0:1]))
    p.sop("vector", lambda e: e.reciprocal(
        rstd_row[0:1, 0:ntok], p._rtmp[0:1, 0:ntok]))
    p.sop("vector", lambda e: e.tensor_mul(
        negms_row[0:1, 0:ntok], sum_row[0:1, 0:ntok], rstd_row[0:1, 0:ntok]))
    p.sop("vector", lambda e: e.tensor_scalar_mul(
        negms_row[0:1, 0:ntok], negms_row[0:1, 0:ntok], -1.0))


def sln_apply(p, x_sb_or_stream, scale_row, negms_row, g_row, b_row,
              out_dram, ntok, nch, psA, psB, tile_f, tile_o, src_dram=None,
              src_dt=BF16, xbuf=None):
    for tb in range(cdiv(ntok, 512)):
        n = min(512, ntok - tb * 512)
        if src_dram is not None:
            src = src_dram.rearrange("(c p) n -> p c n", p=128)[
                :, :, tb * 512:tb * 512 + n]
            p.sop("sync", lambda e, src=src, n=n: e.dma_start(
                out=xbuf[:, 0:nch * n].rearrange("p (c n) -> p c n", c=nch),
                in_=src), dma=True)
            xsb, XN = xbuf, n
            xoff = 0
        else:
            xsb, XN = x_sb_or_stream
            xoff = tb * 512
        for ci in range(nch):
            grow = (g_row[0:1, ci * 128:(ci + 1) * 128] if g_row is not None
                    else p._onesrow_f[0:1, 0:128])
            p.sop("tensor", lambda e, grow=grow, tb=tb, n=n: e.matmul(
                psA[:, 0:n], grow, scale_row[0:1, tb * 512:tb * 512 + n],
                start=True, stop=True))
            if negms_row is not None:
                p.sop("tensor", lambda e, grow=grow, tb=tb, n=n: e.matmul(
                    psB[:, 0:n], grow,
                    negms_row[0:1, tb * 512:tb * 512 + n],
                    start=True, stop=False, skip_group_check=True))
            p.sop("tensor", lambda e, ci=ci, n=n: e.matmul(
                psB[:, 0:n], b_row[0:1, ci * 128:(ci + 1) * 128],
                p._onesrow_f[0:1, 0:n],
                start=(negms_row is None), stop=True, skip_group_check=True))
            p.sop("vector", lambda e, ci=ci, n=n, xsb=xsb, XN=XN, xoff=xoff:
                  e.tensor_mul(tile_f[:, 0:n],
                               xsb[:, ci * XN + xoff:ci * XN + xoff + n],
                               psA[:, 0:n]))
            p.sop("vector", lambda e, n=n: e.tensor_add(
                tile_o[:, 0:n], tile_f[:, 0:n], psB[:, 0:n]))
            p.sop("sync", lambda e, ci=ci, tb=tb, n=n: e.dma_start(
                out=out_dram[ci * 128:(ci + 1) * 128, tb * 512:tb * 512 + n],
                in_=tile_o[:, 0:n]), dma=True)


def build_aoln(dd=D, ow_name="o_w"):
    p = Prog()
    attn = p.dram("attn", [dd, TH], BF16, kind="ExternalInput")
    xn0 = p.dram("xn0", [dd, TH], F32, kind="ExternalInput")
    ow = p.dram("o_w", [dd, dd], F32, kind="ExternalInput")
    ob = p.dram("o_b", [dd], F32, kind="ExternalInput")
    ln2w = p.dram("ln2_w", [dd], F32, kind="ExternalInput")
    ln2b = p.dram("ln2_b", [dd], F32, kind="ExternalInput")
    xa = p.dram("xa", [dd, TH], BF16, kind="ExternalOutput")
    h2a = p.dram("h2a", [dd, TH], BF16, kind="ExternalOutput")

    nch = dd // 128
    common_tiles(p)
    at_sb = p.sb("at_sb", [128, nch * 512], BF16)
    xn_sb = p.sb("xn_sb", [128, nch * 512], BF16)
    ow_sb = p.sb("ow_sb", [128, nch * dd], BF16)
    ob_c = p.sb("ob_c", [128, nch], F32)
    ln2w_r = p.sb("ln2w_r", [1, dd], F32)
    ln2b_r = p.sb("ln2b_r", [1, dd], F32)
    sum_row = p.sb("sum_row", [1, TH], F32)
    sq_row = p.sb("sq_row", [1, TH], F32)
    rstd_row = p.sb("rstd_row", [1, TH], F32)
    negms_row = p.sb("negms_row", [1, TH], F32)
    t1 = p.sb("t1", [128, 512], F32)
    t2 = p.sb("t2", [128, 512], BF16)
    sqt = p.sb("sqt", [128, 512], BF16)
    ps = p.ps("ps", [128, 512], F32)
    psA = p.ps("psA", [128, 512], F32)
    psB = p.ps("psB", [128, 512], F32)
    psS1 = p.ps("psS1", [1, 512], F32)
    psS2 = p.ps("psS2", [1, 512], F32)

    p.sop("gpsimd", lambda e: e.dma_start(
        out=ow_sb[:, :].rearrange("p (c n) -> p c n", c=nch),
        in_=ow.rearrange("(c p) n -> p c n", p=128)), dma=True)
    p.sop("gpsimd", lambda e: e.dma_start(
        out=ob_c[:, 0:nch], in_=ob.rearrange("(c p) -> p c", p=128),
        allow_slow_non_contiguous=True), dma=True)
    p.sop("sync", lambda e: e.dma_start(out=ln2w_r[0:1, 0:dd],
                                        in_=ln2w[None, :]), dma=True)
    p.sop("sync", lambda e: e.dma_start(out=ln2b_r[0:1, 0:dd],
                                        in_=ln2b[None, :]), dma=True)

    for tb in range(TH // 512):
        p.sop("sync", lambda e, tb=tb: e.dma_start(
            out=at_sb[:, :].rearrange("p (c n) -> p c n", c=nch),
            in_=attn.rearrange("(c p) n -> p c n", p=128)[
                :, :, tb * 512:(tb + 1) * 512]), dma=True)
        p.sop("gpsimd", lambda e, tb=tb: e.dma_start(
            out=xn_sb[:, :].rearrange("p (c n) -> p c n", c=nch),
            in_=xn0.rearrange("(c p) n -> p c n", p=128)[
                :, :, tb * 512:(tb + 1) * 512]), dma=True)
        for co in range(nch):
            for ci in range(nch):
                p.sop("tensor", lambda e, co=co, ci=ci, tb=tb: e.matmul(
                    ps[:, :], ow_sb[:, ci * dd + co * 128:ci * dd + co * 128 + 128],
                    at_sb[:, ci * 512:(ci + 1) * 512],
                    start=(ci == 0), stop=(ci == nch - 1)))
            p.sop("scalar", lambda e, co=co: e.activation(
                t1[:, :], ps[:, :], AF.Identity, bias=ob_c[:, co:co + 1]))
            p.sop("vector", lambda e, co=co, tb=tb: e.tensor_add(
                t2[:, :], t1[:, :],
                xn_sb[:, co * 512:(co + 1) * 512]))
            p.sop("sync", lambda e, co=co, tb=tb: e.dma_start(
                out=xa[co * 128:(co + 1) * 128, tb * 512:tb * 512 + 512],
                in_=t2[:, :]), dma=True)
            p.sop("scalar", lambda e: e.activation(sqt[:, :], t2[:, :],
                                                   AF.Square))
            p.sop("tensor", lambda e, co=co: e.matmul(
                psS1[0:1, :], p._onescol_b[:, 0:1], t2[:, :],
                start=(co == 0), stop=(co == nch - 1)))
            p.sop("tensor", lambda e, co=co: e.matmul(
                psS2[0:1, :], p._onescol_b[:, 0:1], sqt[:, :],
                start=(co == 0), stop=(co == nch - 1)))
        p.sop("vector", lambda e, tb=tb: e.tensor_copy(
            sum_row[0:1, tb * 512:(tb + 1) * 512], psS1[0:1, :]))
        p.sop("vector", lambda e, tb=tb: e.tensor_copy(
            sq_row[0:1, tb * 512:(tb + 1) * 512], psS2[0:1, :]))

    srows_meanvar_rstd(p, sum_row, sq_row, TH, dd, rstd_row, negms_row)
    xbuf = p.sb("xbuf", [128, nch * 512], BF16)
    sln_apply(p, None, rstd_row, negms_row, ln2w_r, ln2b_r, h2a, TH, nch,
              psA, psB, t1, t2, src_dram=xa, xbuf=xbuf)
    p.wait("sync", "seq")
    return p.build()


def build_affn(dff=4096):
    p = Prog()
    h2a = p.dram("h2a", [D, TH], BF16, kind="ExternalInput")
    xa = p.dram("xa", [D, TH], BF16, kind="ExternalInput")
    f1w = p.dram("f1_w", [D, dff], F32, kind="ExternalInput")
    f1b = p.dram("f1_b", [dff], F32, kind="ExternalInput")
    f2w = p.dram("f2_w", [dff, D], F32, kind="ExternalInput")
    f2b = p.dram("f2_b", [D], F32, kind="ExternalInput")
    a0 = p.dram("a0", [D, TH], BF16, kind="ExternalOutput")

    nh = dff // 128
    common_tiles(p)
    w1_sb = p.sb("w1_sb", [128, NCH * dff], BF16)
    w2_sb = p.sb("w2_sb", [128, nh * D], BF16)
    b1_c = p.sb("b1_c", [128, nh], F32)
    b2_c = p.sb("b2_c", [128, NCH], F32)
    hb = p.sb("hb", [128, nh * 512], BF16)
    xab = p.sb("xab", [128, NCH * 512], BF16)
    h2b = p.sb("h2b", [128, NCH * 512], BF16)
    t1 = p.sb("t1", [128, 512], F32)
    t2 = p.sb("t2", [128, 512], BF16)
    ps = p.ps("ps", [128, 512], F32)

    p.sop("gpsimd", lambda e: e.dma_start(
        out=w1_sb[:, :].rearrange("p (c n) -> p c n", c=NCH),
        in_=f1w.rearrange("(c p) n -> p c n", p=128)), dma=True)
    p.sop("gpsimd", lambda e: e.dma_start(
        out=w2_sb[:, :].rearrange("p (c n) -> p c n", c=nh),
        in_=f2w.rearrange("(c p) n -> p c n", p=128)), dma=True)
    p.sop("gpsimd", lambda e: e.dma_start(
        out=b1_c[:, 0:nh], in_=f1b.rearrange("(c p) -> p c", p=128),
        allow_slow_non_contiguous=True), dma=True)
    p.sop("gpsimd", lambda e: e.dma_start(
        out=b2_c[:, 0:NCH], in_=f2b.rearrange("(c p) -> p c", p=128),
        allow_slow_non_contiguous=True), dma=True)

    for tb in range(TH // 512):
        p.sop("sync", lambda e, tb=tb: e.dma_start(
            out=h2b[:, :].rearrange("p (c n) -> p c n", c=NCH),
            in_=h2a.rearrange("(c p) n -> p c n", p=128)[
                :, :, tb * 512:(tb + 1) * 512]), dma=True)
        p.sop("sync", lambda e, tb=tb: e.dma_start(
            out=xab[:, :].rearrange("p (c n) -> p c n", c=NCH),
            in_=xa.rearrange("(c p) n -> p c n", p=128)[
                :, :, tb * 512:(tb + 1) * 512]), dma=True)
        for co in range(nh):
            for ci in range(NCH):
                p.sop("tensor", lambda e, co=co, ci=ci: e.matmul(
                    ps[:, :],
                    w1_sb[:, ci * dff + co * 128:ci * dff + co * 128 + 128],
                    h2b[:, ci * 512:(ci + 1) * 512],
                    start=(ci == 0), stop=(ci == NCH - 1)))
            p.sop("scalar", lambda e, co=co: e.activation(
                hb[:, co * 512:(co + 1) * 512], ps[:, :], AF.Gelu,
                bias=b1_c[:, co:co + 1]))
        for co in range(NCH):
            for ci in range(nh):
                p.sop("tensor", lambda e, co=co, ci=ci: e.matmul(
                    ps[:, :],
                    w2_sb[:, ci * D + co * 128:ci * D + co * 128 + 128],
                    hb[:, ci * 512:(ci + 1) * 512],
                    start=(ci == 0), stop=(ci == nh - 1)))
            p.sop("scalar", lambda e, co=co: e.activation(
                t1[:, :], ps[:, :], AF.Identity, bias=b2_c[:, co:co + 1]))
            p.sop("vector", lambda e, co=co: e.tensor_add(
                t2[:, :], t1[:, :], xab[:, co * 512:(co + 1) * 512]))
            p.sop("sync", lambda e, co=co, tb=tb: e.dma_start(
                out=a0[co * 128:(co + 1) * 128, tb * 512:tb * 512 + 512],
                in_=t2[:, :]), dma=True)
    p.wait("sync", "seq")
    return p.build()


def run_launch(nc, in_maps):
    res = run_bass_kernel_spmd(nc, in_maps, core_ids=list(range(NCORE)))
    return res.results


def shard_rows(x):
    out = []
    for c in range(NCORE):
        b, h = c // 2, c % 2
        sl = np.ascontiguousarray(x[b, h * TH:(h + 1) * TH, :].T)
        out.append(sl)
    return out



def build_fqkv():
    p = Prog()
    xn0 = p.dram("xn0", [D, TH], F32, kind="ExternalInput")
    vyd = p.dram("vy", [1, TH], F32, kind="ExternalInput")
    wq = p.dram("wq", [D, D], F32, kind="ExternalInput")
    wk = p.dram("wk", [D, D], F32, kind="ExternalInput")
    wv = p.dram("wv", [D, D], F32, kind="ExternalInput")
    bq = p.dram("bq", [D], F32, kind="ExternalInput")
    bk = p.dram("bk", [D], F32, kind="ExternalInput")
    bv = p.dram("bv", [D], F32, kind="ExternalInput")
    ln1g = p.dram("ln1_g", [D], F32, kind="ExternalInput")
    ln1b = p.dram("ln1_b", [D], F32, kind="ExternalInput")
    qF = p.dram("qF", [D, TH], BF16, kind="ExternalOutput")
    kF = p.dram("kF", [D, TH], BF16, kind="ExternalOutput")
    vF = p.dram("vF", [TH, D], BF16, kind="ExternalOutput")

    common_tiles(p)
    wq_sb = p.sb("wq_sb", [128, NCH * D], BF16)
    wk_sb = p.sb("wk_sb", [128, NCH * D], BF16)
    wv_sb = p.sb("wv_sb", [128, NCH * D], BF16)
    qb_c = p.sb("qb_c", [128, NCH], F32)
    kb_c = p.sb("kb_c", [128, NCH], F32)
    vb_r = p.sb("vb_r", [1, D], BF16)
    g_r = p.sb("g_r", [1, D], F32)
    b_r = p.sb("b_r", [1, D], F32)
    vy_r = p.sb("vy_r", [1, TH], F32)
    hf = p.sb("hf", [128, NCH * TH], BF16)
    xbuf = p.sb("xbuf", [128, NCH * 512], F32)
    t1 = p.sb("t1", [128, 512], F32)
    t2 = p.sb("t2", [128, 512], BF16)
    ps = p.ps("ps", [128, 512], F32)
    psA = p.ps("psA", [128, 512], F32)
    psB = p.ps("psB", [128, 512], F32)

    p.sop("gpsimd", lambda e: e.dma_start(
        out=wq_sb[:, :].rearrange("p (c n) -> p c n", c=NCH),
        in_=wq.rearrange("(c p) n -> p c n", p=128)), dma=True)
    p.sop("gpsimd", lambda e: e.dma_start(
        out=wk_sb[:, :].rearrange("p (c n) -> p c n", c=NCH),
        in_=wk.rearrange("(c p) n -> p c n", p=128)), dma=True)
    p.sop("gpsimd", lambda e: e.dma_start(
        out=wv_sb[:, :].rearrange("p (c n) -> p c n", c=NCH),
        in_=wv.rearrange("(c p) n -> p c n", p=128)), dma=True)
    p.sop("gpsimd", lambda e: e.dma_start(
        out=qb_c[:, 0:NCH], in_=bq.rearrange("(c p) -> p c", p=128),
        allow_slow_non_contiguous=True), dma=True)
    p.sop("gpsimd", lambda e: e.dma_start(
        out=kb_c[:, 0:NCH], in_=bk.rearrange("(c p) -> p c", p=128),
        allow_slow_non_contiguous=True), dma=True)
    p.sop("gpsimd", lambda e: e.dma_start(out=vb_r[0:1, 0:D],
                                          in_=bv[None, :]), dma=True)
    p.sop("sync", lambda e: e.dma_start(out=g_r[0:1, 0:D],
                                        in_=ln1g[None, :]), dma=True)
    p.sop("sync", lambda e: e.dma_start(out=b_r[0:1, 0:D],
                                        in_=ln1b[None, :]), dma=True)
    p.sop("sync", lambda e: e.dma_start(out=vy_r[0:1, :],
                                        in_=vyd[0:1, :]), dma=True)
    p.sop("vector", lambda e: e.tensor_scalar_mul(qb_c[:, :], qb_c[:, :],
                                                  0.125))
    p.sop("scalar", lambda e: e.activation(p._rtmp[0:1, 0:TH],
                                           vy_r[0:1, 0:TH], AF.Sqrt,
                                           bias=p._eps_col[0:1, 0:1]))
    p.sop("vector", lambda e: e.reciprocal(vy_r[0:1, 0:TH],
                                           p._rtmp[0:1, 0:TH]))
    for tb in range(TH // 512):
        p.sop("sync", lambda e, tb=tb: e.dma_start(
            out=xbuf[:, :].rearrange("p (c n) -> p c n", c=NCH),
            in_=xn0.rearrange("(c p) n -> p c n", p=128)[
                :, :, tb * 512:(tb + 1) * 512]), dma=True)
        for ci in range(NCH):
            p.sop("tensor", lambda e, ci=ci, tb=tb: e.matmul(
                psA[:, :], g_r[0:1, ci * 128:(ci + 1) * 128],
                vy_r[0:1, tb * 512:(tb + 1) * 512], start=True, stop=True))
            p.sop("tensor", lambda e, ci=ci: e.matmul(
                psB[:, :], b_r[0:1, ci * 128:(ci + 1) * 128],
                p._onesrow_f[0:1, 0:512], start=True, stop=True))
            p.sop("vector", lambda e, ci=ci: e.tensor_mul(
                t1[:, :], xbuf[:, ci * 512:(ci + 1) * 512], psA[:, :]))
            p.sop("vector", lambda e, ci=ci, tb=tb: e.tensor_add(
                hf[:, ci * TH + tb * 512:ci * TH + tb * 512 + 512],
                t1[:, :], psB[:, :]))
    for which, wsb, dst in (("q", wq_sb, qF), ("k", wk_sb, kF)):
        for co in range(NCH):
            for tb in range(TH // 512):
                for ci in range(NCH):
                    p.sop("tensor", lambda e, wsb=wsb, co=co, ci=ci, tb=tb:
                          e.matmul(ps[:, :],
                                   wsb[:, ci * D + co * 128:ci * D + co * 128 + 128],
                                   hf[:, ci * TH + tb * 512:ci * TH + tb * 512 + 512],
                                   start=(ci == 0), stop=(ci == NCH - 1)))
                if which == "q":
                    p.sop("scalar", lambda e, co=co: e.activation(
                        t2[:, :], ps[:, :], AF.Identity,
                        bias=qb_c[:, co:co + 1], scale=0.125))
                else:
                    p.sop("scalar", lambda e, co=co: e.activation(
                        t2[:, :], ps[:, :], AF.Identity,
                        bias=kb_c[:, co:co + 1]))
                p.sop("sync", lambda e, dst=dst, co=co, tb=tb: e.dma_start(
                    out=dst[co * 128:(co + 1) * 128, tb * 512:tb * 512 + 512],
                    in_=t2[:, :]), dma=True)
    for tt in range(TH // 128):
        for ob in range(D // 512):
            for ci in range(NCH):
                p.sop("tensor", lambda e, tt=tt, ob=ob, ci=ci: e.matmul(
                    ps[:, :],
                    hf[:, ci * TH + tt * 128:ci * TH + tt * 128 + 128],
                    wv_sb[:, ci * D + ob * 512:ci * D + ob * 512 + 512],
                    start=(ci == 0), stop=False))
            p.sop("tensor", lambda e, tt=tt, ob=ob: e.matmul(
                ps[:, :], p._ones_b[0:1, 0:128],
                vb_r[0:1, ob * 512:ob * 512 + 512],
                start=False, stop=True))
            p.sop("scalar", lambda e: e.activation(t2[:, :], ps[:, :],
                                                   AF.Copy))
            p.sop("sync", lambda e, tt=tt, ob=ob: e.dma_start(
                out=vF[tt * 128:(tt + 1) * 128, ob * 512:ob * 512 + 512],
                in_=t2[:, :]), dma=True)
    p.wait("sync", "seq")
    return p.build()



def build_fattn(NP=8, TT=T):
    p = Prog()
    qF = p.dram("qF", [NP * 64, TT], BF16, kind="ExternalInput")
    kF = p.dram("kF", [NP * 64, TT], BF16, kind="ExternalInput")
    vF = p.dram("vF", [NP * TT, 64], BF16, kind="ExternalInput")
    attnF = p.dram("attnF", [NP * 64, TT], BF16, kind="ExternalOutput")

    common_tiles(p)
    qsb = p.sb("qsb", [64, TT], BF16)
    ksb = p.sb("ksb", [64, TT], BF16)
    vsb = p.sb("vsb", [128, (TT // 128) * 64], BF16)
    sqf = p.sb("sqf", [64, 512], F32)
    qmax = p.sb("qmax", [1, 2], F32)
    kmax = p.sb("kmax", [1, 2], F32)
    tmax = p.sb("tmax", [1, 2], F32)
    negc = p.sb("negc", [1, 1], F32)
    negc_col = p.sb("negc_col", [128, 1], F32)
    esb = p.sb("esb", [128, 512], BF16)
    denr = p.sb("denr", [1, 512], F32)
    tt1 = p.sb("tt1", [64, 512], F32)
    oatt = p.sb("oatt", [64, 512], BF16)
    psSc = p.ps("psSc", [128, 512], F32)
    psO = p.ps("psO", [64, 512], F32)
    psD = p.ps("psD", [1, 512], F32)
    psC = p.ps("psC", [128, 1], F32)
    psB = p.ps("psB", [64, 512], F32)

    nkc = TT // 128
    for pr in range(NP):
        p.sop("sync", lambda e, pr=pr: e.dma_start(
            out=qsb[:, :], in_=qF[pr * 64:(pr + 1) * 64, :]), dma=True)
        p.sop("sync", lambda e, pr=pr: e.dma_start(
            out=ksb[:, :], in_=kF[pr * 64:(pr + 1) * 64, :]), dma=True)
        p.sop("sync", lambda e, pr=pr: e.dma_start(
            out=vsb[:, :].rearrange("p (c n) -> p c n", c=TT // 128),
            in_=vF[pr * TT:(pr + 1) * TT, :].rearrange(
                "(c p) n -> p c n", p=128)), dma=True)
        for which, sb_, mx in (("q", qsb, qmax), ("k", ksb, kmax)):
            p.sop("vector", lambda e, mx=mx: e.memset(mx[:, :], 0.0))
            for tb in range(TT // 512):
                p.sop("scalar", lambda e, sb_=sb_, tb=tb: e.activation(
                    sqf[:, :], sb_[:, tb * 512:(tb + 1) * 512], AF.Square))
                p.sop("tensor", lambda e: e.matmul(
                    psD[0:1, :], p._ones_f[0:64, 0:1], sqf[:, :],
                    start=True, stop=True))
                p.sop("vector", lambda e: e.tensor_reduce(
                    tmax[0:1, 0:1], psD[0:1, :], AX.X, ALU.max))
                p.sop("vector", lambda e, mx=mx: e.tensor_max(
                    mx[0:1, 0:1], mx[0:1, 0:1], tmax[0:1, 0:1]))
        p.sop("scalar", lambda e: e.activation(qmax[0:1, 0:1], qmax[0:1, 0:1],
                                               AF.Sqrt))
        p.sop("scalar", lambda e: e.activation(kmax[0:1, 0:1], kmax[0:1, 0:1],
                                               AF.Sqrt))
        p.sop("vector", lambda e: e.tensor_mul(negc[0:1, 0:1], qmax[0:1, 0:1],
                                               kmax[0:1, 0:1]))
        p.sop("vector", lambda e: e.tensor_scalar_mul(negc[0:1, 0:1],
                                                      negc[0:1, 0:1], -1.0))
        p.sop("tensor", lambda e: e.matmul(
            psC[:, :], p._onesrow_f[0:1, 0:128], negc[0:1, 0:1],
            start=True, stop=True))
        p.sop("vector", lambda e: e.tensor_copy(negc_col[:, :], psC[:, :]))
        for qb in range(TT // 512):
            for kc in range(nkc):
                p.sop("tensor", lambda e, qb=qb, kc=kc: e.matmul(
                    psSc[:, :], ksb[:, kc * 128:(kc + 1) * 128],
                    qsb[:, qb * 512:(qb + 1) * 512], start=True, stop=True))
                p.sop("scalar", lambda e: e.activation(
                    esb[:, :], psSc[:, :], AF.Exp, bias=negc_col[:, 0:1]))
                p.sop("tensor", lambda e, kc=kc: e.matmul(
                    psO[:, :], vsb[:, kc * 64:(kc + 1) * 64], esb[:, :],
                    start=(kc == 0), stop=(kc == nkc - 1),
                    skip_group_check=True))
                p.sop("tensor", lambda e, kc=kc: e.matmul(
                    psD[0:1, :], p._onescol_b[:, 0:1], esb[:, :],
                    start=(kc == 0), stop=(kc == nkc - 1),
                    skip_group_check=True))
            p.sop("vector", lambda e: e.tensor_copy(denr[0:1, :], psD[0:1, :]))
            p.sop("vector", lambda e: e.reciprocal(denr[0:1, :],
                                                   denr[0:1, :]))
            p.sop("tensor", lambda e: e.matmul(
                psB[:, :], p._onesrow_f[0:1, 0:64], denr[0:1, :],
                start=True, stop=True))
            p.sop("vector", lambda e: e.tensor_copy(tt1[:, :], psO[:, :]))
            p.sop("vector", lambda e: e.tensor_mul(oatt[:, :], tt1[:, :],
                                                   psB[:, :]))
            p.sop("sync", lambda e, pr=pr, qb=qb: e.dma_start(
                out=attnF[pr * 64:(pr + 1) * 64, qb * 512:(qb + 1) * 512],
                in_=oatt[:, :]), dma=True)
    p.wait("sync", "seq")
    return p.build()



def build_route():
    p = Prog()
    h2f = p.dram("h2f", [D, TH], BF16, kind="ExternalInput")
    wr = p.dram("wr", [D, 8], F32, kind="ExternalInput")
    p1d = p.dram("p1", [1, TH], F32, kind="ExternalOutput")
    p2d = p.dram("p2", [1, TH], F32, kind="ExternalOutput")
    e1d = p.dram("e1", [1, TH], F32, kind="ExternalOutput")
    e2d = p.dram("e2", [1, TH], F32, kind="ExternalOutput")
    s1d = p.dram("s1", [1, TH], F32, kind="ExternalOutput")
    s2d = p.dram("s2", [1, TH], F32, kind="ExternalOutput")
    cntd = p.dram("cnt", [2, 8], F32, kind="ExternalOutput")
    ident = p.dram("ident", [128, 128], F32, kind="ExternalInput")

    common_tiles(p)
    identf = p.sb("identf", [128, 128], F32)
    wr_sb = p.sb("wr_sb", [128, NCH * 8], BF16)
    hbuf = p.sb("hbuf", [128, NCH * 128], BF16)
    lg = p.sb("lg", [128, 8], F32)
    m1c = p.sb("m1c", [128, 1], F32)
    oh = p.sb("oh", [128, 8], F32)
    cum = p.sb("cum", [128, 8], F32)
    oh1 = p.sb("oh1", [128, 8], F32)
    oh2 = p.sb("oh2", [128, 8], F32)
    msk = p.sb("msk", [128, 8], F32)
    exps = p.sb("exps", [128, 8], F32)
    sumc = p.sb("sumc", [128, 1], F32)
    pc1 = p.sb("pc1", [128, 1], F32)
    pc2 = p.sb("pc2", [128, 1], F32)
    iota8 = p.sb("iota8", [128, 8], mybir.dt.int32)
    iota8f = p.sb("iota8f", [128, 8], F32)
    oh1f = p.sb("oh1f", [8, TH], F32)
    oh2f = p.sb("oh2f", [8, TH], F32)
    cum1f = p.sb("cum1f", [8, TH], F32)
    ones8 = p.sb("ones8", [8, TH], F32)
    prod8 = p.sb("prod8", [8, 512], F32)
    rowo = p.sb("rowo", [1, 512], F32)
    pst = p.ps("pst", [128, 8], F32)
    psr = p.ps("psr", [1, 512], F32)
    pstr = p.ps("pstr", [128, 128], F32)

    p.sop("gpsimd", lambda e: e.dma_start(
        out=wr_sb[:, :].rearrange("p (c n) -> p c n", c=NCH),
        in_=wr.rearrange("(c p) n -> p c n", p=128)), dma=True)
    p.sop("sync", lambda e: e.dma_start(out=identf[:, :], in_=ident),
          dma=True)
    p.sop("gpsimd", lambda e: e.iota(iota8[:, :], pattern=[[1, 8]], base=0,
                                     channel_multiplier=0))
    p.sop("vector", lambda e: e.tensor_copy(iota8f[:, :], iota8[:, :]))
    p.sop("vector", lambda e: e.memset(ones8[:, :], 1.0))

    for tt in range(TH // 128):
        p.sop("sync", lambda e, tt=tt: e.dma_start(
            out=hbuf[:, :].rearrange("p (c n) -> p c n", c=NCH),
            in_=h2f.rearrange("(c p) n -> p c n", p=128)[
                :, :, tt * 128:(tt + 1) * 128]), dma=True)
        for ci in range(NCH):
            p.sop("tensor", lambda e, ci=ci: e.matmul(
                pst[:, :], hbuf[:, ci * 128:(ci + 1) * 128],
                wr_sb[:, ci * 8:(ci + 1) * 8],
                start=(ci == 0), stop=(ci == NCH - 1)))
        p.sop("vector", lambda e: e.tensor_copy(lg[:, :], pst[:, :]))
        p.sop("vector", lambda e: e.tensor_reduce(m1c[:, :], lg[:, :], AX.X,
                                                  ALU.max))
        p.sop("vector", lambda e: e.tensor_scalar(
            oh[:, :], lg[:, :], m1c[:, 0:1], None, ALU.is_ge))
        p.sop("vector", lambda e: e.tensor_tensor_scan(
            cum[:, :], oh[:, :], p._ones_f[:, 0:8], 0.0, ALU.add, ALU.mult))
        p.sop("vector", lambda e: e.tensor_scalar(
            oh1[:, :], cum[:, :], 1.0, None, ALU.is_equal))
        p.sop("vector", lambda e: e.tensor_mul(oh1[:, :], oh1[:, :],
                                               oh[:, :]))
        p.sop("vector", lambda e: e.scalar_tensor_tensor(
            msk[:, :], oh1[:, :], -1e30, lg[:, :], ALU.mult, ALU.add))
        p.sop("vector", lambda e: e.tensor_reduce(pc1[:, :], msk[:, :], AX.X,
                                                  ALU.max))
        p.sop("vector", lambda e: e.tensor_scalar(
            oh[:, :], msk[:, :], pc1[:, 0:1], None, ALU.is_ge))
        p.sop("vector", lambda e: e.tensor_tensor_scan(
            cum[:, :], oh[:, :], p._ones_f[:, 0:8], 0.0, ALU.add, ALU.mult))
        p.sop("vector", lambda e: e.tensor_scalar(
            oh2[:, :], cum[:, :], 1.0, None, ALU.is_equal))
        p.sop("vector", lambda e: e.tensor_mul(oh2[:, :], oh2[:, :],
                                               oh[:, :]))
        p.sop("vector", lambda e: e.tensor_scalar_mul(m1c[:, :], m1c[:, :],
                                                      -1.0))
        p.sop("scalar", lambda e: e.activation(
            exps[:, :], lg[:, :], AF.Exp, bias=m1c[:, 0:1],
            accum_out=sumc[:, 0:1]))
        p.sop("vector", lambda e: e.reciprocal(sumc[:, :], sumc[:, :]))
        p.sop("vector", lambda e: e.tensor_mul(msk[:, :], exps[:, :],
                                               oh1[:, :]))
        p.sop("vector", lambda e: e.tensor_reduce(pc1[:, :], msk[:, :], AX.X,
                                                  ALU.add))
        p.sop("vector", lambda e: e.tensor_mul(pc1[:, :], pc1[:, :],
                                               sumc[:, :]))
        p.sop("vector", lambda e: e.tensor_mul(msk[:, :], exps[:, :],
                                               oh2[:, :]))
        p.sop("vector", lambda e: e.tensor_reduce(pc2[:, :], msk[:, :], AX.X,
                                                  ALU.add))
        p.sop("vector", lambda e: e.tensor_mul(pc2[:, :], pc2[:, :],
                                               sumc[:, :]))
        p.sop("vector", lambda e: e.tensor_mul(msk[:, :], oh1[:, :],
                                               iota8f[:, :]))
        p.sop("vector", lambda e: e.tensor_reduce(m1c[:, :], msk[:, :], AX.X,
                                                  ALU.add))
        p.sop("vector", lambda e: e.tensor_mul(exps[:, :], oh2[:, :],
                                               iota8f[:, :]))
        p.sop("vector", lambda e: e.tensor_reduce(sumc[:, :], exps[:, :],
                                                  AX.X, ALU.add))
        for src, dstf in ((oh1, oh1f), (oh2, oh2f)):
            p.sop("tensor", lambda e, src=src: e.transpose(
                pstr[0:8, 0:128], src[:, :], identf[:, :]))
            p.sop("vector", lambda e, dstf=dstf, tt=tt: e.tensor_copy(
                dstf[0:8, tt * 128:(tt + 1) * 128], pstr[0:8, 0:128]))
        for src, dstd in ((pc1, p1d), (pc2, p2d), (m1c, e1d), (sumc, e2d)):
            p.sop("tensor", lambda e, src=src: e.transpose(
                pstr[0:1, 0:128], src[:, :], identf[:, :]))
            p.sop("vector", lambda e: e.tensor_copy(rowo[0:1, 0:128],
                                                    pstr[0:1, 0:128]))
            p.sop("sync", lambda e, dstd=dstd, tt=tt: e.dma_start(
                out=dstd[0:1, tt * 128:(tt + 1) * 128],
                in_=rowo[0:1, 0:128]), dma=True)
    for ohf, sd, row in ((oh1f, s1d, 0), (oh2f, s2d, 1)):
        p.sop("vector", lambda e, ohf=ohf: e.tensor_tensor_scan(
            cum1f[0:8, :], ohf[0:8, :], ones8[0:8, :], 0.0,
            ALU.add, ALU.mult))
        for tb in range(TH // 512):
            p.sop("vector", lambda e, ohf=ohf, tb=tb: e.tensor_mul(
                prod8[0:8, :], ohf[0:8, tb * 512:(tb + 1) * 512],
                cum1f[0:8, tb * 512:(tb + 1) * 512]))
            p.sop("tensor", lambda e: e.matmul(
                psr[0:1, :], p._ones_f[0:8, 0:1], prod8[0:8, :],
                start=True, stop=True))
            p.sop("vector", lambda e: e.tensor_copy(rowo[0:1, :],
                                                    psr[0:1, :]))
            p.sop("sync", lambda e, sd=sd, tb=tb: e.dma_start(
                out=sd[0:1, tb * 512:(tb + 1) * 512], in_=rowo[0:1, 0:512]),
                dma=True)
        p.sop("tensor", lambda e: e.transpose(
            pstr[0:1, 0:8], cum1f[0:8, TH - 1:TH], identf[0:8, 0:8]))
        p.sop("vector", lambda e: e.tensor_copy(rowo[0:1, 0:8],
                                                pstr[0:1, 0:8]))
        p.sop("sync", lambda e, row=row: e.dma_start(
            out=cntd[row:row + 1, 0:8], in_=rowo[0:1, 0:8]), dma=True)
    p.wait("sync", "seq")
    return p.build()



def build_moegates(CAP=8192):
    p = Prog()
    cnt_all = p.dram("cnt_all", [16, 8], F32, kind="ExternalInput")
    pref = p.dram("pref", [8], F32, kind="ExternalInput")
    p1d = p.dram("p1", [1, TH], F32, kind="ExternalInput")
    p2d = p.dram("p2", [1, TH], F32, kind="ExternalInput")
    e1d = p.dram("e1", [1, TH], F32, kind="ExternalInput")
    e2d = p.dram("e2", [1, TH], F32, kind="ExternalInput")
    s1d = p.dram("s1", [1, TH], F32, kind="ExternalInput")
    s2d = p.dram("s2", [1, TH], F32, kind="ExternalInput")
    g1d = p.dram("g1", [1, TH], F32, kind="ExternalOutput")
    g2d = p.dram("g2", [1, TH], F32, kind="ExternalOutput")
    l1d = p.dram("loc1", [1, TH], F32, kind="ExternalOutput")
    l2d = p.dram("loc2", [1, TH], F32, kind="ExternalOutput")

    common_tiles(p)
    cnt_sb = p.sb("cnt_sb", [8, 8], F32)
    cnt2_sb = p.sb("cnt2_sb", [8, 8], F32)
    pref_c = p.sb("pref_c", [8, 1], F32)
    rows = {}
    for nm in ("p1r", "p2r", "e1r", "e2r", "l1r", "l2r", "t1r", "t2r", "mr"):
        rows[nm] = p.sb(nm, [1, TH], F32)
    n1row = p.sb("n1row", [1, 8], F32)
    off1 = p.sb("off1", [1, 8], F32)
    off2 = p.sb("off2", [1, 8], F32)
    psr = p.ps("psr", [1, 8], F32)

    p.sop("sync", lambda e: e.dma_start(out=cnt_sb[:, :],
                                        in_=cnt_all[0:8, :]), dma=True)
    p.sop("sync", lambda e: e.dma_start(out=cnt2_sb[:, :],
                                        in_=cnt_all[8:16, :]), dma=True)
    p.sop("sync", lambda e: e.dma_start(out=pref_c[:, 0:1],
                                        in_=pref[:][:, None]), dma=True)
    for nm, src in (("p1r", p1d), ("p2r", p2d), ("e1r", e1d), ("e2r", e2d),
                    ("l1r", s1d), ("l2r", s2d)):
        p.sop("sync", lambda e, nm=nm, src=src: e.dma_start(
            out=rows[nm][0:1, :], in_=src[0:1, :]), dma=True)
    p.sop("tensor", lambda e: e.matmul(psr[0:1, :], p._ones_f[0:8, 0:1],
                                       cnt_sb[0:8, :], start=True, stop=True))
    p.sop("vector", lambda e: e.tensor_copy(n1row[0:1, :], psr[0:1, :]))
    p.sop("tensor", lambda e: e.matmul(psr[0:1, :], pref_c[0:8, 0:1],
                                       cnt_sb[0:8, :], start=True, stop=True))
    p.sop("vector", lambda e: e.tensor_copy(off1[0:1, :], psr[0:1, :]))
    p.sop("tensor", lambda e: e.matmul(psr[0:1, :], pref_c[0:8, 0:1],
                                       cnt2_sb[0:8, :], start=True, stop=True))
    p.sop("vector", lambda e: e.tensor_copy(off2[0:1, :], psr[0:1, :]))
    p.sop("vector", lambda e: e.tensor_add(off2[0:1, :], off2[0:1, :],
                                           n1row[0:1, :]))
    for lr, er, offr in (("l1r", "e1r", off1), ("l2r", "e2r", off2)):
        p.sop("vector", lambda e, lr=lr: e.tensor_scalar_add(
            rows[lr][0:1, :], rows[lr][0:1, :], -1.0))
        for ex in range(8):
            p.sop("vector", lambda e, er=er, ex=ex: e.tensor_scalar(
                rows["mr"][0:1, :], rows[er][0:1, :], float(ex), None,
                ALU.is_equal))
            p.sop("vector", lambda e, lr=lr, offr=offr, ex=ex:
                  e.scalar_tensor_tensor(rows[lr][0:1, :], rows["mr"][0:1, :],
                                         offr[0:1, ex:ex + 1],
                                         rows[lr][0:1, :], ALU.mult, ALU.add))
    p.sop("vector", lambda e: e.tensor_scalar(
        rows["t1r"][0:1, :], rows["l1r"][0:1, :], float(CAP), None,
        ALU.is_lt))
    p.sop("vector", lambda e: e.tensor_scalar(
        rows["t2r"][0:1, :], rows["l2r"][0:1, :], float(CAP), None,
        ALU.is_lt))
    p.sop("vector", lambda e: e.tensor_mul(rows["t1r"][0:1, :],
                                           rows["t1r"][0:1, :],
                                           rows["p1r"][0:1, :]))
    p.sop("vector", lambda e: e.tensor_mul(rows["t2r"][0:1, :],
                                           rows["t2r"][0:1, :],
                                           rows["p2r"][0:1, :]))
    p.sop("vector", lambda e: e.tensor_add(rows["mr"][0:1, :],
                                           rows["t1r"][0:1, :],
                                           rows["t2r"][0:1, :]))
    p.sop("vector", lambda e: e.tensor_scalar_max(rows["mr"][0:1, :],
                                                  rows["mr"][0:1, :],
                                                  1.1920929e-07))
    p.sop("vector", lambda e: e.reciprocal(rows["mr"][0:1, :],
                                           rows["mr"][0:1, :]))
    p.sop("vector", lambda e: e.tensor_mul(rows["t1r"][0:1, :],
                                           rows["t1r"][0:1, :],
                                           rows["mr"][0:1, :]))
    p.sop("vector", lambda e: e.tensor_mul(rows["t2r"][0:1, :],
                                           rows["t2r"][0:1, :],
                                           rows["mr"][0:1, :]))
    for src, dst in (("t1r", g1d), ("t2r", g2d), ("l1r", l1d), ("l2r", l2d)):
        p.sop("sync", lambda e, src=src, dst=dst: e.dma_start(
            out=dst[0:1, :], in_=rows[src][0:1, :]), dma=True)
    p.wait("sync", "seq")
    return p.build()



def build_expert(RC=8192, dff=4096):
    p = Prog()
    Xe = p.dram("Xe", [D, RC], BF16, kind="ExternalInput")
    w1 = p.dram("w1", [D, dff], F32, kind="ExternalInput")
    b1 = p.dram("b1", [dff], F32, kind="ExternalInput")
    w2 = p.dram("w2", [dff, D], F32, kind="ExternalInput")
    b2 = p.dram("b2", [D], F32, kind="ExternalInput")
    Ye = p.dram("Ye", [D, RC], BF16, kind="ExternalOutput")

    nh = dff // 128
    common_tiles(p)
    w1_sb = p.sb("w1_sb", [128, NCH * dff], BF16)
    w2_sb = p.sb("w2_sb", [128, nh * D], BF16)
    b1_c = p.sb("b1_c", [128, nh], F32)
    b2_c = p.sb("b2_c", [128, NCH], F32)
    hb = p.sb("hb", [128, nh * 512], BF16)
    xb_ = p.sb("xb_", [128, NCH * 512], BF16)
    t2 = p.sb("t2", [128, 512], BF16)
    ps = p.ps("ps", [128, 512], F32)

    p.sop("gpsimd", lambda e: e.dma_start(
        out=w1_sb[:, :].rearrange("p (c n) -> p c n", c=NCH),
        in_=w1.rearrange("(c p) n -> p c n", p=128)), dma=True)
    p.sop("gpsimd", lambda e: e.dma_start(
        out=w2_sb[:, :].rearrange("p (c n) -> p c n", c=nh),
        in_=w2.rearrange("(c p) n -> p c n", p=128)), dma=True)
    p.sop("gpsimd", lambda e: e.dma_start(
        out=b1_c[:, 0:nh], in_=b1.rearrange("(c p) -> p c", p=128),
        allow_slow_non_contiguous=True), dma=True)
    p.sop("gpsimd", lambda e: e.dma_start(
        out=b2_c[:, 0:NCH], in_=b2.rearrange("(c p) -> p c", p=128),
        allow_slow_non_contiguous=True), dma=True)
    p.sop("vector", lambda e: e.tensor_scalar_mul(b2_c[:, :], b2_c[:, :],
                                                  0.8))
    for tb in range(RC // 512):
        p.sop("sync", lambda e, tb=tb: e.dma_start(
            out=xb_[:, :].rearrange("p (c n) -> p c n", c=NCH),
            in_=Xe.rearrange("(c p) n -> p c n", p=128)[
                :, :, tb * 512:(tb + 1) * 512]), dma=True)
        for co in range(nh):
            for ci in range(NCH):
                p.sop("tensor", lambda e, co=co, ci=ci: e.matmul(
                    ps[:, :],
                    w1_sb[:, ci * dff + co * 128:ci * dff + co * 128 + 128],
                    xb_[:, ci * 512:(ci + 1) * 512],
                    start=(ci == 0), stop=(ci == NCH - 1)))
            p.sop("scalar", lambda e, co=co: e.activation(
                hb[:, co * 512:(co + 1) * 512], ps[:, :], AF.Relu,
                bias=b1_c[:, co:co + 1]))
        for co in range(NCH):
            for ci in range(nh):
                p.sop("tensor", lambda e, co=co, ci=ci: e.matmul(
                    ps[:, :],
                    w2_sb[:, ci * D + co * 128:ci * D + co * 128 + 128],
                    hb[:, ci * 512:(ci + 1) * 512],
                    start=(ci == 0), stop=(ci == nh - 1)))
            p.sop("scalar", lambda e, co=co: e.activation(
                t2[:, :], ps[:, :], AF.Identity, bias=b2_c[:, co:co + 1],
                scale=0.8))
            p.sop("sync", lambda e, co=co, tb=tb: e.dma_start(
                out=Ye[co * 128:(co + 1) * 128, tb * 512:tb * 512 + 512],
                in_=t2[:, :]), dma=True)
    p.wait("sync", "seq")
    return p.build()



def build_combine():
    p = Prog()
    xT = p.dram("xT", [D, TH], F32, kind="ExternalInput")
    a0 = p.dram("a0", [D, TH], BF16, kind="ExternalInput")
    xf = p.dram("xf", [D, TH], BF16, kind="ExternalInput")
    y1 = p.dram("y1", [D, TH], BF16, kind="ExternalInput")
    y2 = p.dram("y2", [D, TH], BF16, kind="ExternalInput")
    g1 = p.dram("g1", [1, TH], F32, kind="ExternalInput")
    g2 = p.dram("g2", [1, TH], F32, kind="ExternalInput")
    x1d = p.dram("x1", [D, TH], F32, kind="ExternalOutput")
    xn1d = p.dram("xn1", [D, TH], F32, kind="ExternalOutput")
    vy1d = p.dram("vy1", [1, TH], F32, kind="ExternalOutput")

    common_tiles(p)
    bufs = {}
    for nm, dt_ in (("bxT", F32), ("ba0", BF16), ("bxf", BF16),
                    ("by1", BF16), ("by2", BF16)):
        bufs[nm] = p.sb(nm, [128, NCH * 512], dt_)
    g1r = p.sb("g1r", [1, TH], F32)
    g2r = p.sb("g2r", [1, TH], F32)
    sum_row = p.sb("sum_row", [1, TH], F32)
    sq_row = p.sb("sq_row", [1, TH], F32)
    rstd_row = p.sb("rstd_row", [1, TH], F32)
    negms_row = p.sb("negms_row", [1, TH], F32)
    t1 = p.sb("t1", [128, 512], F32)
    t2 = p.sb("t2", [128, 512], F32)
    sqt = p.sb("sqt", [128, 512], F32)
    psA = p.ps("psA", [128, 512], F32)
    psB = p.ps("psB", [128, 512], F32)
    psS1 = p.ps("psS1", [1, 512], F32)
    psS2 = p.ps("psS2", [1, 512], F32)

    p.sop("sync", lambda e: e.dma_start(out=g1r[0:1, :], in_=g1[0:1, :]),
          dma=True)
    p.sop("sync", lambda e: e.dma_start(out=g2r[0:1, :], in_=g2[0:1, :]),
          dma=True)
    srcs = {"bxT": xT, "ba0": a0, "bxf": xf, "by1": y1, "by2": y2}
    for tb in range(TH // 512):
        for nm, src in srcs.items():
            p.sop("sync", lambda e, nm=nm, src=src, tb=tb: e.dma_start(
                out=bufs[nm][:, :].rearrange("p (c n) -> p c n", c=NCH),
                in_=src.rearrange("(c p) n -> p c n", p=128)[
                    :, :, tb * 512:(tb + 1) * 512]), dma=True)
        p.sop("tensor", lambda e, tb=tb: e.matmul(
            psA[:, :], p._onesrow_f[0:1, 0:128],
            g1r[0:1, tb * 512:(tb + 1) * 512], start=True, stop=True))
        p.sop("tensor", lambda e, tb=tb: e.matmul(
            psB[:, :], p._onesrow_f[0:1, 0:128],
            g2r[0:1, tb * 512:(tb + 1) * 512], start=True, stop=True))
        for ci in range(NCH):
            p.sop("vector", lambda e, ci=ci: e.tensor_mul(
                t1[:, :], bufs["by1"][:, ci * 512:(ci + 1) * 512], psA[:, :]))
            p.sop("vector", lambda e, ci=ci: e.tensor_mul(
                t2[:, :], bufs["by2"][:, ci * 512:(ci + 1) * 512], psB[:, :]))
            p.sop("vector", lambda e: e.tensor_add(t1[:, :], t1[:, :],
                                                   t2[:, :]))
            p.sop("vector", lambda e, ci=ci: e.tensor_add(
                t1[:, :], t1[:, :], bufs["bxf"][:, ci * 512:(ci + 1) * 512]))
            p.sop("vector", lambda e, ci=ci: e.tensor_add(
                t1[:, :], t1[:, :], bufs["ba0"][:, ci * 512:(ci + 1) * 512]))
            p.sop("vector", lambda e, ci=ci: e.tensor_add(
                t1[:, :], t1[:, :], bufs["bxT"][:, ci * 512:(ci + 1) * 512]))
            p.sop("sync", lambda e, ci=ci, tb=tb: e.dma_start(
                out=x1d[ci * 128:(ci + 1) * 128, tb * 512:tb * 512 + 512],
                in_=t1[:, :]), dma=True)
            p.sop("scalar", lambda e: e.activation(sqt[:, :], t1[:, :],
                                                   AF.Square))
            p.sop("tensor", lambda e, ci=ci: e.matmul(
                psS1[0:1, :], p._ones_f[:, 0:1], t1[:, :],
                start=(ci == 0), stop=(ci == NCH - 1)))
            p.sop("tensor", lambda e, ci=ci: e.matmul(
                psS2[0:1, :], p._ones_f[:, 0:1], sqt[:, :],
                start=(ci == 0), stop=(ci == NCH - 1)))
        p.sop("vector", lambda e, tb=tb: e.tensor_copy(
            sum_row[0:1, tb * 512:(tb + 1) * 512], psS1[0:1, :]))
        p.sop("vector", lambda e, tb=tb: e.tensor_copy(
            sq_row[0:1, tb * 512:(tb + 1) * 512], psS2[0:1, :]))
    srows_meanvar_rstd(p, sum_row, sq_row, TH, D, rstd_row, negms_row)
    p.sop("vector", lambda e: e.tensor_mul(p._rtmp[0:1, :], sq_row[0:1, :],
                                           rstd_row[0:1, :]))
    p.sop("vector", lambda e: e.tensor_mul(p._rtmp[0:1, :], p._rtmp[0:1, :],
                                           rstd_row[0:1, :]))
    p.sop("sync", lambda e: e.dma_start(out=vy1d[0:1, :],
                                        in_=p._rtmp[0:1, :]), dma=True)
    for tb in range(TH // 512):
        p.sop("sync", lambda e, tb=tb: e.dma_start(
            out=bufs["bxT"][:, :].rearrange("p (c n) -> p c n", c=NCH),
            in_=x1d.rearrange("(c p) n -> p c n", p=128)[
                :, :, tb * 512:(tb + 1) * 512]), dma=True)
        p.sop("tensor", lambda e, tb=tb: e.matmul(
            psA[:, :], p._onesrow_f[0:1, 0:128],
            rstd_row[0:1, tb * 512:(tb + 1) * 512], start=True, stop=True))
        p.sop("tensor", lambda e, tb=tb: e.matmul(
            psB[:, :], p._onesrow_f[0:1, 0:128],
            negms_row[0:1, tb * 512:(tb + 1) * 512], start=True, stop=True))
        for ci in range(NCH):
            p.sop("vector", lambda e, ci=ci: e.tensor_mul(
                t1[:, :], bufs["bxT"][:, ci * 512:(ci + 1) * 512], psA[:, :]))
            p.sop("vector", lambda e: e.tensor_add(t1[:, :], t1[:, :],
                                                   psB[:, :]))
            p.sop("sync", lambda e, ci=ci, tb=tb: e.dma_start(
                out=xn1d[ci * 128:(ci + 1) * 128, tb * 512:tb * 512 + 512],
                in_=t1[:, :]), dma=True)
    p.wait("sync", "seq")
    return p.build()



def build_mpre():
    p = Prog()
    DIP = 4384
    xn1 = p.dram("xn1", [D, TH], F32, kind="ExternalInput")
    vy1 = p.dram("vy1", [1, TH], F32, kind="ExternalInput")
    inwT = p.dram("in_wT", [D, DIP], F32, kind="ExternalInput")
    normw = p.dram("norm_w", [D], F32, kind="ExternalInput")
    dtb = p.dram("dt_bias", [32], F32, kind="ExternalInput")
    alog = p.dram("A_log", [32], F32, kind="ExternalInput")
    zd = p.dram("z", [2048, TH], BF16, kind="ExternalOutput")
    xbcd = p.dram("xbc", [2304, TH], BF16, kind="ExternalOutput")
    dtd = p.dram("dtr", [32, TH], F32, kind="ExternalOutput")
    lld = p.dram("llm", [32, TH], F32, kind="ExternalOutput")

    common_tiles(p)
    w_sb = p.sb("w_sb", [128, NCH * DIP], BF16)
    nw_r = p.sb("nw_r", [1, D], F32)
    zero_r = p.sb("zero_r", [1, D], F32)
    vy_r = p.sb("vy_r", [1, TH], F32)
    dtb_c = p.sb("dtb_c", [32, 1], F32)
    negA_c = p.sb("negA_c", [32, 1], F32)
    xr = p.sb("xr", [128, NCH * TH], BF16)
    xbuf = p.sb("xbuf", [128, NCH * 512], F32)
    t1 = p.sb("t1", [128, 512], F32)
    t2 = p.sb("t2", [128, 512], BF16)
    v1 = p.sb("v1", [32, 512], F32)
    v2 = p.sb("v2", [32, 512], F32)
    v3 = p.sb("v3", [32, 512], F32)
    ps = p.ps("ps", [128, 512], F32)
    psA = p.ps("psA", [128, 512], F32)
    psB = p.ps("psB", [128, 512], F32)

    p.sop("gpsimd", lambda e: e.dma_start(
        out=w_sb[:, :].rearrange("p (c n) -> p c n", c=NCH),
        in_=inwT.rearrange("(c p) n -> p c n", p=128)), dma=True)
    p.sop("sync", lambda e: e.dma_start(out=nw_r[0:1, 0:D],
                                        in_=normw[None, :]), dma=True)
    p.sop("sync", lambda e: e.dma_start(out=vy_r[0:1, :], in_=vy1[0:1, :]),
          dma=True)
    p.sop("sync", lambda e: e.dma_start(out=dtb_c[:, 0:1],
                                        in_=dtb[:][:, None]), dma=True)
    p.sop("sync", lambda e: e.dma_start(out=negA_c[:, 0:1],
                                        in_=alog[:][:, None]), dma=True)
    p.sop("vector", lambda e: e.memset(zero_r[:, :], 0.0))
    p.sop("scalar", lambda e: e.activation(negA_c[:, :], negA_c[:, :],
                                           AF.Exp))
    p.sop("vector", lambda e: e.tensor_scalar_mul(negA_c[:, :],
                                                  negA_c[:, :], -1.0))
    p.sop("scalar", lambda e: e.activation(p._rtmp[0:1, 0:TH],
                                           vy_r[0:1, 0:TH], AF.Sqrt,
                                           bias=p._eps_col[0:1, 0:1]))
    p.sop("vector", lambda e: e.reciprocal(vy_r[0:1, 0:TH],
                                           p._rtmp[0:1, 0:TH]))
    for tb in range(TH // 512):
        p.sop("sync", lambda e, tb=tb: e.dma_start(
            out=xbuf[:, :].rearrange("p (c n) -> p c n", c=NCH),
            in_=xn1.rearrange("(c p) n -> p c n", p=128)[
                :, :, tb * 512:(tb + 1) * 512]), dma=True)
        for ci in range(NCH):
            p.sop("tensor", lambda e, ci=ci, tb=tb: e.matmul(
                psA[:, :], nw_r[0:1, ci * 128:(ci + 1) * 128],
                vy_r[0:1, tb * 512:(tb + 1) * 512], start=True, stop=True))
            p.sop("vector", lambda e, ci=ci, tb=tb: e.tensor_mul(
                xr[:, ci * TH + tb * 512:ci * TH + tb * 512 + 512],
                xbuf[:, ci * 512:(ci + 1) * 512], psA[:, :]))
    NCO = DIP // 128 + 1
    for co in range(35):
        cod = 128 if co < 34 else 32
        for tb in range(TH // 512):
            for ci in range(NCH):
                p.sop("tensor", lambda e, co=co, ci=ci, tb=tb, cod=cod:
                      e.matmul(ps[0:cod, :],
                               w_sb[:, ci * DIP + co * 128:
                                    ci * DIP + co * 128 + cod],
                               xr[:, ci * TH + tb * 512:ci * TH + tb * 512 + 512],
                               start=(ci == 0), stop=(ci == NCH - 1)))
            if co < 16:
                p.sop("scalar", lambda e: e.activation(t2[:, :], ps[:, :],
                                                       AF.Copy))
                p.sop("sync", lambda e, co=co, tb=tb: e.dma_start(
                    out=zd[co * 128:(co + 1) * 128, tb * 512:tb * 512 + 512],
                    in_=t2[:, :]), dma=True)
            elif co < 34:
                p.sop("scalar", lambda e: e.activation(t2[:, :], ps[:, :],
                                                       AF.Copy))
                p.sop("sync", lambda e, co=co, tb=tb: e.dma_start(
                    out=xbcd[(co - 16) * 128:(co - 15) * 128,
                             tb * 512:tb * 512 + 512],
                    in_=t2[:, :]), dma=True)
            else:
                p.sop("scalar", lambda e: e.activation(
                    v1[:, :], ps[0:32, :], AF.Identity, bias=dtb_c[0:32, 0:1]))
                p.sop("scalar", lambda e: e.activation(v2[:, :], v1[:, :],
                                                       AF.Abs))
                p.sop("scalar", lambda e: e.activation(v2[:, :], v2[:, :],
                                                       AF.Exp, scale=-1.0))
                p.sop("scalar", lambda e: e.activation(
                    v2[:, :], v2[:, :], AF.Ln, bias=p._ones_f[0:32, 0:1]))
                p.sop("vector", lambda e: e.tensor_scalar_max(v3[:, :],
                                                              v1[:, :], 0.0))
                p.sop("vector", lambda e: e.tensor_add(v3[:, :], v3[:, :],
                                                       v2[:, :]))
                p.sop("sync", lambda e, tb=tb: e.dma_start(
                    out=dtd[0:32, tb * 512:tb * 512 + 512], in_=v3[:, :]),
                    dma=True)
                p.sop("vector", lambda e: e.tensor_scalar_mul(
                    v3[:, :], v3[:, :], negA_c[0:32, 0:1]))
                p.sop("sync", lambda e, tb=tb: e.dma_start(
                    out=lld[0:32, tb * 512:tb * 512 + 512], in_=v3[:, :]),
                    dma=True)
    p.wait("sync", "seq")
    return p.build()



def build_mconv():
    p = Prog()
    xbcd = p.dram("xbc", [2304, TH], BF16, kind="ExternalInput")
    halo = p.dram("halo", [2304, 3], BF16, kind="ExternalInput")
    convw = p.dram("conv_w", [2304, 4], F32, kind="ExternalInput")
    convb = p.dram("conv_b", [2304], F32, kind="ExternalInput")
    dtd = p.dram("dtr", [32, TH], F32, kind="ExternalInput")
    ident = p.dram("ident", [128, 128], F32, kind="ExternalInput")
    xsd = p.dram("xsf", [2048, TH], BF16, kind="ExternalOutput")
    dtxd = p.dram("dtx", [TH, 2048], BF16, kind="ExternalOutput")
    Bd = p.dram("Bf", [128, TH], BF16, kind="ExternalOutput")
    Cd = p.dram("Cf", [128, TH], BF16, kind="ExternalOutput")

    common_tiles(p)
    identf = p.sb("identf", [128, 128], F32)
    cw_c = p.sb("cw_c", [128, 18 * 4], F32)
    cb_c = p.sb("cb_c", [128, 18], F32)
    xext = p.sb("xext", [128, 515], BF16)
    acc = p.sb("acc", [128, 512], F32)
    sil = p.sb("sil", [128, 512], BF16)
    dts = p.sb("dts", [32, 512], F32)
    dtt = p.sb("dtt", [128, 4 * 32], F32)
    ttok = p.sb("ttok", [128, 128], BF16)
    dtx_t = p.sb("dtx_t", [128, 128], BF16)
    pst = p.ps("pst", [128, 128], BF16)
    pst2 = p.ps("pst2", [128, 32], F32)

    p.sop("sync", lambda e: e.dma_start(out=identf[:, :], in_=ident),
          dma=True)
    p.sop("gpsimd", lambda e: e.dma_start(
        out=cw_c[:, :].rearrange("p (c t) -> p c t", c=18),
        in_=convw.rearrange("(c p) t -> p c t", p=128)), dma=True)
    p.sop("gpsimd", lambda e: e.dma_start(
        out=cb_c[:, 0:18], in_=convb.rearrange("(c p) -> p c", p=128),
        allow_slow_non_contiguous=True), dma=True)
    identb2 = p._identb
    p.sop("vector", lambda e: e.tensor_copy(identb2[:, :], identf[:, :]))

    for tb in range(TH // 512):
        p.sop("sync", lambda e, tb=tb: e.dma_start(
            out=dts[:, :], in_=dtd[0:32, tb * 512:(tb + 1) * 512]), dma=True)
        for sub in range(4):
            p.sop("tensor", lambda e, sub=sub: e.transpose(
                pst2[:, 0:32], dts[:, sub * 128:(sub + 1) * 128],
                identf[0:32, 0:32]))
            p.sop("vector", lambda e, sub=sub: e.tensor_copy(
                dtt[:, sub * 32:(sub + 1) * 32], pst2[:, 0:32]))
        for c in range(18):
            if tb == 0:
                p.sop("sync", lambda e, c=c: e.dma_start(
                    out=xext[:, 0:3], in_=halo[c * 128:(c + 1) * 128, :]),
                    dma=True)
            else:
                p.sop("sync", lambda e, c=c, tb=tb: e.dma_start(
                    out=xext[:, 0:3],
                    in_=xbcd[c * 128:(c + 1) * 128,
                             tb * 512 - 3:tb * 512]), dma=True)
            p.sop("sync", lambda e, c=c, tb=tb: e.dma_start(
                out=xext[:, 3:515],
                in_=xbcd[c * 128:(c + 1) * 128, tb * 512:(tb + 1) * 512]),
                dma=True)
            p.sop("vector", lambda e, c=c: e.tensor_scalar_mul(
                acc[:, :], xext[:, 0:512], cw_c[:, 4 * c:4 * c + 1]))
            for d in range(1, 4):
                p.sop("vector", lambda e, c=c, d=d: e.scalar_tensor_tensor(
                    acc[:, :], xext[:, d:512 + d], cw_c[:, 4 * c + d:4 * c + d + 1],
                    acc[:, :], ALU.mult, ALU.add))
            p.sop("scalar", lambda e, c=c: e.activation(
                sil[:, :], acc[:, :], AF.Silu, bias=cb_c[:, c:c + 1]))
            if c < 16:
                p.sop("sync", lambda e, c=c, tb=tb: e.dma_start(
                    out=xsd[c * 128:(c + 1) * 128, tb * 512:(tb + 1) * 512],
                    in_=sil[:, :]), dma=True)
                for sub in range(4):
                    p.sop("tensor", lambda e, sub=sub: e.transpose(
                        pst[:, :], sil[:, sub * 128:(sub + 1) * 128],
                        identb2[:, :]))
                    p.sop("vector", lambda e: e.tensor_copy(ttok[:, :],
                                                            pst[:, :]))
                    p.sop("vector", lambda e, c=c, sub=sub: e.tensor_scalar(
                        dtx_t[:, 0:64], ttok[:, 0:64],
                        dtt[:, sub * 32 + 2 * c:sub * 32 + 2 * c + 1], None,
                        ALU.mult))
                    p.sop("vector", lambda e, c=c, sub=sub: e.tensor_scalar(
                        dtx_t[:, 64:128], ttok[:, 64:128],
                        dtt[:, sub * 32 + 2 * c + 1:sub * 32 + 2 * c + 2],
                        None, ALU.mult))
                    p.sop("sync", lambda e, c=c, tb=tb, sub=sub: e.dma_start(
                        out=dtxd[tb * 512 + sub * 128:tb * 512 + sub * 128 + 128,
                                 c * 128:(c + 1) * 128],
                        in_=dtx_t[:, :]), dma=True)
            elif c == 16:
                p.sop("sync", lambda e, tb=tb: e.dma_start(
                    out=Bd[:, tb * 512:(tb + 1) * 512], in_=sil[:, :]),
                    dma=True)
            else:
                p.sop("sync", lambda e, tb=tb: e.dma_start(
                    out=Cd[:, tb * 512:(tb + 1) * 512], in_=sil[:, :]),
                    dma=True)
    p.wait("sync", "seq")
    return p.build()



def build_mamba(NH=16, TT=T, CH=128):
    p = Prog()
    nchk = TT // CH
    dtxd = p.dram("dtx", [TT, NH * 64], BF16, kind="ExternalInput")
    Bd = p.dram("Bf", [128, TT], BF16, kind="ExternalInput")
    Cd = p.dram("Cf", [128, TT], BF16, kind="ExternalInput")
    lld = p.dram("llm", [NH, TT], F32, kind="ExternalInput")
    maskL = p.dram("maskL", [128, 128], F32, kind="ExternalInput")
    ident = p.dram("ident", [128, 128], F32, kind="ExternalInput")
    yd = p.dram("yssm", [NH * 64, TT], BF16, kind="ExternalOutput")

    common_tiles(p)
    identf = p.sb("identf", [128, 128], F32)
    mask_sb = p.sb("mask_sb", [128, 128], F32)
    Bsb = p.sb("Bsb", [128, TT], BF16)
    Csb = p.sb("Csb", [128, TT], BF16)
    llsb = p.sb("llsb", [NH, TT], F32)
    dtxc = p.sb("dtxc", [128, NH * 64], BF16)
    llc = p.sb("llc", [1, CH], F32)
    llcall = p.sb("llcall", [NH, CH], F32)
    ones16 = p.sb("ones16", [NH, CH], F32)
    ltT = p.sb("ltT", [128, NH], F32)
    nltT = p.sb("nltT", [128, NH], F32)
    lc128 = p.sb("lc128", [128, 1], F32)
    wcol = p.sb("wcol", [128, 1], F32)
    ecall = p.sb("ecall", [128, 1], F32)
    bcsb = p.sb("bcsb", [128, 128], F32)
    e1in = p.sb("e1in", [128, 128], F32)
    Esb = p.sb("Esb", [128, 128], BF16)
    CBs = p.sb("CBs", [128, 128], BF16)
    Btok = p.sb("Btok", [128, 128], BF16)
    Asb = p.sb("Asb", [128, 128], BF16)
    Cw = p.sb("Cw", [128, 128], BF16)
    expb = p.sb("expb", [128, 128], BF16)
    dtxw = p.sb("dtxw", [128, 64], BF16)
    hts = p.sb("hts", [128, NH * 64], F32)
    htb = p.sb("htb", [128, 64], BF16)
    oy = p.sb("oy", [64, 128], BF16)
    psCB = p.ps("psCB", [128, 128], F32)
    psBT = p.ps("psBT", [128, 128], BF16)
    psb = p.ps("psb", [128, 128], F32)
    ps2c = p.ps("ps2c", [128, 2], F32)
    psY = p.ps("psY", [64, 128], F32)
    psH = p.ps("psH", [128, 64], F32)
    psLR = p.ps("psLR", [1, 128], F32)
    psLT = p.ps("psLT", [128, 16], F32)

    p.sop("sync", lambda e: e.dma_start(out=identf[:, :], in_=ident),
          dma=True)
    p.sop("sync", lambda e: e.dma_start(out=mask_sb[:, :], in_=maskL),
          dma=True)
    p.sop("vector", lambda e: e.tensor_copy(p._identb[:, :], identf[:, :]))
    p.sop("sync", lambda e: e.dma_start(out=Bsb[:, :], in_=Bd), dma=True)
    p.sop("sync", lambda e: e.dma_start(out=Csb[:, :], in_=Cd), dma=True)
    p.sop("sync", lambda e: e.dma_start(out=llsb[:, :], in_=lld), dma=True)
    p.sop("vector", lambda e: e.memset(hts[:, :], 0.0))
    p.sop("vector", lambda e: e.memset(ones16[:, :], 1.0))

    for ch in range(nchk):
        p.sop("sync", lambda e, ch=ch: e.dma_start(
            out=dtxc[:, :], in_=dtxd[ch * 128:(ch + 1) * 128, :]), dma=True)
        p.sop("vector", lambda e, ch=ch: e.tensor_tensor_scan(
            llcall[0:NH, :], llsb[0:NH, ch * CH:(ch + 1) * CH],
            ones16[0:NH, :], 0.0, ALU.add, ALU.mult))
        p.sop("tensor", lambda e: e.transpose(
            psLT[:, 0:NH], llcall[0:NH, :], identf[0:NH, 0:NH]))
        p.sop("vector", lambda e: e.tensor_copy(ltT[:, :], psLT[:, 0:NH]))
        p.sop("vector", lambda e: e.tensor_scalar_mul(nltT[:, :], ltT[:, :],
                                                      -1.0))
        p.sop("tensor", lambda e, ch=ch: e.matmul(
            psCB[:, :], Bsb[:, ch * CH:(ch + 1) * CH],
            Csb[:, ch * CH:(ch + 1) * CH], start=True, stop=True))
        p.sop("vector", lambda e: e.tensor_copy(CBs[:, :], psCB[:, :]))
        p.sop("tensor", lambda e, ch=ch: e.transpose(
            psBT[:, :], Bsb[:, ch * CH:(ch + 1) * CH], p._identb[:, :]))
        p.sop("vector", lambda e: e.tensor_copy(Btok[:, :], psBT[:, :]))
        for h in range(NH):
            p.sop("tensor", lambda e, h=h: e.transpose(
                psLR[0:1, :], ltT[:, h:h + 1], identf[:, :]))
            p.sop("vector", lambda e: e.tensor_copy(llc[0:1, :],
                                                    psLR[0:1, :]))
            p.sop("tensor", lambda e: e.matmul(
                psb[:, :], p._onesrow_f[0:1, 0:128], llc[0:1, :],
                start=True, stop=True))
            p.sop("vector", lambda e: e.tensor_copy(bcsb[:, :], psb[:, :]))
            p.sop("tensor", lambda e: e.matmul(
                ps2c[:, 1:2], p._onesrow_f[0:1, 0:128], llc[0:1, 127:128],
                start=True, stop=True, skip_group_check=True))
            p.sop("vector", lambda e: e.tensor_copy(lc128[:, :],
                                                    ps2c[:, 1:2]))
            p.sop("scalar", lambda e, h=h: e.activation(
                wcol[:, :], ltT[:, h:h + 1], AF.Exp, bias=lc128[:, 0:1],
                scale=-1.0))
            p.sop("scalar", lambda e: e.activation(ecall[:, :], lc128[:, :],
                                                   AF.Exp))
            p.sop("vector", lambda e: e.tensor_add(e1in[:, :], bcsb[:, :],
                                                   mask_sb[:, :]))
            p.sop("scalar", lambda e, h=h: e.activation(
                Esb[:, :], e1in[:, :], AF.Exp, bias=nltT[:, h:h + 1]))
            p.sop("scalar", lambda e: e.activation(expb[:, :], bcsb[:, :],
                                                   AF.Exp))
            p.sop("vector", lambda e: e.tensor_mul(Asb[:, :], CBs[:, :],
                                                   Esb[:, :]))
            p.sop("vector", lambda e, ch=ch: e.tensor_mul(
                Cw[:, :], Csb[:, ch * CH:(ch + 1) * CH], expb[:, :]))
            p.sop("vector", lambda e, h=h: e.tensor_copy(
                htb[:, :], hts[:, h * 64:(h + 1) * 64]))
            p.sop("tensor", lambda e, h=h: e.matmul(
                psY[:, :], dtxc[:, h * 64:(h + 1) * 64], Asb[:, :],
                start=True, stop=False, skip_group_check=True))
            p.sop("tensor", lambda e: e.matmul(
                psY[:, :], htb[:, :], Cw[:, :],
                start=False, stop=True, skip_group_check=True))
            p.sop("vector", lambda e: e.tensor_copy(oy[:, :], psY[:, :]))
            p.sop("sync", lambda e, h=h, ch=ch: e.dma_start(
                out=yd[h * 64:(h + 1) * 64, ch * CH:(ch + 1) * CH],
                in_=oy[:, :]), dma=True)
            p.sop("vector", lambda e, h=h: e.tensor_scalar(
                dtxw[:, :], dtxc[:, h * 64:(h + 1) * 64], wcol[:, 0:1],
                None, ALU.mult))
            p.sop("tensor", lambda e: e.matmul(
                psH[:, :], Btok[:, :], dtxw[:, :], start=True, stop=True))
            p.sop("vector", lambda e, h=h: e.tensor_scalar(
                hts[:, h * 64:(h + 1) * 64], hts[:, h * 64:(h + 1) * 64],
                ecall[:, 0:1], None, ALU.mult))
            p.sop("vector", lambda e, h=h: e.tensor_add(
                hts[:, h * 64:(h + 1) * 64], hts[:, h * 64:(h + 1) * 64],
                psH[:, :]))
    p.wait("sync", "seq")
    return p.build()



def build_mtail():
    p = Prog()
    DI, NCI = 2048, 16
    yssm = p.dram("yssm", [DI, TH], BF16, kind="ExternalInput")
    xsf = p.dram("xsf", [DI, TH], BF16, kind="ExternalInput")
    zd = p.dram("z", [DI, TH], BF16, kind="ExternalInput")
    Drep = p.dram("Drep", [DI], F32, kind="ExternalInput")
    gnw = p.dram("gnorm_w", [DI], F32, kind="ExternalInput")
    owT = p.dram("out_wT", [DI, D], F32, kind="ExternalInput")
    s1T = p.dram("sc1T", [D, 256], F32, kind="ExternalInput")
    s2T = p.dram("sc2T", [256, 1], F32, kind="ExternalInput")
    smT = p.dram("summT", [D, 64], F32, kind="ExternalInput")
    qrT = p.dram("qrT", [D, 64], F32, kind="ExternalInput")
    y2tmp = p.dram("y2tmp", [DI, TH], BF16, kind="Internal")
    y1024d = p.dram("y1024", [D, TH], BF16, kind="ExternalOutput")
    scod = p.dram("scores", [1, TH], F32, kind="ExternalOutput")
    sumd = p.dram("summtok", [TH, 64], BF16, kind="ExternalOutput")
    qrd = p.dram("qrf", [64, TH], BF16, kind="ExternalOutput")

    common_tiles(p)
    ow_sb = p.sb("ow_sb", [128, NCI * D], BF16)
    s1_sb = p.sb("s1_sb", [128, NCH * 256], BF16)
    s2_sb = p.sb("s2_sb", [128, 2 * 1], BF16)
    sm_sb = p.sb("sm_sb", [128, NCH * 64], BF16)
    qr_sb = p.sb("qr_sb", [128, NCH * 64], BF16)
    D_c = p.sb("D_c", [128, NCI], F32)
    gn_c = p.sb("gn_c", [128, NCI], F32)
    by = p.sb("by", [128, NCI * 512], BF16)
    bx = p.sb("bx", [128, NCI * 512], BF16)
    bz = p.sb("bz", [128, NCI * 512], BF16)
    y1k = p.sb("y1k", [128, NCH * TH], BF16)
    hsc = p.sb("hsc", [128, 2 * 512], BF16)
    sq_row = p.sb("sq_row", [1, TH], F32)
    t1 = p.sb("t1", [128, 512], F32)
    t2 = p.sb("t2", [128, 512], BF16)
    sqt = p.sb("sqt", [128, 512], BF16)
    rowo = p.sb("rowo", [1, 512], F32)
    ps = p.ps("ps", [128, 512], F32)
    psA = p.ps("psA", [128, 512], F32)
    psS = p.ps("psS", [1, 512], F32)

    p.sop("gpsimd", lambda e: e.dma_start(
        out=ow_sb[:, :].rearrange("p (c n) -> p c n", c=NCI),
        in_=owT.rearrange("(c p) n -> p c n", p=128)), dma=True)
    p.sop("gpsimd", lambda e: e.dma_start(
        out=s1_sb[:, :].rearrange("p (c n) -> p c n", c=NCH),
        in_=s1T.rearrange("(c p) n -> p c n", p=128)), dma=True)
    p.sop("gpsimd", lambda e: e.dma_start(
        out=s2_sb[:, :].rearrange("p (c n) -> p c n", c=2),
        in_=s2T.rearrange("(c p) n -> p c n", p=128),
        allow_slow_non_contiguous=True), dma=True)
    p.sop("gpsimd", lambda e: e.dma_start(
        out=sm_sb[:, :].rearrange("p (c n) -> p c n", c=NCH),
        in_=smT.rearrange("(c p) n -> p c n", p=128)), dma=True)
    p.sop("gpsimd", lambda e: e.dma_start(
        out=qr_sb[:, :].rearrange("p (c n) -> p c n", c=NCH),
        in_=qrT.rearrange("(c p) n -> p c n", p=128)), dma=True)
    p.sop("gpsimd", lambda e: e.dma_start(
        out=D_c[:, 0:NCI], in_=Drep.rearrange("(c p) -> p c", p=128),
        allow_slow_non_contiguous=True), dma=True)
    p.sop("gpsimd", lambda e: e.dma_start(
        out=gn_c[:, 0:NCI], in_=gnw.rearrange("(c p) -> p c", p=128),
        allow_slow_non_contiguous=True), dma=True)

    for tb in range(TH // 512):
        for nm, src, buf in (("y", yssm, by), ("x", xsf, bx), ("z", zd, bz)):
            p.sop("sync", lambda e, src=src, buf=buf, tb=tb: e.dma_start(
                out=buf[:, :].rearrange("p (c n) -> p c n", c=NCI),
                in_=src.rearrange("(c p) n -> p c n", p=128)[
                    :, :, tb * 512:(tb + 1) * 512]), dma=True)
        for ci in range(NCI):
            p.sop("vector", lambda e, ci=ci: e.scalar_tensor_tensor(
                t1[:, :], bx[:, ci * 512:(ci + 1) * 512], D_c[:, ci:ci + 1],
                by[:, ci * 512:(ci + 1) * 512], ALU.mult, ALU.add))
            p.sop("scalar", lambda e, ci=ci: e.activation(
                t2[:, :], bz[:, ci * 512:(ci + 1) * 512], AF.Silu))
            p.sop("vector", lambda e: e.tensor_mul(t2[:, :], t1[:, :],
                                                   t2[:, :]))
            p.sop("sync", lambda e, ci=ci, tb=tb: e.dma_start(
                out=y2tmp[ci * 128:(ci + 1) * 128, tb * 512:(tb + 1) * 512],
                in_=t2[:, :]), dma=True)
            p.sop("scalar", lambda e: e.activation(sqt[:, :], t2[:, :],
                                                   AF.Square))
            p.sop("tensor", lambda e, ci=ci: e.matmul(
                psS[0:1, :], p._onescol_b[:, 0:1], sqt[:, :],
                start=(ci == 0), stop=(ci == NCI - 1)))
        p.sop("vector", lambda e, tb=tb: e.tensor_copy(
            sq_row[0:1, tb * 512:(tb + 1) * 512], psS[0:1, :]))
    p.sop("vector", lambda e: e.tensor_scalar_mul(sq_row[0:1, :],
                                                  sq_row[0:1, :], 1.0 / DI))
    p.sop("scalar", lambda e: e.activation(p._rtmp[0:1, :], sq_row[0:1, :],
                                           AF.Sqrt,
                                           bias=p._eps_col[0:1, 0:1]))
    p.sop("vector", lambda e: e.reciprocal(sq_row[0:1, :], p._rtmp[0:1, :]))
    for tb in range(TH // 512):
        p.sop("sync", lambda e, tb=tb: e.dma_start(
            out=by[:, :].rearrange("p (c n) -> p c n", c=NCI),
            in_=y2tmp.rearrange("(c p) n -> p c n", p=128)[
                :, :, tb * 512:(tb + 1) * 512]), dma=True)
        p.sop("tensor", lambda e, tb=tb: e.matmul(
            psA[:, :], p._onesrow_f[0:1, 0:128],
            sq_row[0:1, tb * 512:(tb + 1) * 512], start=True, stop=True))
        for ci in range(NCI):
            p.sop("vector", lambda e, ci=ci: e.tensor_mul(
                t1[:, :], by[:, ci * 512:(ci + 1) * 512], psA[:, :]))
            p.sop("vector", lambda e, ci=ci: e.tensor_scalar(
                by[:, ci * 512:(ci + 1) * 512], t1[:, :], gn_c[:, ci:ci + 1],
                None, ALU.mult))
        for co in range(NCH):
            for ci in range(NCI):
                p.sop("tensor", lambda e, co=co, ci=ci: e.matmul(
                    ps[:, :], ow_sb[:, ci * D + co * 128:ci * D + co * 128 + 128],
                    by[:, ci * 512:(ci + 1) * 512],
                    start=(ci == 0), stop=(ci == NCI - 1)))
            p.sop("scalar", lambda e: e.activation(t2[:, :], ps[:, :],
                                                   AF.Copy))
            p.sop("vector", lambda e, co=co, tb=tb: e.tensor_copy(
                y1k[:, co * TH + tb * 512:co * TH + tb * 512 + 512],
                t2[:, :]))
            p.sop("sync", lambda e, co=co, tb=tb: e.dma_start(
                out=y1024d[co * 128:(co + 1) * 128, tb * 512:(tb + 1) * 512],
                in_=t2[:, :]), dma=True)
    for tb in range(TH // 512):
        for co in range(2):
            for ci in range(NCH):
                p.sop("tensor", lambda e, co=co, ci=ci, tb=tb: e.matmul(
                    ps[:, :],
                    s1_sb[:, ci * 256 + co * 128:ci * 256 + co * 128 + 128],
                    y1k[:, ci * TH + tb * 512:ci * TH + tb * 512 + 512],
                    start=(ci == 0), stop=(ci == NCH - 1)))
            p.sop("scalar", lambda e, co=co: e.activation(
                hsc[:, co * 512:(co + 1) * 512], ps[:, :], AF.Relu))
        for ci in range(2):
            p.sop("tensor", lambda e, ci=ci: e.matmul(
                psS[0:1, :], s2_sb[:, ci:ci + 1],
                hsc[:, ci * 512:(ci + 1) * 512],
                start=(ci == 0), stop=(ci == 1)))
        p.sop("scalar", lambda e: e.activation(rowo[0:1, :], psS[0:1, :],
                                               AF.Sigmoid))
        p.sop("sync", lambda e, tb=tb: e.dma_start(
            out=scod[0:1, tb * 512:(tb + 1) * 512], in_=rowo[0:1, :]),
            dma=True)
    for tt in range(TH // 128):
        for ci in range(NCH):
            p.sop("tensor", lambda e, tt=tt, ci=ci: e.matmul(
                ps[0:128, 0:64],
                y1k[:, ci * TH + tt * 128:ci * TH + tt * 128 + 128],
                sm_sb[:, ci * 64:(ci + 1) * 64],
                start=(ci == 0), stop=(ci == NCH - 1)))
        p.sop("scalar", lambda e: e.activation(t2[:, 0:64], ps[:, 0:64],
                                               AF.Copy))
        p.sop("sync", lambda e, tt=tt: e.dma_start(
            out=sumd[tt * 128:(tt + 1) * 128, :], in_=t2[0:128, 0:64]),
            dma=True)
    for tb in range(TH // 512):
        for ci in range(NCH):
            p.sop("tensor", lambda e, ci=ci, tb=tb: e.matmul(
                ps[0:64, :], qr_sb[:, ci * 64:(ci + 1) * 64],
                y1k[:, ci * TH + tb * 512:ci * TH + tb * 512 + 512],
                start=(ci == 0), stop=(ci == NCH - 1)))
        p.sop("scalar", lambda e: e.activation(t2[0:64, :], ps[0:64, :],
                                               AF.Copy))
        p.sop("sync", lambda e, tb=tb: e.dma_start(
            out=qrd[0:64, tb * 512:(tb + 1) * 512], in_=t2[0:64, 0:512]),
            dma=True)
    p.wait("sync", "seq")
    return p.build()



def build_retr():
    p = Prog()
    scoF = p.dram("scoresF", [1, T], F32, kind="ExternalInput")
    sumF = p.dram("summF", [T, 64], BF16, kind="ExternalInput")
    y1024 = p.dram("y1024", [D, TH], BF16, kind="ExternalInput")
    qrd = p.dram("qrf", [64, TH], BF16, kind="ExternalInput")
    xn1 = p.dram("xn1", [D, TH], F32, kind="ExternalInput")
    kwT = p.dram("k_wT", [64, 64], F32, kind="ExternalInput")
    vwT = p.dram("v_wT", [64, D], F32, kind="ExternalInput")
    gwAT = p.dram("gwAT", [D, D], F32, kind="ExternalInput")
    gwBT = p.dram("gwBT", [D, D], F32, kind="ExternalInput")
    ident = p.dram("ident", [128, 128], F32, kind="ExternalInput")
    a1d = p.dram("a1", [D, TH], BF16, kind="ExternalOutput")

    common_tiles(p)
    identf = p.sb("identf", [128, 128], F32)
    kw_sb = p.sb("kw_sb", [64, 64], BF16)
    vw_sb = p.sb("vw_sb", [64, D], BF16)
    gA_sb = p.sb("gA_sb", [128, NCH * D], BF16)
    gB_sb = p.sb("gB_sb", [128, NCH * D], BF16)
    y1k = p.sb("y1k", [128, NCH * TH], BF16)
    qr_sb = p.sb("qr_sb", [64, TH], BF16)
    sco = p.sb("sco", [32, 256], F32)
    srtA = p.sb("srtA", [32, 256], F32)
    srtB = p.sb("srtB", [32, 256], F32)
    mxs = p.sb("mxs", [32, 56], F32)
    cand = p.sb("cand", [1, 1792], F32)
    cndB = p.sb("cndB", [1, 1792], F32)
    mx8 = p.sb("mx8", [1, 64], F32)
    r1 = p.sb("r1", [32, 256], F32)
    r2 = p.sb("r2", [32, 256], F32)
    rk = p.sb("rk", [32, 256], F32)
    on32 = p.sb("on32", [32, 256], F32)
    col32 = p.sb("col32", [32, 4], F32)
    segrow = p.sb("segrow", [1, 32], F32)
    sc1 = p.sb("sc1", [1, 8], F32)
    rkrow = p.sb("rkrow", [1, 128], F32)
    iota50 = p.sb("iota50", [128, 50], mybir.dt.int32)
    iota50f = p.sb("iota50f", [128, 50], F32)
    sumt = p.sb("sumt", [128, 64], BF16)
    rkcol = p.sb("rkcol", [128, 1], F32)
    oht = p.sb("oht", [128, 50], BF16)
    pool = p.sb("pool", [50, 64], BF16)
    poolT = p.sb("poolT", [64, 50], BF16)
    krT = p.sb("krT", [64, 50], BF16)
    vr = p.sb("vr", [50, D], BF16)
    maddb = p.sb("maddb", [128, 50], F32)
    lg = p.sb("lg", [128, 50], F32)
    mxc = p.sb("mxc", [128, 1], F32)
    dsum = p.sb("dsum", [128, 1], F32)
    esb = p.sb("esb", [128, 50], BF16)
    esT = p.sb("esT", [50, 128], BF16)
    retr = p.sb("retr", [128, NCH * 128], BF16)
    rdb = p.sb("rdb", [128, 128], F32)
    rfcol = p.sb("rfcol", [128, 1], F32)
    xb_ = p.sb("xb_", [128, NCH * 128], BF16)
    gateb = p.sb("gateb", [128, 128], BF16)
    t1 = p.sb("t1", [128, 128], F32)
    t2 = p.sb("t2", [128, 128], BF16)
    psP = p.ps("psP", [50, 64], F32)
    psL = p.ps("psL", [128, 50], F32)
    psT = p.ps("psT", [128, 128], BF16)
    psR = p.ps("psR", [128, 128], F32)
    psA2 = p.ps("psA2", [128, 128], F32)
    psB2 = p.ps("psB2", [128, 128], F32)
    psM = p.ps("psM", [128, 50], F32)
    psD2 = p.ps("psD2", [1, 128], F32)

    p.sop("sync", lambda e: e.dma_start(out=identf[:, :], in_=ident),
          dma=True)
    p.sop("vector", lambda e: e.tensor_copy(p._identb[:, :], identf[:, :]))
    p.sop("gpsimd", lambda e: e.dma_start(out=kw_sb[:, :], in_=kwT),
          dma=True)
    p.sop("gpsimd", lambda e: e.dma_start(out=vw_sb[:, :], in_=vwT),
          dma=True)
    p.sop("gpsimd", lambda e: e.dma_start(
        out=gA_sb[:, :].rearrange("p (c n) -> p c n", c=NCH),
        in_=gwAT.rearrange("(c p) n -> p c n", p=128)), dma=True)
    p.sop("gpsimd", lambda e: e.dma_start(
        out=gB_sb[:, :].rearrange("p (c n) -> p c n", c=NCH),
        in_=gwBT.rearrange("(c p) n -> p c n", p=128)), dma=True)
    p.sop("sync", lambda e: e.dma_start(
        out=y1k[:, :].rearrange("p (c n) -> p c n", c=NCH),
        in_=y1024.rearrange("(c p) n -> p c n", p=128)), dma=True)
    p.sop("sync", lambda e: e.dma_start(out=qr_sb[:, :], in_=qrd), dma=True)
    p.sop("gpsimd", lambda e: e.iota(iota50[:, :], pattern=[[1, 50]], base=1,
                                     channel_multiplier=0))
    p.sop("vector", lambda e: e.tensor_copy(iota50f[:, :], iota50[:, :]))

    candd = p.dram("candd", [32, 56], F32, kind="Internal")
    rkd = p.dram("rkd", [1, T], F32, kind="Internal")

    def bc32(src11, dstcol):
        p.sop("tensor", lambda e: e.matmul(
            psL[0:32, 0:1], p._onesrow_f[0:1, 0:32], src11,
            start=True, stop=True))
        p.sop("vector", lambda e: e.tensor_copy(dstcol, psL[0:32, 0:1]))

    def scan32(xin, out):
        p.sop("vector", lambda e: e.tensor_tensor_scan(
            out[0:32, :], xin[0:32, :], on32[0:32, :], 0.0, ALU.add,
            ALU.mult))
        p.sop("tensor", lambda e: e.transpose(
            psD2[0:1, 0:32], out[0:32, 255:256], identf[0:32, 0:32]))
        p.sop("vector", lambda e: e.tensor_copy(segrow[0:1, :],
                                                psD2[0:1, 0:32]))
        p.sop("vector", lambda e: e.tensor_tensor_scan(
            segrow[0:1, :], segrow[0:1, :], p._onesrow_f[0:1, 0:32], 0.0,
            ALU.add, ALU.mult))
        p.sop("tensor", lambda e: e.transpose(
            psL[0:32, 0:1], segrow[0:1, :], p._onesrow_f[0:1, 0:1]))
        p.sop("vector", lambda e: e.tensor_copy(col32[:, 0:1],
                                                psL[0:32, 0:1]))
        p.sop("vector", lambda e: e.tensor_sub(col32[:, 1:2], col32[:, 0:1],
                                               out[0:32, 255:256]))
        p.sop("vector", lambda e, out=out: e.tensor_scalar(
            out[0:32, :], out[0:32, :], col32[:, 1:2], None, ALU.add))

    p.sop("sync", lambda e: e.dma_start(
        out=sco[:, :], in_=scoF[0, :].rearrange("(p n) -> p n", p=32)),
        dma=True)
    p.sop("vector", lambda e: e.memset(on32[:, :], 1.0))
    p.sop("vector", lambda e: e.tensor_copy(srtA[:, :], sco[:, :]))
    cur, nxt = srtA, srtB
    for rnd in range(7):
        p.sop("vector", lambda e, cur=cur, rnd=rnd: e.max(
            out=mxs[:, rnd * 8:(rnd + 1) * 8], in_=cur[:, :]))
        p.sop("vector", lambda e, cur=cur, nxt=nxt, rnd=rnd: e.match_replace(
            out=nxt[:, :], in_to_replace=mxs[:, rnd * 8:(rnd + 1) * 8],
            in_values=cur[:, :], imm_value=-1e30))
        cur, nxt = nxt, cur
    p.sop("sync", lambda e: e.dma_start(out=candd, in_=mxs[:, :]), dma=True)
    p.sop("sync", lambda e: e.dma_start(
        out=cand[0:1, :], in_=candd.rearrange("p n -> (p n)")[None, :]),
        dma=True)
    for rnd in range(7):
        p.sop("vector", lambda e, cur2=(cand if rnd % 2 == 0 else cndB),
              rnd=rnd: e.max(out=mx8[0:1, rnd * 8:(rnd + 1) * 8],
                             in_=cur2[0:1, :]))
        p.sop("vector", lambda e, cur2=(cand if rnd % 2 == 0 else cndB),
              nxt2=(cndB if rnd % 2 == 0 else cand), rnd=rnd:
              e.match_replace(out=nxt2[0:1, :],
                              in_to_replace=mx8[0:1, rnd * 8:(rnd + 1) * 8],
                              in_values=cur2[0:1, :], imm_value=-1e30))
    p.sop("vector", lambda e: e.tensor_scalar_max(
        sc1[0:1, 0:1], mx8[0:1, 50:51], 0.5))
    bc32(sc1[0:1, 0:1], col32[:, 2:3])
    p.sop("vector", lambda e: e.tensor_scalar(
        r1[:, :], sco[:, :], col32[:, 2:3], None, ALU.is_gt))
    p.sop("vector", lambda e: e.tensor_scalar(
        r2[:, :], sco[:, :], 0.5, None, ALU.is_gt))
    p.sop("vector", lambda e: e.tensor_reduce(col32[:, 3:4], r2[:, :], AX.X,
                                              ALU.add))
    p.sop("tensor", lambda e: e.matmul(psD2[0:1, 0:1], col32[0:32, 3:4],
                                       p._ones_f[0:32, 0:1],
                                       start=True, stop=True))
    p.sop("vector", lambda e: e.tensor_copy(sc1[0:1, 1:2], psD2[0:1, 0:1]))
    p.sop("vector", lambda e: e.tensor_scalar_min(sc1[0:1, 1:2],
                                                  sc1[0:1, 1:2], 50.0))
    p.sop("vector", lambda e: e.tensor_reduce(col32[:, 3:4], r1[:, :], AX.X,
                                              ALU.add))
    p.sop("tensor", lambda e: e.matmul(psD2[0:1, 0:1], col32[0:32, 3:4],
                                       p._ones_f[0:32, 0:1],
                                       start=True, stop=True))
    p.sop("vector", lambda e: e.tensor_copy(sc1[0:1, 2:3], psD2[0:1, 0:1]))
    p.sop("vector", lambda e: e.tensor_sub(sc1[0:1, 3:4], sc1[0:1, 1:2],
                                           sc1[0:1, 2:3]))
    p.sop("vector", lambda e: e.tensor_scalar(
        r2[:, :], sco[:, :], col32[:, 2:3], None, ALU.is_equal))
    scan32(r2, rk)
    bc32(sc1[0:1, 3:4], col32[:, 2:3])
    p.sop("vector", lambda e: e.tensor_scalar(
        rk[:, :], rk[:, :], col32[:, 2:3], None, ALU.is_le))
    p.sop("vector", lambda e: e.tensor_mul(r2[:, :], r2[:, :], rk[:, :]))
    p.sop("vector", lambda e: e.tensor_add(r1[:, :], r1[:, :], r2[:, :]))
    p.sop("vector", lambda e: e.tensor_reduce(col32[:, 3:4], r1[:, :], AX.X,
                                              ALU.add))
    p.sop("tensor", lambda e: e.matmul(psD2[0:1, 0:1], col32[0:32, 3:4],
                                       p._ones_f[0:32, 0:1],
                                       start=True, stop=True))
    p.sop("vector", lambda e: e.tensor_copy(sc1[0:1, 4:5], psD2[0:1, 0:1]))
    scan32(r1, rk)
    p.sop("vector", lambda e: e.tensor_mul(rk[:, :], rk[:, :], r1[:, :]))
    p.sop("vector", lambda e: e.tensor_add(rk[:, :], rk[:, :], r1[:, :]))
    p.sop("vector", lambda e: e.tensor_scalar_add(rk[:, :], rk[:, :], -1.0))
    p.sop("sync", lambda e: e.dma_start(
        out=rkd[0, :].rearrange("(p n) -> p n", p=32), in_=rk[:, :]),
        dma=True)
    p.sop("vector", lambda e: e.tensor_reduce(col32[:, 3:4], sco[:, :], AX.X,
                                              ALU.add))
    p.sop("tensor", lambda e: e.matmul(psD2[0:1, 0:1], col32[0:32, 3:4],
                                       p._ones_f[0:32, 0:1],
                                       start=True, stop=True))
    p.sop("vector", lambda e: e.tensor_copy(sc1[0:1, 5:6], psD2[0:1, 0:1]))
    p.sop("vector", lambda e: e.tensor_scalar_mul(sc1[0:1, 5:6],
                                                  sc1[0:1, 5:6], 1.0 / T))
    p.sop("vector", lambda e: e.tensor_scalar(sc1[0:1, 5:6], sc1[0:1, 5:6],
                                              0.3, None, ALU.is_gt))
    p.sop("vector", lambda e: e.tensor_scalar(sc1[0:1, 6:7], sc1[0:1, 4:5],
                                              0.5, None, ALU.is_gt))
    p.sop("vector", lambda e: e.tensor_mul(sc1[0:1, 5:6], sc1[0:1, 5:6],
                                           sc1[0:1, 6:7]))
    p.sop("tensor", lambda e: e.matmul(
        psL[:, 0:1], p._onesrow_f[0:1, 0:128], sc1[0:1, 5:6],
        start=True, stop=True))
    p.sop("vector", lambda e: e.tensor_copy(rfcol[:, :], psL[:, 0:1]))
    for tt in range(T // 128):
        p.sop("sync", lambda e, tt=tt: e.dma_start(
            out=sumt[:, :], in_=sumF[tt * 128:(tt + 1) * 128, :]), dma=True)
        p.sop("sync", lambda e, tt=tt: e.dma_start(
            out=rkrow[0:1, :], in_=rkd[0:1, tt * 128:(tt + 1) * 128]),
            dma=True)
        p.sop("tensor", lambda e: e.transpose(
            psL[:, 0:1], rkrow[0:1, :], p._onesrow_f[0:1, 0:1]))
        p.sop("vector", lambda e: e.tensor_copy(rkcol[:, :], psL[:, 0:1]))
        p.sop("vector", lambda e: e.tensor_scalar(
            oht[:, :], iota50f[:, :], rkcol[:, 0:1], None, ALU.is_equal))
        p.sop("tensor", lambda e, tt=tt: e.matmul(
            psP[:, :], oht[:, :], sumt[:, :],
            start=(tt == 0), stop=(tt == T // 128 - 1)))
    p.sop("vector", lambda e: e.tensor_copy(pool[:, :], psP[:, :]))
    p.sop("tensor", lambda e: e.transpose(psT[0:64, 0:50], pool[:, :],
                                          p._identb[0:50, 0:50]))
    p.sop("vector", lambda e: e.tensor_copy(poolT[:, :], psT[0:64, 0:50]))
    p.sop("tensor", lambda e: e.matmul(psL[0:64, 0:50], kw_sb[:, :],
                                       poolT[:, :], start=True, stop=True))
    p.sop("vector", lambda e: e.tensor_copy(krT[:, :], psL[0:64, 0:50]))
    for ob in range(8):
        p.sop("tensor", lambda e, ob=ob: e.matmul(
            psR[0:50, 0:128], poolT[:, :],
            vw_sb[:, ob * 128:(ob + 1) * 128], start=True, stop=True))
        p.sop("vector", lambda e, ob=ob: e.tensor_copy(
            vr[:, ob * 128:(ob + 1) * 128], psR[0:50, 0:128]))
    madr = p.sb("madr", [1, 50], F32)
    rdrow = p.sb("rdrow", [1, 128], F32)
    p.sop("vector", lambda e: e.tensor_scalar(
        madr[0:1, :], iota50f[0:1, :], sc1[0:1, 4:5], None, ALU.is_le))
    p.sop("vector", lambda e: e.tensor_scalar(
        madr[0:1, :], madr[0:1, :], 1.0, 1e9, ALU.subtract, ALU.mult))
    p.sop("tensor", lambda e: e.matmul(
        psM[:, :], p._onesrow_f[0:1, 0:128], madr[0:1, :],
        start=True, stop=True))
    p.sop("vector", lambda e: e.tensor_copy(maddb[:, :], psM[:, :]))
    for tt in range(TH // 128):
        p.sop("gpsimd", lambda e, tt=tt: e.dma_start(
            out=xb_[:, :].rearrange("p (c n) -> p c n", c=NCH),
            in_=xn1.rearrange("(c p) n -> p c n", p=128)[
                :, :, tt * 128:(tt + 1) * 128]), dma=True)
        p.sop("tensor", lambda e, tt=tt: e.matmul(
            psL[:, :], qr_sb[:, tt * 128:(tt + 1) * 128], krT[:, :],
            start=True, stop=True))
        p.sop("vector", lambda e: e.scalar_tensor_tensor(
            lg[:, :], psL[:, :], 0.25, maddb[:, :], ALU.mult, ALU.add))
        p.sop("vector", lambda e: e.tensor_reduce(mxc[:, :], lg[:, :], AX.X,
                                                  ALU.max))
        p.sop("vector", lambda e: e.tensor_scalar_mul(mxc[:, :], mxc[:, :],
                                                      -1.0))
        p.sop("scalar", lambda e: e.activation(
            esb[:, :], lg[:, :], AF.Exp, bias=mxc[:, 0:1],
            accum_out=dsum[:, 0:1]))
        p.sop("vector", lambda e: e.reciprocal(dsum[:, :], dsum[:, :]))
        p.sop("tensor", lambda e: e.transpose(psT[0:50, 0:128], esb[:, :],
                                              p._identb[:, :]))
        p.sop("vector", lambda e: e.tensor_copy(esT[:, :], psT[0:50, 0:128]))
        for co in range(NCH):
            p.sop("tensor", lambda e, co=co: e.matmul(
                psR[:, :], vr[0:50, co * 128:(co + 1) * 128], esT[:, :],
                start=True, stop=True))
            p.sop("vector", lambda e, co=co: e.tensor_copy(
                retr[:, co * 128:(co + 1) * 128], psR[:, :]))
        p.sop("tensor", lambda e: e.transpose(psD2[0:1, :], dsum[:, :],
                                              identf[:, :]))
        p.sop("vector", lambda e: e.tensor_copy(rdrow[0:1, :], psD2[0:1, :]))
        p.sop("tensor", lambda e: e.matmul(
            psA2[:, :], p._onesrow_f[0:1, 0:128], rdrow[0:1, :],
            start=True, stop=True))
        p.sop("vector", lambda e: e.tensor_copy(rdb[:, :], psA2[:, :]))
        for co in range(NCH):
            for ci in range(NCH):
                p.sop("tensor", lambda e, co=co, ci=ci, tt=tt: e.matmul(
                    psA2[:, :],
                    gA_sb[:, ci * D + co * 128:ci * D + co * 128 + 128],
                    y1k[:, ci * TH + tt * 128:ci * TH + tt * 128 + 128],
                    start=(ci == 0), stop=(ci == NCH - 1)))
            for ci in range(NCH):
                p.sop("tensor", lambda e, co=co, ci=ci: e.matmul(
                    psB2[:, :],
                    gB_sb[:, ci * D + co * 128:ci * D + co * 128 + 128],
                    retr[:, ci * 128:(ci + 1) * 128],
                    start=(ci == 0), stop=(ci == NCH - 1)))
            p.sop("vector", lambda e: e.tensor_mul(t1[:, :], psB2[:, :],
                                                   rdb[:, :]))
            p.sop("vector", lambda e: e.tensor_add(t1[:, :], t1[:, :],
                                                   psA2[:, :]))
            p.sop("scalar", lambda e: e.activation(gateb[:, :], t1[:, :],
                                                   AF.Sigmoid))
            p.sop("vector", lambda e, co=co: e.tensor_mul(
                t2[:, :], gateb[:, :], retr[:, co * 128:(co + 1) * 128]))
            p.sop("vector", lambda e: e.tensor_mul(t1[:, :], t2[:, :],
                                                   rdb[:, :]))
            p.sop("vector", lambda e: e.tensor_scalar(
                t1[:, :], t1[:, :], rfcol[:, 0:1], None, ALU.mult))
            p.sop("vector", lambda e, co=co, tt=tt: e.tensor_add(
                t1[:, :], t1[:, :],
                y1k[:, co * TH + tt * 128:co * TH + tt * 128 + 128]))
            p.sop("vector", lambda e, co=co: e.tensor_add(
                t2[:, :], t1[:, :], xb_[:, co * 128:(co + 1) * 128]))
            p.sop("sync", lambda e, co=co, tt=tt: e.dma_start(
                out=a1d[co * 128:(co + 1) * 128, tt * 128:(tt + 1) * 128],
                in_=t2[:, :]), dma=True)
    p.wait("sync", "seq")
    return p.build()



def build_l1route():
    p = Prog()
    xn1 = p.dram("xn1", [D, TH], F32, kind="ExternalInput")
    vy1 = p.dram("vy1", [1, TH], F32, kind="ExternalInput")
    ctT = p.dram("centsT", [D, 8], F32, kind="ExternalInput")
    wrT = p.dram("w_routeT", [D, 8], F32, kind="ExternalInput")
    ident = p.dram("ident", [128, 128], F32, kind="ExternalInput")
    gd = p.dram("gatesF", [8, TH], F32, kind="ExternalOutput")

    common_tiles(p)
    identf = p.sb("identf", [128, 128], F32)
    ct_sb = p.sb("ct_sb", [128, NCH * 8], BF16)
    ctf_sb = p.sb("ctf_sb", [128, NCH * 8], F32)
    wr_sb = p.sb("wr_sb", [128, NCH * 8], BF16)
    s2r = p.sb("s2r", [1, TH], F32)
    cnrow = p.sb("cnrow", [1, 8], F32)
    cnb = p.sb("cnb", [128, 8], F32)
    xb_ = p.sb("xb_", [128, NCH * 128], BF16)
    lg = p.sb("lg", [128, 8], F32)
    scol = p.sb("scol", [128, 1], F32)
    m1c = p.sb("m1c", [128, 1], F32)
    oh = p.sb("oh", [128, 8], F32)
    cum = p.sb("cum", [128, 8], F32)
    oh1 = p.sb("oh1", [128, 8], F32)
    keep = p.sb("keep", [128, 8], F32)
    es = p.sb("es", [128, 8], F32)
    sumc = p.sb("sumc", [128, 1], F32)
    gt = p.sb("gt", [128, 8], F32)
    gtf = p.sb("gtf", [8, 128], F32)
    sqc = p.sb("sqc", [128, 8], F32)
    psA = p.ps("psA", [128, 8], F32)
    psB = p.ps("psB", [128, 8], F32)
    psr = p.ps("psr", [1, 8], F32)
    pst = p.ps("pst", [128, 128], F32)

    p.sop("sync", lambda e: e.dma_start(out=identf[:, :], in_=ident),
          dma=True)
    p.sop("gpsimd", lambda e: e.dma_start(
        out=ct_sb[:, :].rearrange("p (c n) -> p c n", c=NCH),
        in_=ctT.rearrange("(c p) n -> p c n", p=128)), dma=True)
    p.sop("sync", lambda e: e.dma_start(
        out=ctf_sb[:, :].rearrange("p (c n) -> p c n", c=NCH),
        in_=ctT.rearrange("(c p) n -> p c n", p=128)), dma=True)
    p.sop("gpsimd", lambda e: e.dma_start(
        out=wr_sb[:, :].rearrange("p (c n) -> p c n", c=NCH),
        in_=wrT.rearrange("(c p) n -> p c n", p=128)), dma=True)
    p.sop("sync", lambda e: e.dma_start(out=s2r[0:1, :], in_=vy1[0:1, :]),
          dma=True)
    p.sop("scalar", lambda e: e.activation(p._rtmp[0:1, :], s2r[0:1, :],
                                           AF.Sqrt, scale=float(D)))
    p.sop("vector", lambda e: e.tensor_scalar_max(p._rtmp[0:1, :],
                                                  p._rtmp[0:1, :], 1e-12))
    p.sop("vector", lambda e: e.reciprocal(s2r[0:1, :], p._rtmp[0:1, :]))
    for ci in range(NCH):
        p.sop("scalar", lambda e, ci=ci: e.activation(
            sqc[:, :], ctf_sb[:, ci * 8:(ci + 1) * 8], AF.Square))
        p.sop("tensor", lambda e, ci=ci: e.matmul(
            psr[0:1, :], p._ones_f[:, 0:1], sqc[:, :],
            start=(ci == 0), stop=(ci == NCH - 1)))
    p.sop("vector", lambda e: e.tensor_copy(cnrow[0:1, :], psr[0:1, :]))
    p.sop("scalar", lambda e: e.activation(cnrow[0:1, :], cnrow[0:1, :],
                                           AF.Sqrt))
    p.sop("vector", lambda e: e.tensor_scalar_max(cnrow[0:1, :],
                                                  cnrow[0:1, :], 1e-12))
    p.sop("vector", lambda e: e.reciprocal(cnrow[0:1, :], cnrow[0:1, :]))
    p.sop("tensor", lambda e: e.matmul(
        psA[:, :], p._onesrow_f[0:1, 0:128], cnrow[0:1, :],
        start=True, stop=True))
    p.sop("vector", lambda e: e.tensor_copy(cnb[:, :], psA[:, :]))
    for tt in range(TH // 128):
        p.sop("gpsimd", lambda e, tt=tt: e.dma_start(
            out=xb_[:, :].rearrange("p (c n) -> p c n", c=NCH),
            in_=xn1.rearrange("(c p) n -> p c n", p=128)[
                :, :, tt * 128:(tt + 1) * 128]), dma=True)
        for ci in range(NCH):
            p.sop("tensor", lambda e, ci=ci: e.matmul(
                psA[:, :], xb_[:, ci * 128:(ci + 1) * 128],
                ct_sb[:, ci * 8:(ci + 1) * 8],
                start=(ci == 0), stop=(ci == NCH - 1)))
            p.sop("tensor", lambda e, ci=ci: e.matmul(
                psB[:, :], xb_[:, ci * 128:(ci + 1) * 128],
                wr_sb[:, ci * 8:(ci + 1) * 8],
                start=(ci == 0), stop=(ci == NCH - 1)))
        p.sop("tensor", lambda e, tt=tt: e.transpose(
            pst[0:1, 0:128], s2r[0:1, tt * 128:(tt + 1) * 128],
            p._onesrow_f[0:1, 0:1])) if False else None
        p.sop("tensor", lambda e, tt=tt: e.transpose(
            pst[:, 0:1], s2r[0:1, tt * 128:(tt + 1) * 128],
            p._onesrow_f[0:1, 0:1]))
        p.sop("vector", lambda e: e.tensor_copy(scol[:, :], pst[:, 0:1]))
        p.sop("vector", lambda e: e.tensor_mul(lg[:, :], psA[:, :],
                                               cnb[:, :]))
        p.sop("vector", lambda e: e.tensor_scalar(lg[:, :], lg[:, :],
                                                  scol[:, 0:1], None,
                                                  ALU.mult))
        p.sop("vector", lambda e: e.tensor_add(lg[:, :], lg[:, :],
                                               psB[:, :]))
        p.sop("vector", lambda e: e.tensor_reduce(m1c[:, :], lg[:, :], AX.X,
                                                  ALU.max))
        p.sop("vector", lambda e: e.tensor_scalar(
            oh[:, :], lg[:, :], m1c[:, 0:1], None, ALU.is_ge))
        p.sop("vector", lambda e: e.tensor_tensor_scan(
            cum[:, :], oh[:, :], p._ones_f[:, 0:8], 0.0, ALU.add, ALU.mult))
        p.sop("vector", lambda e: e.tensor_scalar(
            oh1[:, :], cum[:, :], 1.0, None, ALU.is_equal))
        p.sop("vector", lambda e: e.tensor_mul(oh1[:, :], oh1[:, :],
                                               oh[:, :]))
        p.sop("vector", lambda e: e.scalar_tensor_tensor(
            es[:, :], oh1[:, :], -1e30, lg[:, :], ALU.mult, ALU.add))
        p.sop("vector", lambda e: e.tensor_reduce(sumc[:, :], es[:, :],
                                                  AX.X, ALU.max))
        p.sop("vector", lambda e: e.tensor_scalar(
            keep[:, :], lg[:, :], sumc[:, 0:1], None, ALU.is_ge))
        p.sop("vector", lambda e: e.tensor_scalar_mul(m1c[:, :], m1c[:, :],
                                                      -1.0))
        p.sop("scalar", lambda e: e.activation(es[:, :], lg[:, :], AF.Exp,
                                               bias=m1c[:, 0:1]))
        p.sop("vector", lambda e: e.tensor_mul(es[:, :], es[:, :],
                                               keep[:, :]))
        p.sop("vector", lambda e: e.tensor_reduce(sumc[:, :], es[:, :],
                                                  AX.X, ALU.add))
        p.sop("vector", lambda e: e.reciprocal(sumc[:, :], sumc[:, :]))
        p.sop("vector", lambda e: e.tensor_scalar(gt[:, :], es[:, :],
                                                  sumc[:, 0:1], None,
                                                  ALU.mult))
        p.sop("tensor", lambda e: e.transpose(pst[0:8, 0:128], gt[:, :],
                                              identf[:, :]))
        p.sop("vector", lambda e: e.tensor_copy(gtf[0:8, :], pst[0:8, 0:128]))
        p.sop("sync", lambda e, tt=tt: e.dma_start(
            out=gd[0:8, tt * 128:(tt + 1) * 128], in_=gtf[0:8, 0:128]),
            dma=True)
    p.wait("sync", "seq")
    return p.build()


def build_l1ffn(dff=4096):
    p = Prog()
    xn1 = p.dram("xn1", [D, TH], F32, kind="ExternalInput")
    w1c = p.dram("w1cat", [D, dff], F32, kind="ExternalInput")
    b1c = p.dram("b1cat", [dff], F32, kind="ExternalInput")
    w2c = p.dram("w2cat", [dff, D], F32, kind="ExternalInput")
    b2d = p.dram("b2", [8, D], F32, kind="ExternalInput")
    gd = p.dram("gatesF", [8, TH], F32, kind="ExternalInput")
    x1d = p.dram("x1", [D, TH], F32, kind="ExternalInput")
    a1d = p.dram("a1", [D, TH], BF16, kind="ExternalInput")
    outd = p.dram("out", [D, TH], F32, kind="ExternalOutput")

    nh = dff // 128
    common_tiles(p)
    w1_sb = p.sb("w1_sb", [128, NCH * dff], BF16)
    w2_sb = p.sb("w2_sb", [128, nh * D], BF16)
    b1_c = p.sb("b1_c", [128, nh], F32)
    b2_sb = p.sb("b2_sb", [8, D], BF16)
    g8 = p.sb("g8", [8, 512], BF16)
    ger = p.sb("ger", [1, 512], F32)
    hb = p.sb("hb", [128, nh * 512], BF16)
    xb_ = p.sb("xb_", [128, NCH * 512], BF16)
    x1b = p.sb("x1b", [128, 512], F32)
    a1b = p.sb("a1b", [128, 512], BF16)
    t1 = p.sb("t1", [128, 512], F32)
    t2 = p.sb("t2", [128, 512], BF16)
    ps = p.ps("ps", [128, 512], F32)
    psG = p.ps("psG", [128, 512], F32)

    p.sop("gpsimd", lambda e: e.dma_start(
        out=w1_sb[:, :].rearrange("p (c n) -> p c n", c=NCH),
        in_=w1c.rearrange("(c p) n -> p c n", p=128)), dma=True)
    p.sop("gpsimd", lambda e: e.dma_start(
        out=w2_sb[:, :].rearrange("p (c n) -> p c n", c=nh),
        in_=w2c.rearrange("(c p) n -> p c n", p=128)), dma=True)
    p.sop("gpsimd", lambda e: e.dma_start(
        out=b1_c[:, 0:nh], in_=b1c.rearrange("(c p) -> p c", p=128),
        allow_slow_non_contiguous=True), dma=True)
    p.sop("gpsimd", lambda e: e.dma_start(out=b2_sb[:, :], in_=b2d),
          dma=True)
    for tb in range(TH // 512):
        p.sop("gpsimd", lambda e, tb=tb: e.dma_start(
            out=xb_[:, :].rearrange("p (c n) -> p c n", c=NCH),
            in_=xn1.rearrange("(c p) n -> p c n", p=128)[
                :, :, tb * 512:(tb + 1) * 512]), dma=True)
        p.sop("gpsimd", lambda e, tb=tb: e.dma_start(
            out=g8[:, :], in_=gd[0:8, tb * 512:(tb + 1) * 512]), dma=True)

        for co in range(nh):
            for ci in range(NCH):
                p.sop("tensor", lambda e, co=co, ci=ci: e.matmul(
                    ps[:, :],
                    w1_sb[:, ci * dff + co * 128:ci * dff + co * 128 + 128],
                    xb_[:, ci * 512:(ci + 1) * 512],
                    start=(ci == 0), stop=(ci == NCH - 1)))
            p.sop("scalar", lambda e, co=co: e.activation(
                hb[:, co * 512:(co + 1) * 512], ps[:, :], AF.Gelu,
                bias=b1_c[:, co:co + 1]))
        for ex in range(8):
            p.sop("sync", lambda e, ex=ex, tb=tb: e.dma_start(
                out=ger[0:1, :], in_=gd[ex:ex + 1, tb * 512:(tb + 1) * 512]),
                dma=True)
            p.sop("tensor", lambda e: e.matmul(
                psG[:, :], p._onesrow_f[0:1, 0:128], ger[0:1, :],
                start=True, stop=True))
            for cc in range(4):
                p.sop("vector", lambda e, ex=ex, cc=cc: e.tensor_mul(
                    hb[:, (4 * ex + cc) * 512:(4 * ex + cc + 1) * 512],
                    hb[:, (4 * ex + cc) * 512:(4 * ex + cc + 1) * 512],
                    psG[:, :]))
        for co in range(NCH):
            for ci in range(nh):
                p.sop("tensor", lambda e, co=co, ci=ci: e.matmul(
                    ps[:, :],
                    w2_sb[:, ci * D + co * 128:ci * D + co * 128 + 128],
                    hb[:, ci * 512:(ci + 1) * 512],
                    start=(ci == 0), stop=False))
            p.sop("tensor", lambda e, co=co: e.matmul(
                ps[:, :], b2_sb[0:8, co * 128:(co + 1) * 128], g8[:, :],
                start=False, stop=True))
            p.sop("sync", lambda e, co=co, tb=tb: e.dma_start(
                out=x1b[:, :],
                in_=x1d[co * 128:(co + 1) * 128, tb * 512:(tb + 1) * 512]),
                dma=True)
            p.sop("sync", lambda e, co=co, tb=tb: e.dma_start(
                out=a1b[:, :],
                in_=a1d[co * 128:(co + 1) * 128, tb * 512:(tb + 1) * 512]),
                dma=True)
            p.sop("vector", lambda e: e.tensor_add(
                t1[:, :], ps[:, :], x1b[:, :]))
            p.sop("vector", lambda e: e.tensor_add(
                t1[:, :], t1[:, :], a1b[:, :]))
            p.sop("sync", lambda e, co=co, tb=tb: e.dma_start(
                out=outd[co * 128:(co + 1) * 128, tb * 512:(tb + 1) * 512],
                in_=t1[:, :]), dma=True)
    p.wait("sync", "seq")
    return p.build()


MASKL = NoneMASKL = NoneMASKL = NoneMASKL = NoneMASKL = NoneMASKL = None
IDENT = None


def _consts():
    global MASKL, IDENT
    if MASKL is None:
        MASKL = np.where(np.arange(128)[None, :] >= np.arange(128)[:, None],
                         0.0, -1e4).astype(np.float32)
        IDENT = np.eye(128, dtype=np.float32)
    return MASKL, IDENT


def chain_l0attn(inputs, upto="a0"):
    maskL, ident = _consts()
    x = inputs["L0_attn_x"]
    g = lambda k: np.ascontiguousarray(inputs[k]).astype(np.float32)
    st = {}
    nc = build_pre()
    xT = shard_rows(x)
    r1 = run_launch(nc, [{"xT": s_} for s_ in xT])
    st["xn0"] = [r["xn0"] for r in r1]
    st["vy"] = [r["vy"] for r in r1]
    if upto == "pre":
        return st
    nc = build_aqkv()
    ins = [{"xn0": st["xn0"][c], "vy": st["vy"][c],
            "q_w": g("L0_attn_q_w"), "k_w": g("L0_attn_k_w"),
            "v_w": g("L0_attn_v_w"), "dec_w": g("L0_attn_dec_w"),
            "q_b": g("L0_attn_q_b"), "k_b": g("L0_attn_k_b"),
            "v_b": g("L0_attn_v_b"), "dec_b": g("L0_attn_dec_b"),
            "ln1_w": g("L0_attn_ln1_w"), "ln1_b": g("L0_attn_ln1_b")}
           for c in range(NCORE)]
    r2 = run_launch(nc, ins)
    nc = build_lion()
    ins = []
    for c in range(NCORE):
        b, j = c // 2, c % 2
        heads = range(8 * j, 8 * j + 8)
        pq = np.concatenate([
            np.concatenate([r2[2 * b]["phiq"][h * 64:(h + 1) * 64, :],
                            r2[2 * b + 1]["phiq"][h * 64:(h + 1) * 64, :]],
                           axis=1) for h in heads], axis=0)
        pk = np.concatenate([
            np.concatenate([r2[2 * b]["phik"][h * 64:(h + 1) * 64, :],
                            r2[2 * b + 1]["phik"][h * 64:(h + 1) * 64, :]],
                           axis=1) for h in heads], axis=0)
        vt = np.concatenate([
            np.concatenate([r2[2 * b]["vtok"][:, h * 64:(h + 1) * 64],
                            r2[2 * b + 1]["vtok"][:, h * 64:(h + 1) * 64]],
                           axis=0) for h in heads], axis=0)
        ll = np.stack([np.concatenate([r2[2 * b]["loglam"][h],
                                       r2[2 * b + 1]["loglam"][h]])
                       for h in heads], axis=0)
        ins.append({"phiq": np.ascontiguousarray(pq),
                    "phik": np.ascontiguousarray(pk),
                    "vtok": np.ascontiguousarray(vt),
                    "loglam": np.ascontiguousarray(ll),
                    "maskL": maskL, "ident": ident})
    r3 = run_launch(nc, ins)
    if upto == "lion":
        st["r3"] = r3
        return st
    attn_half = []
    for c in range(NCORE):
        b, j = c // 2, c % 2
        rows = []
        for h in range(16):
            lc = 2 * b + h // 8
            hr = h % 8
            rows.append(r3[lc]["attn"][hr * 64:(hr + 1) * 64,
                                       j * TH:(j + 1) * TH])
        attn_half.append(np.ascontiguousarray(np.concatenate(rows, axis=0)))
    nc = build_aoln()
    ins = [{"attn": attn_half[c], "xn0": st["xn0"][c],
            "o_w": g("L0_attn_o_w"), "o_b": g("L0_attn_o_b"),
            "ln2_w": g("L0_attn_ln2_w"), "ln2_b": g("L0_attn_ln2_b")}
           for c in range(NCORE)]
    r4 = run_launch(nc, ins)
    nc = build_affn()
    ins = [{"h2a": r4[c]["h2a"], "xa": r4[c]["xa"],
            "f1_w": g("L0_attn_f1_w"), "f1_b": g("L0_attn_f1_b"),
            "f2_w": g("L0_attn_f2_w"), "f2_b": g("L0_attn_f2_b")}
           for c in range(NCORE)]
    r5 = run_launch(nc, ins)
    st["a0"] = [r["a0"] for r in r5]
    return st


def chain_l0ffn(inputs, st, upto="route"):
    maskL, ident = _consts()
    g = lambda k: np.ascontiguousarray(inputs[k]).astype(np.float32)
    nc = build_fqkv()
    ins = [{"xn0": st["xn0"][c], "vy": st["vy"][c],
            "wq": g("L0_ffn_wq"), "wk": g("L0_ffn_wk"), "wv": g("L0_ffn_wv"),
            "bq": g("L0_ffn_bq"), "bk": g("L0_ffn_bk"), "bv": g("L0_ffn_bv"),
            "ln1_g": g("L0_ffn_ln1_g"), "ln1_b": g("L0_ffn_ln1_b")}
           for c in range(NCORE)]
    r6 = run_launch(nc, ins)
    nc = build_fattn()
    ins = []
    for c in range(NCORE):
        b, j = c // 2, c % 2
        heads = range(8 * j, 8 * j + 8)
        qf = np.concatenate([
            np.concatenate([r6[2 * b]["qF"][h * 64:(h + 1) * 64, :],
                            r6[2 * b + 1]["qF"][h * 64:(h + 1) * 64, :]],
                           axis=1) for h in heads], axis=0)
        kf = np.concatenate([
            np.concatenate([r6[2 * b]["kF"][h * 64:(h + 1) * 64, :],
                            r6[2 * b + 1]["kF"][h * 64:(h + 1) * 64, :]],
                           axis=1) for h in heads], axis=0)
        vf = np.concatenate([
            np.concatenate([r6[2 * b]["vF"][:, h * 64:(h + 1) * 64],
                            r6[2 * b + 1]["vF"][:, h * 64:(h + 1) * 64]],
                           axis=0) for h in heads], axis=0)
        ins.append({"qF": np.ascontiguousarray(qf),
                    "kF": np.ascontiguousarray(kf),
                    "vF": np.ascontiguousarray(vf)})
    r7 = run_launch(nc, ins)
    attn_half = []
    for c in range(NCORE):
        b, j = c // 2, c % 2
        rows = []
        for h in range(16):
            lc = 2 * b + h // 8
            hr = h % 8
            rows.append(r7[lc]["attnF"][hr * 64:(hr + 1) * 64,
                                        j * TH:(j + 1) * TH])
        attn_half.append(np.ascontiguousarray(np.concatenate(rows, axis=0)))
    nc = build_aoln()
    ins = [{"attn": attn_half[c], "xn0": st["xn0"][c],
            "o_w": g("L0_ffn_wo"), "o_b": g("L0_ffn_bo"),
            "ln2_w": g("L0_ffn_ln2_g"), "ln2_b": g("L0_ffn_ln2_b")}
           for c in range(NCORE)]
    r8 = run_launch(nc, ins)
    st["xf"] = [r["xa"] for r in r8]
    st["h2f"] = [r["h2a"] for r in r8]
    if upto == "oln":
        return st
    nc = build_route()
    ins = [{"h2f": st["h2f"][c], "wr": g("L0_ffn_wr"), "ident": ident}
           for c in range(NCORE)]
    r8b = run_launch(nc, ins)
    for k in ("p1", "p2", "e1", "e2", "s1", "s2", "cnt"):
        st[k] = [r[k] for r in r8b]
    return st


def chain_moe(inputs, st, xT_shards):
    import ml_dtypes
    g = lambda k: np.ascontiguousarray(inputs[k]).astype(np.float32)
    CAP = 8192
    cnt_all = np.zeros((16, 8), np.float32)
    for c in range(NCORE):
        cnt_all[c] = st["cnt"][c][0]
        cnt_all[8 + c] = st["cnt"][c][1]
    nc = build_moegates()
    ins = [{"cnt_all": cnt_all,
            "pref": (np.arange(8) < c).astype(np.float32),
            "p1": st["p1"][c], "p2": st["p2"][c], "e1": st["e1"][c],
            "e2": st["e2"][c], "s1": st["s1"][c], "s2": st["s2"][c]}
           for c in range(NCORE)]
    r9 = run_launch(nc, ins)
    Xe = [np.zeros((D, CAP), ml_dtypes.bfloat16) for _ in range(8)]
    idxmaps = []
    for c in range(NCORE):
        h2 = st["h2f"][c]
        e1 = st["e1"][c][0].astype(np.int64)
        e2 = st["e2"][c][0].astype(np.int64)
        l1 = r9[c]["loc1"][0].astype(np.int64)
        l2 = r9[c]["loc2"][0].astype(np.int64)
        v1 = (l1 >= 0) & (l1 < CAP)
        v2 = (l2 >= 0) & (l2 < CAP)
        for e in range(8):
            m = (e1 == e) & v1
            Xe[e][:, l1[m]] = h2[:, m]
            m2 = (e2 == e) & v2
            Xe[e][:, l2[m2]] = h2[:, m2]
        idxmaps.append((e1, l1, v1, e2, l2, v2))
    nc = build_expert()
    ins = [{"Xe": Xe[c], "w1": g("L0_ffn_w1")[c], "b1": g("L0_ffn_b1")[c],
            "w2": g("L0_ffn_w2")[c], "b2": g("L0_ffn_b2")[c]}
           for c in range(8)]
    r10 = run_launch(nc, ins)
    Ye = [r["Ye"] for r in r10]
    nc = build_combine()
    ins = []
    for c in range(NCORE):
        e1, l1, v1, e2, l2, v2 = idxmaps[c]
        y1 = np.zeros((D, TH), ml_dtypes.bfloat16)
        y2 = np.zeros((D, TH), ml_dtypes.bfloat16)
        for e in range(8):
            m = (e1 == e) & v1
            y1[:, m] = Ye[e][:, l1[m]]
            m2 = (e2 == e) & v2
            y2[:, m2] = Ye[e][:, l2[m2]]
        ins.append({"xT": xT_shards[c], "a0": st["a0"][c], "xf": st["xf"][c],
                    "y1": y1, "y2": y2, "g1": r9[c]["g1"], "g2": r9[c]["g2"]})
    r11 = run_launch(nc, ins)
    st["x1"] = [r["x1"] for r in r11]
    st["xn1"] = [r["xn1"] for r in r11]
    st["vy1"] = [r["vy1"] for r in r11]
    return st


def chain_mamba(inputs, st):
    maskL, ident = _consts()
    import ml_dtypes
    g = lambda k: np.ascontiguousarray(inputs[k]).astype(np.float32)
    nc = build_mpre()
    inwT = np.ascontiguousarray(g("L1_attn_in_w").T)
    ins = [{"xn1": st["xn1"][c], "vy1": st["vy1"][c], "in_wT": inwT,
            "norm_w": g("L1_attn_norm_w"), "dt_bias": g("L1_attn_dt_bias"),
            "A_log": g("L1_attn_A_log")} for c in range(NCORE)]
    r12 = run_launch(nc, ins)
    nc = build_mconv()
    ins = []
    for c in range(NCORE):
        b, j = c // 2, c % 2
        if j == 1:
            halo = np.ascontiguousarray(r12[2 * b]["xbc"][:, TH - 3:TH])
        else:
            halo = np.zeros((2304, 3), ml_dtypes.bfloat16)
        ins.append({"xbc": r12[c]["xbc"], "halo": halo,
                    "conv_w": g("L1_attn_conv_w"),
                    "conv_b": g("L1_attn_conv_b"),
                    "dtr": r12[c]["dtr"], "ident": ident})
    r12b = run_launch(nc, ins)
    st["z"] = [r["z"] for r in r12]
    st["xsf"] = [r["xsf"] for r in r12b]
    nc = build_mamba()
    ins = []
    for c in range(NCORE):
        b, j = c // 2, c % 2
        dtx = np.concatenate([r12b[2 * b]["dtx"], r12b[2 * b + 1]["dtx"]],
                             axis=0)[:, j * 1024:(j + 1) * 1024]
        Bf = np.concatenate([r12b[2 * b]["Bf"], r12b[2 * b + 1]["Bf"]],
                            axis=1)
        Cf = np.concatenate([r12b[2 * b]["Cf"], r12b[2 * b + 1]["Cf"]],
                            axis=1)
        llm = np.concatenate([r12[2 * b]["llm"], r12[2 * b + 1]["llm"]],
                             axis=1)[16 * j:16 * j + 16, :]
        ins.append({"dtx": np.ascontiguousarray(dtx),
                    "Bf": np.ascontiguousarray(Bf),
                    "Cf": np.ascontiguousarray(Cf),
                    "llm": np.ascontiguousarray(llm),
                    "maskL": maskL, "ident": ident})
    r13 = run_launch(nc, ins)
    st["r13"] = r13
    st["yssm"] = []
    for c in range(NCORE):
        b, j = c // 2, c % 2
        rows = []
        for h in range(32):
            lc = 2 * b + h // 16
            hr = h % 16
            rows.append(r13[lc]["yssm"][hr * 64:(hr + 1) * 64,
                                        j * TH:(j + 1) * TH])
        st["yssm"].append(np.ascontiguousarray(np.concatenate(rows, axis=0)))
    return st


def chain_tail(inputs, st):
    maskL, ident = _consts()
    g = lambda k: np.ascontiguousarray(inputs[k]).astype(np.float32)
    ct = np.ascontiguousarray
    nc = build_mtail()
    Drep = np.repeat(g("L1_attn_D_param"), 64)
    ins = [{"yssm": st["yssm"][c], "xsf": st["xsf"][c], "z": st["z"][c],
            "Drep": Drep, "gnorm_w": g("L1_attn_gnorm_w"),
            "out_wT": ct(g("L1_attn_out_w").T),
            "sc1T": ct(g("L1_attn_scorer_w1").T),
            "sc2T": ct(g("L1_attn_scorer_w2").T),
            "summT": ct(g("L1_attn_summ_w").T),
            "qrT": ct(g("L1_attn_q_w").T)} for c in range(NCORE)]
    r14 = run_launch(nc, ins)
    nc = build_retr()
    gw = g("L1_attn_gate_w")
    ins = []
    for c in range(NCORE):
        b = c // 2
        scoresF = np.concatenate([r14[2 * b]["scores"],
                                  r14[2 * b + 1]["scores"]], axis=1)
        summF = np.concatenate([r14[2 * b]["summtok"],
                                r14[2 * b + 1]["summtok"]], axis=0)
        ins.append({"scoresF": ct(scoresF), "summF": ct(summF),
                    "y1024": r14[c]["y1024"], "qrf": r14[c]["qrf"],
                    "xn1": st["xn1"][c],
                    "k_wT": ct(g("L1_attn_k_w").T),
                    "v_wT": ct(g("L1_attn_v_w").T),
                    "gwAT": ct(gw[:, 0:D].T), "gwBT": ct(gw[:, D:2 * D].T),
                    "ident": ident})
    r15 = run_launch(nc, ins)
    nc = build_l1route()
    ins = [{"xn1": st["xn1"][c], "vy1": st["vy1"][c],
            "centsT": ct(g("L1_ffn_centroids").T),
            "w_routeT": ct(g("L1_ffn_w_route").T), "ident": ident}
           for c in range(NCORE)]
    r16a = run_launch(nc, ins)
    nc = build_l1ffn()
    w1 = g("L1_ffn_w1")
    w2 = g("L1_ffn_w2")
    w1cat = ct(np.concatenate([w1[e] for e in range(8)], axis=1))
    b1cat = ct(np.concatenate([g("L1_ffn_b1")[e] for e in range(8)]))
    w2cat = ct(np.concatenate([w2[e] for e in range(8)], axis=0))
    ins = [{"xn1": st["xn1"][c], "w1cat": w1cat, "b1cat": b1cat,
            "w2cat": w2cat, "b2": g("L1_ffn_b2"), "gatesF": r16a[c]["gatesF"],
            "x1": st["x1"][c], "a1": r15[c]["a1"]} for c in range(NCORE)]
    r16 = run_launch(nc, ins)
    st["out"] = [r["out"] for r in r16]
    return st


def kernel(**inputs):
    st = chain_l0attn(inputs, upto="a0")
    st = chain_l0ffn(inputs, st, upto="route")
    xT = shard_rows(inputs["L0_attn_x"])
    st = chain_moe(inputs, st, xT)
    st = chain_mamba(inputs, st)
    st = chain_tail(inputs, st)
    out = np.zeros((B, T, D), np.float32)
    for c in range(NCORE):
        b, j = c // 2, c % 2
        out[b, j * TH:(j + 1) * TH, :] = st["out"][c].T
    return out



# version 03
# speedup vs baseline: 1.4597x; 1.4597x over previous
import numpy as np
from contextlib import ExitStack

import concourse.bass as bass
import concourse.mybir as mybir
from concourse.bass_utils import run_bass_kernel_spmd

F32 = mybir.dt.float32
BF16 = mybir.dt.bfloat16
U32 = mybir.dt.uint32
AF = mybir.ActivationFunctionType
ALU = mybir.AluOpType
AX = mybir.AxisListType

NCORE = 8
B, T, D = 4, 8192, 1024
TH = T // 2
NCH = D // 128
LN_EPS = 1e-5

ENGS = ("sync", "gpsimd", "vector", "scalar", "tensor")


class Prog:

    def __init__(self, race=True):
        self.nc = bass.Bass("TRN2", target_bir_lowering=False,
                            detect_race_conditions=race)
        self.q = {e: [] for e in ENGS}
        self.stack = ExitStack()
        self._semcnt = {}
        self._sems = {}
        self._names = set()

    def dram(self, name, shape, dtype=F32, kind="Internal", **kw):
        return self.nc.dram_tensor(name, list(shape), dtype, kind=kind, **kw).ap()

    def sb(self, name, shape, dtype=F32):
        assert name not in self._names, name
        self._names.add(name)
        return self.stack.enter_context(
            self.nc.sbuf_tensor(name, list(shape), dtype))

    def ps(self, name, shape, dtype=F32):
        assert name not in self._names, name
        self._names.add(name)
        return self.stack.enter_context(
            self.nc.psum_tensor(name, list(shape), dtype))

    def sem(self, name):
        if name not in self._sems:
            self._sems[name] = self.stack.enter_context(self.nc.semaphore(name))
            self._semcnt[name] = 0
        return self._sems[name]

    def op(self, eng, fn, *, inc=None, incn=1, wait=None):
        waits = []
        if wait is not None:
            waits = [wait] if isinstance(wait, tuple) else [w for w in wait if w]
        for wn, wv in waits:
            self.sem(wn)
        semh = None
        if inc is not None:
            semh = self.sem(inc)
            self._semcnt[inc] += incn

        def run(e, fn=fn, waits=waits, semh=semh, incn=incn):
            for (wn, wv) in waits:
                e.wait_ge(self._sems[wn], wv)
            ins = fn(e)
            if semh is not None:
                ins.then_inc(semh, incn)
            return ins
        self.q[eng].append(run)

    def count(self, name):
        self.sem(name)
        return self._semcnt[name]

    def after(self, name):
        return (name, self.count(name))

    def wait(self, eng, name, value=None):
        self.sem(name)
        v = self._semcnt[name] if value is None else value
        self.q[eng].append(lambda e, n=name, v=v: e.wait_ge(self._sems[n], v))

    def sop(self, eng, fn, dma=False):
        w = self.after("seq")
        self.op(eng, fn, inc="seq", incn=16 if dma else 1,
                wait=[w] if w[1] else None)

    def build(self):
        with self.nc.Block() as block:
            for e in ENGS:
                ops = self.q[e]
                if not ops:
                    continue
                getattr(block, e)(lambda eng, ops=ops: [f(eng) for f in ops])
        self.stack.close()
        return self.nc


def cdiv(a, b):
    return -(-a // b)


class PsumPool:

    def __init__(self, p, name, n, shape, dtype=F32):
        self.p = p
        self.tiles = [p.ps(f"{name}{i}", shape, dtype) for i in range(n)]
        self.n = n
        self.k = 0
        self.evsem = name + "_ev"
        p.sem(self.evsem)

    def next(self):
        t = self.tiles[self.k % self.n]
        war = (self.evsem, self.k - self.n + 1) if self.k >= self.n else None
        self.k += 1
        return t, war



def ld_weight(p, dst_sb, w_ap):
    din, dout = w_ap.shape
    full = (din // 128) * 128
    if full:
        src = w_ap[0:full, :].rearrange("(c p) n -> p c n", p=128)
        nch = din // 128
        p.op("gpsimd", lambda e: e.dma_start(
            out=dst_sb[:, 0:nch * dout].rearrange("p (c n) -> p c n", c=nch),
            in_=src), inc="ld", incn=16)
    rem = din - full
    if rem:
        off = (din // 128) * dout
        p.op("gpsimd", lambda e: e.dma_start(
            out=dst_sb[0:rem, off:off + dout], in_=w_ap[full:din, :]),
            inc="ld", incn=16)


def ld_col(p, dst_sb, vec_ap, n=None, poff=0):
    n = vec_ap.shape[0] if n is None else n
    p.op("gpsimd", lambda e: e.dma_start(out=dst_sb[poff:poff + n, 0:1],
                                         in_=vec_ap[0:n][:, None]),
         inc="ld", incn=16)


def ld_row(p, dst_sb, vec_ap, n=None, roff=0):
    n = vec_ap.shape[0] if n is None else n
    p.op("gpsimd", lambda e: e.dma_start(out=dst_sb[0:1, roff:roff + n],
                                         in_=vec_ap[0:n][None, :]),
         inc="ld", incn=16)


def common_tiles(p):
    p._ones_f = p.sb("c_ones_f", [128, 8], F32)
    p._ones_b = p.sb("c_ones_b", [1, 128], BF16)
    p._onesrow_f = p.sb("c_onesrow_f", [1, 512], F32)
    p._eps_col = p.sb("c_eps_col", [1, 1], F32)
    p._identb = p.sb("c_identb", [128, 128], BF16)
    p._onescol_b = p.sb("c_onescol_b", [128, 1], BF16)
    p._rtmp = p.sb("c_rtmp", [1, TH], F32)
    p._iotarow = p.sb("c_iotarow", [1, 128], mybir.dt.int32)
    p._lntmp = p.sb("c_lntmp", [128, 512], F32)
    p.op("vector", lambda e: e.memset(p._ones_f[:, :], 1.0), inc="const")
    p.op("vector", lambda e: e.memset(p._ones_b[0:1, :], 1.0), inc="const")
    p.op("vector", lambda e: e.memset(p._onesrow_f[:, :], 1.0), inc="const")
    p.op("vector", lambda e: e.memset(p._eps_col[:, :], LN_EPS), inc="const")
    p.op("vector", lambda e: e.memset(p._onescol_b[:, :], 1.0), inc="const")
    p.op("gpsimd", lambda e: e.iota(p._iotarow[0:1, 0:128],
                                    pattern=[[1, 128]], base=0,
                                    channel_multiplier=0), inc="const")
    p.wait("tensor", "const")
    p.wait("scalar", "const")


def stats_rows(p, name, x_f, ntok, nch, sq_tmp, sum_row, sq_row, pp, wait=()):
    mmsem = "mm_" + name
    sqsem = "sq_" + name
    nblk = 512
    for tb in range(cdiv(ntok, nblk)):
        n = min(nblk, ntok - tb * nblk)
        for ci in range(nch):
            p.op("scalar", lambda e, ci=ci, tb=tb, n=n: e.activation(
                sq_tmp[:, ci * nblk:ci * nblk + n],
                x_f[:, ci * ntok + tb * nblk:ci * ntok + tb * nblk + n],
                AF.Square),
                inc=sqsem, wait=list(wait) + [p.after("sq_done_" + name)]
                if ci == 0 and tb > 0 else (wait if ci == 0 else None))
        sqv = p.count(sqsem)
        for which in range(2):
            pt, war = pp.next()
            for ci in range(nch):
                w = []
                if ci == 0:
                    w = list(wait) if which == 0 else [(sqsem, sqv)]
                    if war:
                        w.append(war)

                def mmfn(t, pt=pt, ci=ci, tb=tb, n=n, which=which):
                    src = (x_f[:, ci * ntok + tb * nblk:
                               ci * ntok + tb * nblk + n] if which == 0
                           else sq_tmp[:, ci * nblk:ci * nblk + n])
                    return t.matmul(pt[0:1, 0:n], p._ones_f[:, 0:1], src,
                                    start=(ci == 0), stop=(ci == nch - 1))
                p.op("tensor", mmfn, wait=w or None,
                     inc=mmsem if ci == nch - 1 else None)
            dst = sum_row if which == 0 else sq_row
            p.op("vector",
                 lambda e, pt=pt, dst=dst, tb=tb, n=n:
                 e.tensor_copy(dst[0:1, tb * nblk:tb * nblk + n], pt[0:1, 0:n]),
                 wait=(mmsem, p.count(mmsem)),
                 inc=pp.evsem)
        p.op("vector", lambda e: e.engine_nop(), inc="sq_done_" + name)


def rows_meanvar_rstd(p, name, sum_row, sq_row, ntok, mean_row, var_row,
                      rstd_row, negms_row, wait):
    sname = name + "_r"
    p.op("vector", lambda e: e.tensor_scalar_mul(
        mean_row[0:1, 0:ntok], sum_row[0:1, 0:ntok], 1.0 / D), wait=wait)
    p.op("vector", lambda e: e.tensor_scalar_mul(
        var_row[0:1, 0:ntok], sq_row[0:1, 0:ntok], 1.0 / D))
    p.op("vector", lambda e: e.tensor_mul(
        p._rtmp[0:1, 0:ntok], mean_row[0:1, 0:ntok], mean_row[0:1, 0:ntok]))
    p.op("vector", lambda e: e.tensor_sub(
        var_row[0:1, 0:ntok], var_row[0:1, 0:ntok], p._rtmp[0:1, 0:ntok]),
        inc=sname)
    p.op("scalar", lambda e: e.activation(
        p._rtmp[0:1, 0:ntok], var_row[0:1, 0:ntok], AF.Sqrt,
        bias=p._eps_col[0:1, 0:1]), wait=(sname, p.count(sname)), inc=sname)
    p.op("vector", lambda e: e.reciprocal(
        rstd_row[0:1, 0:ntok], p._rtmp[0:1, 0:ntok]),
        wait=(sname, p.count(sname)))
    p.op("vector", lambda e: e.tensor_mul(
        negms_row[0:1, 0:ntok], mean_row[0:1, 0:ntok], rstd_row[0:1, 0:ntok]))
    p.op("vector", lambda e: e.tensor_scalar_mul(
        negms_row[0:1, 0:ntok], negms_row[0:1, 0:ntok], -1.0), inc=sname)
    return (sname, p.count(sname))


def ln_apply(p, name, x_f, ntok, nch, scale_row, negms_row, g_row, b_row,
             out_sb, out_ntok, pp, wait=(), xoff=0, out_off=0, x_is_psumfree=True):
    mmsem = "mm_" + name
    nblk = 512
    for tb in range(cdiv(ntok, nblk)):
        n = min(nblk, ntok - tb * nblk)
        for ci in range(nch):
            pt, war = pp.next()
            w = list(wait)
            if war:
                w.append(war)
            grow = (g_row[0:1, ci * 128:(ci + 1) * 128] if g_row is not None
                    else p._onesrow_f[0:1, 0:128])
            p.op("tensor",
                 lambda t, pt=pt, grow=grow, tb=tb, n=n:
                 t.matmul(pt[:, 0:n], grow,
                          scale_row[0:1, tb * nblk:tb * nblk + n],
                          start=True, stop=True),
                 wait=w or None)
            if negms_row is not None:
                p.op("tensor",
                     lambda t, pt=pt, grow=grow, tb=tb, n=n:
                     t.matmul(pt[:, 512:512 + n], grow,
                              negms_row[0:1, tb * nblk:tb * nblk + n],
                              start=True, stop=False))
            p.op("tensor",
                 lambda t, pt=pt, ci=ci, tb=tb, n=n:
                 t.matmul(pt[:, 512:512 + n],
                          b_row[0:1, ci * 128:(ci + 1) * 128],
                          p._onesrow_f[0:1, 0:n],
                          start=(negms_row is None), stop=True),
                 inc=mmsem)
            mmv = p.count(mmsem)
            p.op("vector",
                 lambda e, pt=pt, ci=ci, tb=tb, n=n:
                 e.tensor_mul(p._lntmp[:, 0:n],
                              x_f[:, ci * ntok + xoff + tb * nblk:
                                  ci * ntok + xoff + tb * nblk + n],
                              pt[:, 0:n]),
                 wait=(mmsem, mmv))
            p.op("vector",
                 lambda e, pt=pt, ci=ci, tb=tb, n=n:
                 e.tensor_add(out_sb[:, ci * out_ntok + out_off + tb * nblk:
                                     ci * out_ntok + out_off + tb * nblk + n],
                              p._lntmp[:, 0:n], pt[:, 512:512 + n]),
                 inc=pp.evsem)


def proj_feat(p, name, wsb, d_in, d_out, xsb, ntok, pp, evac,
              xdt_ntok=None, wait=(), nblk=512):
    XN = ntok if xdt_ntok is None else xdt_ntok
    nchin = cdiv(d_in, 128)
    mmsem = "mm_" + name
    for co in range(cdiv(d_out, 128)):
        cod = min(128, d_out - co * 128)
        for tb in range(cdiv(ntok, nblk)):
            n = min(nblk, ntok - tb * nblk)
            pt, war = pp.next()
            for ci in range(nchin):
                cid = min(128, d_in - ci * 128)
                w = list(wait) if ci == 0 else []
                if ci == 0 and war:
                    w.append(war)
                p.op("tensor",
                     lambda t, pt=pt, ci=ci, co=co, cod=cod, cid=cid, tb=tb, n=n:
                     t.matmul(pt[0:cod, 0:n],
                              wsb[0:cid, ci * d_out + co * 128:
                                  ci * d_out + co * 128 + cod],
                              xsb[0:cid, ci * XN + tb * nblk:
                                  ci * XN + tb * nblk + n],
                              start=(ci == 0), stop=(ci == nchin - 1)),
                     wait=w or None,
                     inc=mmsem if ci == nchin - 1 else None)
            evac(co, tb, cod, n, pt, p.count(mmsem), pp.evsem)


def proj_tok(p, name, xsb, d_in, wsb, d_out, ntok, pp, evac, bias_row,
             xdt_ntok=None, wait=(), nblk=512):
    XN = ntok if xdt_ntok is None else xdt_ntok
    nchin = cdiv(d_in, 128)
    mmsem = "mm_" + name
    for tt in range(cdiv(ntok, 128)):
        tn = min(128, ntok - tt * 128)
        for ob in range(cdiv(d_out, nblk)):
            n = min(nblk, d_out - ob * nblk)
            pt, war = pp.next()
            for ci in range(nchin):
                cid = min(128, d_in - ci * 128)
                w = list(wait) if ci == 0 else []
                if ci == 0 and war:
                    w.append(war)
                p.op("tensor",
                     lambda t, pt=pt, ci=ci, cid=cid, tt=tt, tn=tn, ob=ob, n=n:
                     t.matmul(pt[0:tn, 0:n],
                              xsb[0:cid, ci * XN + tt * 128:
                                  ci * XN + tt * 128 + tn],
                              wsb[0:cid, ci * d_out + ob * nblk:
                                  ci * d_out + ob * nblk + n],
                              start=(ci == 0), stop=False),
                     wait=w or None)
            p.op("tensor",
                 lambda t, pt=pt, tt=tt, tn=tn, ob=ob, n=n:
                 t.matmul(pt[0:tn, 0:n], p._ones_b[0:1, 0:tn],
                          bias_row[0:1, ob * nblk:ob * nblk + n],
                          start=False, stop=True),
                 inc=mmsem)
            evac(tt, ob, tn, n, pt, p.count(mmsem), pp.evsem)



def stream_blocks(p, name, src_featmajor, ntok, nch, bufs, consume,
                  width=512, dst_dtype=None, first_wait=None):
    csem = name + "_c"
    p.sem(csem)
    nb = len(bufs)
    for tb in range(cdiv(ntok, width)):
        n = min(width, ntok - tb * width)
        buf = bufs[tb % nb]
        ldsem = f"{name}_ld{tb % nb}"
        w = [(csem, tb - nb + 1)] if tb >= nb else (list(first_wait) if first_wait else None)
        src = src_featmajor.rearrange("(c p) n -> p c n", p=128)[
            :, :, tb * width:tb * width + n]
        p.op("sync", lambda e, buf=buf, src=src, n=n, nch=nch:
             e.dma_start(out=buf[:, 0:nch * n].rearrange(
                 "p (c n) -> p c n", c=nch), in_=src),
             wait=w, inc=ldsem, incn=16)
        consume(tb, n, buf, (ldsem, p.count(ldsem)))


def build_pre():
    p = Prog()
    xT = p.dram("xT", [D, TH], F32, kind="ExternalInput")
    xn0 = p.dram("xn0", [D, TH], F32, kind="ExternalOutput")
    vy = p.dram("vy", [1, TH], F32, kind="ExternalOutput")

    common_tiles(p)
    xbufs = [p.sb(f"xb{i}", [128, NCH * 512], F32) for i in range(2)]
    sq_tmp = p.sb("sq_tmp", [128, NCH * 512], F32)
    sum_row = p.sb("sum_row", [1, TH], F32)
    sq_row = p.sb("sq_row", [1, TH], F32)
    rstd_row = p.sb("rstd_row", [1, TH], F32)
    negms_row = p.sb("negms_row", [1, TH], F32)
    out_f = p.sb("out_f", [128, 512], F32)

    pp1 = PsumPool(p, "pp1", 2, [1, 512])
    pp2 = PsumPool(p, "pp2", 3, [128, 1024])

    def statsA(tb, n, buf, ldw):
        sqsem = "sqA"
        for ci in range(NCH):
            w = None
            if ci == 0:
                w = [ldw]
                if tb > 0:
                    w.append(("mmA", p.count("mmA")))
            p.op("scalar", lambda e, ci=ci, n=n, buf=buf: e.activation(
                sq_tmp[:, ci * 512:ci * 512 + n],
                buf[:, ci * n:ci * n + n], AF.Square),
                wait=w, inc=sqsem)
        sqv = p.count(sqsem)
        for which in range(2):
            pt, war = pp1.next()
            for ci in range(NCH):
                w = []
                if ci == 0:
                    w = [ldw] if which == 0 else [(sqsem, sqv)]
                    if war:
                        w.append(war)

                def mmfn(t, pt=pt, ci=ci, n=n, which=which, buf=buf):
                    src = (buf[:, ci * n:ci * n + n] if which == 0
                           else sq_tmp[:, ci * 512:ci * 512 + n])
                    return t.matmul(pt[0:1, 0:n], p._ones_f[:, 0:1], src,
                                    start=(ci == 0), stop=(ci == NCH - 1))
                p.op("tensor", mmfn, wait=w or None,
                     inc="mmA" if ci == NCH - 1 else None)
            dst = sum_row if which == 0 else sq_row
            p.op("vector",
                 lambda e, pt=pt, dst=dst, tb=tb, n=n:
                 e.tensor_copy(dst[0:1, tb * 512:tb * 512 + n], pt[0:1, 0:n]),
                 wait=("mmA", p.count("mmA")),
                 inc=pp1.evsem if True else None)
            if which == 1:
                p.op("vector", lambda e: e.engine_nop(), inc="xA_c")

    stream_blocks(p, "xA", xT, TH, NCH, xbufs, statsA)

    rw0 = [p.after("mmA"), (pp1.evsem, pp1.k)]
    p.op("vector", lambda e: e.tensor_scalar_mul(
        sum_row[0:1, 0:TH], sum_row[0:1, 0:TH], 1.0 / D), wait=rw0)
    p.op("vector", lambda e: e.tensor_scalar_mul(
        sq_row[0:1, 0:TH], sq_row[0:1, 0:TH], 1.0 / D))
    p.op("vector", lambda e: e.tensor_mul(
        p._rtmp[0:1, 0:TH], sum_row[0:1, 0:TH], sum_row[0:1, 0:TH]))
    p.op("vector", lambda e: e.tensor_sub(
        sq_row[0:1, 0:TH], sq_row[0:1, 0:TH], p._rtmp[0:1, 0:TH]), inc="mv")
    p.op("scalar", lambda e: e.activation(
        p._rtmp[0:1, 0:TH], sq_row[0:1, 0:TH], AF.Sqrt,
        bias=p._eps_col[0:1, 0:1]), wait=p.after("mv"), inc="mv")
    p.op("vector", lambda e: e.reciprocal(
        rstd_row[0:1, 0:TH], p._rtmp[0:1, 0:TH]), wait=p.after("mv"))
    p.op("vector", lambda e: e.tensor_mul(
        negms_row[0:1, 0:TH], sq_row[0:1, 0:TH], rstd_row[0:1, 0:TH]))
    p.op("vector", lambda e: e.tensor_mul(
        negms_row[0:1, 0:TH], negms_row[0:1, 0:TH], rstd_row[0:1, 0:TH]),
        inc="vyr")
    p.op("sync", lambda e: e.dma_start(out=vy, in_=negms_row[0:1, 0:TH]),
         wait=p.after("vyr"), inc="st", incn=16)
    p.op("vector", lambda e: e.tensor_mul(
        negms_row[0:1, 0:TH], sum_row[0:1, 0:TH], rstd_row[0:1, 0:TH]),
        wait=p.after("st"))
    p.op("vector", lambda e: e.tensor_scalar_mul(
        negms_row[0:1, 0:TH], negms_row[0:1, 0:TH], -1.0), inc="mv")
    rw = p.after("mv")

    def applyB(tb, n, buf, ldw):
        for ci in range(NCH):
            pt, war = pp2.next()
            w = [rw, ldw]
            if war:
                w.append(war)
            p.op("tensor", lambda t, pt=pt, tb=tb, n=n:
                 t.matmul(pt[:, 0:n], p._onesrow_f[0:1, 0:128],
                          rstd_row[0:1, tb * 512:tb * 512 + n],
                          start=True, stop=True), wait=w)
            p.op("tensor", lambda t, pt=pt, tb=tb, n=n:
                 t.matmul(pt[:, 512:512 + n], p._onesrow_f[0:1, 0:128],
                          negms_row[0:1, tb * 512:tb * 512 + n],
                          start=True, stop=True), inc="mmB")
            mmv = p.count("mmB")
            p.op("vector", lambda e, pt=pt, ci=ci, n=n, buf=buf:
                 e.tensor_mul(out_f[:, 0:n], buf[:, ci * n:ci * n + n],
                              pt[:, 0:n]),
                 wait=[("mmB", mmv), ldw, p.after("st2")])
            p.op("vector", lambda e, pt=pt, n=n:
                 e.tensor_add(out_f[:, 0:n], out_f[:, 0:n], pt[:, 512:512 + n]),
                 inc=pp2.evsem)
            if ci == NCH - 1:
                p.op("vector", lambda e: e.engine_nop(), inc="xB_c")
            p.op("sync", lambda e, ci=ci, tb=tb, n=n:
                 e.dma_start(out=xn0[ci * 128:(ci + 1) * 128,
                                     tb * 512:tb * 512 + n],
                             in_=out_f[:, 0:n]),
                 wait=p.after(pp2.evsem), inc="st2", incn=16)

    stream_blocks(p, "xB", xT, TH, NCH, xbufs, applyB,
                  first_wait=[p.after("xA_c")])
    p.wait("sync", "st2")
    p.wait("sync", "st")
    return p.build()



def drv_scale_row(p, vy_sb, out_row, ntok, wait, eps=LN_EPS):
    p.op("scalar", lambda e: e.activation(
        p._rtmp[0:1, 0:ntok], vy_sb[0:1, 0:ntok], AF.Sqrt,
        bias=p._eps_col[0:1, 0:1]), wait=wait, inc="drv")
    p.op("vector", lambda e: e.reciprocal(
        out_row[0:1, 0:ntok], p._rtmp[0:1, 0:ntok]),
        wait=p.after("drv"), inc="drv")
    return p.after("drv")


def ln_stream(p, name, srcT, ntok, nch, bufs, scale_row, negms_row,
              g_row, b_row, out_sb, out_ntok, pp, rw, first_wait=None):
    mmsem = "mm_" + name

    def consume(tb, n, buf, ldw):
        for ci in range(nch):
            pt, war = pp.next()
            w = [rw, ldw]
            if war:
                w.append(war)
            grow = (g_row[0:1, ci * 128:(ci + 1) * 128] if g_row is not None
                    else p._onesrow_f[0:1, 0:128])
            p.op("tensor",
                 lambda t, pt=pt, grow=grow, tb=tb, n=n:
                 t.matmul(pt[:, 0:n], grow,
                          scale_row[0:1, tb * 512:tb * 512 + n],
                          start=True, stop=True), wait=w)
            if negms_row is not None:
                p.op("tensor",
                     lambda t, pt=pt, grow=grow, tb=tb, n=n:
                     t.matmul(pt[:, 512:512 + n], grow,
                              negms_row[0:1, tb * 512:tb * 512 + n],
                              start=True, stop=False))
            p.op("tensor",
                 lambda t, pt=pt, ci=ci, tb=tb, n=n:
                 t.matmul(pt[:, 512:512 + n],
                          b_row[0:1, ci * 128:(ci + 1) * 128],
                          p._onesrow_f[0:1, 0:n],
                          start=(negms_row is None), stop=True),
                 inc=mmsem)
            mmv = p.count(mmsem)
            p.op("vector",
                 lambda e, pt=pt, ci=ci, n=n, buf=buf:
                 e.tensor_mul(p._lntmp[:, 0:n], buf[:, ci * n:ci * n + n],
                              pt[:, 0:n]),
                 wait=[(mmsem, mmv), ldw])
            p.op("vector",
                 lambda e, pt=pt, ci=ci, tb=tb, n=n:
                 e.tensor_add(out_sb[:, ci * out_ntok + tb * 512:
                                     ci * out_ntok + tb * 512 + n],
                              p._lntmp[:, 0:n], pt[:, 512:512 + n]),
                 inc=pp.evsem)
            if ci == nch - 1:
                p.op("vector", lambda e: e.engine_nop(), inc=name + "_c")

    stream_blocks(p, name, srcT, ntok, nch, bufs, consume,
                  first_wait=first_wait)
    return p.after(name + "_c")


class OutTiles:

    def __init__(self, p, name, n, shape, dtype):
        self.p = p
        self.tiles = [p.sb(f"{name}{i}", shape, dtype) for i in range(n)]
        self.n = n
        self.k = 0
        self.stsem = name + "_st"
        p.sem(self.stsem)

    def next(self):
        t = self.tiles[self.k % self.n]
        war = (self.stsem, 16 * (self.k - self.n + 1)) if self.k >= self.n \
            else None
        self.k += 1
        return t, war


def build_aqkv():
    p = Prog()
    xn0 = p.dram("xn0", [D, TH], F32, kind="ExternalInput")
    vyd = p.dram("vy", [1, TH], F32, kind="ExternalInput")
    wq = p.dram("q_w", [D, D], F32, kind="ExternalInput")
    wk = p.dram("k_w", [D, D], F32, kind="ExternalInput")
    wv = p.dram("v_w", [D, D], F32, kind="ExternalInput")
    wdec = p.dram("dec_w", [D, 16], F32, kind="ExternalInput")
    qb = p.dram("q_b", [D], F32, kind="ExternalInput")
    kb = p.dram("k_b", [D], F32, kind="ExternalInput")
    vb = p.dram("v_b", [D], F32, kind="ExternalInput")
    decb = p.dram("dec_b", [16], F32, kind="ExternalInput")
    ln1w = p.dram("ln1_w", [D], F32, kind="ExternalInput")
    ln1b = p.dram("ln1_b", [D], F32, kind="ExternalInput")
    phiq = p.dram("phiq", [D, TH], BF16, kind="ExternalOutput")
    phik = p.dram("phik", [D, TH], BF16, kind="ExternalOutput")
    vtok = p.dram("vtok", [TH, D], BF16, kind="ExternalOutput")
    loglam = p.dram("loglam", [16, TH], F32, kind="ExternalOutput")

    common_tiles(p)
    wq_sb = p.sb("wq_sb", [128, NCH * D], BF16)
    wk_sb = p.sb("wk_sb", [128, NCH * D], BF16)
    wv_sb = p.sb("wv_sb", [128, NCH * D], BF16)
    wd_sb = p.sb("wd_sb", [128, NCH * 16], BF16)
    qb_c = p.sb("qb_c", [128, NCH], F32)
    kb_c = p.sb("kb_c", [128, NCH], F32)
    db_c = p.sb("db_c", [16, 1], F32)
    vb_r = p.sb("vb_r", [1, D], BF16)
    ln1w_r = p.sb("ln1w_r", [1, D], F32)
    ln1b_r = p.sb("ln1b_r", [1, D], F32)
    vy_r = p.sb("vy_r", [1, TH], F32)
    s_row = vy_r
    ha = p.sb("ha", [128, NCH * TH], BF16)
    xbufs = [p.sb(f"xb{i}", [128, NCH * 512], F32) for i in range(2)]
    t1 = p.sb("t1", [128, 512], F32)
    t2 = p.sb("t2", [128, 512], F32)
    t3 = p.sb("t3", [128, 512], F32)

    ld_weight(p, wq_sb, wq)
    ld_weight(p, wk_sb, wk)
    ld_weight(p, wv_sb, wv)
    ld_weight(p, wd_sb, wdec)
    for ci in range(NCH):
        ld_col(p, qb_c[:, ci:ci + 1].tensor[:, ci:ci + 1] if False else qb_c,
               qb[ci * 128:(ci + 1) * 128], n=128) if False else None
    p.op("gpsimd", lambda e: e.dma_start(
        out=qb_c[:, :], in_=qb.rearrange("(c p) -> p c", p=128),
        allow_slow_non_contiguous=True), inc="ld", incn=16)
    p.op("gpsimd", lambda e: e.dma_start(
        out=kb_c[:, :], in_=kb.rearrange("(c p) -> p c", p=128),
        allow_slow_non_contiguous=True), inc="ld", incn=16)
    ld_col(p, db_c, decb)
    ld_row(p, vb_r, vb)
    ld_row(p, ln1w_r, ln1w)
    ld_row(p, ln1b_r, ln1b)
    ld_row(p, vy_r, vyd[0, :])
    ldall = p.after("ld")

    pp = PsumPool(p, "pp", 3, [128, 1024])
    ppv = PsumPool(p, "ppv", 2, [128, 512])
    ot = OutTiles(p, "ot", 3, [128, 512], BF16)
    otl = OutTiles(p, "otl", 2, [16, 512], F32)

    rw = drv_scale_row(p, vy_r, s_row, TH, [ldall])
    hadone = ln_stream(p, "ha", xn0, TH, NCH, xbufs, s_row, None,
                       ln1w_r, ln1b_r, ha, TH, pp, rw, first_wait=[ldall])

    def phi_evac(which, qbc):
        def evac(co, tb, cod, n, pt, mmv, evsem):
            otile, war = ot.next()
            w = [("mm_" + which, mmv)]
            if war:
                w.append(war)
            p.op("vector", lambda e, pt=pt, co=co, n=n:
                 e.tensor_scalar(t1[:, 0:n], pt[:, 0:n],
                                 qbc[:, co:co + 1], 0.0, ALU.add, ALU.min),
                 wait=w)
            p.op("vector", lambda e, pt=pt, co=co, n=n:
                 e.tensor_scalar(t2[:, 0:n], pt[:, 0:n],
                                 qbc[:, co:co + 1], 0.0, ALU.add, ALU.max),
                 inc=evsem)
            p.op("scalar", lambda e, n=n: e.activation(
                t3[:, 0:n], t1[:, 0:n], AF.Exp),
                wait=p.after(evsem), inc="phiexp")
            p.op("vector", lambda e, otile=otile, n=n:
                 e.tensor_add(otile[:, 0:n], t3[:, 0:n], t2[:, 0:n]),
                 wait=("phiexp", p.count("phiexp")), inc="phiadd")
            dst = phiq if which == "q" else phik
            p.op("sync", lambda e, otile=otile, co=co, tb=tb, cod=cod, n=n,
                 dst=dst:
                 e.dma_start(out=dst[co * 128:co * 128 + cod,
                                     tb * 512:tb * 512 + n],
                             in_=otile[0:cod, 0:n]),
                 wait=p.after("phiadd"), inc=ot.stsem, incn=16)
        return evac

    proj_feat(p, "q", wq_sb, D, D, ha, TH, pp, phi_evac("q", qb_c),
              wait=[hadone])
    proj_feat(p, "k", wk_sb, D, D, ha, TH, pp, phi_evac("k", kb_c),
              wait=[hadone])

    def dec_evac(co, tb, cod, n, pt, mmv, evsem):
        otile, war = otl.next()
        w = [("mm_dec", mmv)]
        if war:
            w.append(war)
        p.op("scalar", lambda e, pt=pt, n=n, cod=cod: e.activation(
            t1[0:cod, 0:n], pt[0:cod, 0:n], AF.Sigmoid,
            bias=db_c[0:16, 0:1]), wait=w, inc="decs")
        p.op("vector", lambda e, n=n, cod=cod:
             e.tensor_scalar_max(t2[0:cod, 0:n], t1[0:cod, 0:n], 1e-8),
             wait=p.after("decs"), inc=evsem)
        p.op("scalar", lambda e, otile=otile, n=n, cod=cod: e.activation(
            otile[0:cod, 0:n], t2[0:cod, 0:n], AF.Ln),
            wait=p.after(evsem), inc="decl")
        p.op("sync", lambda e, otile=otile, tb=tb, cod=cod, n=n:
             e.dma_start(out=loglam[0:cod, tb * 512:tb * 512 + n],
                         in_=otile[0:cod, 0:n]),
             wait=p.after("decl"), inc=otl.stsem, incn=16)

    proj_feat(p, "dec", wd_sb, D, 16, ha, TH, pp, dec_evac, wait=[hadone])

    def v_evac(tt, ob, tn, n, pt, mmv, evsem):
        otile, war = ot.next()
        w = [("mm_v", mmv)]
        if war:
            w.append(war)
        p.op("scalar", lambda e, pt=pt, otile=otile, tn=tn, n=n:
             e.activation(otile[0:tn, 0:n], pt[0:tn, 0:n], AF.Copy),
             wait=w, inc=evsem)
        p.op("sync", lambda e, otile=otile, tt=tt, ob=ob, tn=tn, n=n:
             e.dma_start(out=vtok[tt * 128:tt * 128 + tn,
                                  ob * 512:ob * 512 + n],
                         in_=otile[0:tn, 0:n]),
             wait=p.after(evsem), inc=ot.stsem, incn=16)

    proj_tok(p, "v", ha, D, wv_sb, D, TH, ppv, v_evac, vb_r, wait=[hadone])

    p.wait("sync", ot.stsem)
    p.wait("sync", otl.stsem)
    return p.build()



def build_lion(NP=8, TT=T, CH=128, dbg=False):
    p = Prog()
    nchk = TT // CH
    phiq = p.dram("phiq", [NP * 64, TT], BF16, kind="ExternalInput")
    phik = p.dram("phik", [NP * 64, TT], BF16, kind="ExternalInput")
    vtok = p.dram("vtok", [NP * TT, 64], BF16, kind="ExternalInput")
    loglam = p.dram("loglam", [NP, TT], F32, kind="ExternalInput")
    maskL = p.dram("maskL", [128, 128], F32, kind="ExternalInput")
    ident = p.dram("ident", [128, 128], F32, kind="ExternalInput")
    attn = p.dram("attn", [NP * 64, TT], BF16, kind="ExternalOutput")
    if dbg:
        dres = p.dram("dres", [65, TT], F32, kind="ExternalOutput")
        dE = p.dram("dE", [4 * 128, 128], F32, kind="ExternalOutput")
        dst8 = p.dram("dst8", [64, 65 * 8], BF16, kind="ExternalOutput")
        dG = p.dram("dG", [128, 128], F32, kind="ExternalOutput")

    common_tiles(p)
    mask_sb = p.sb("mask_sb", [128, 128], F32)
    pq = p.sb("pq", [64, TT], BF16)
    pk = p.sb("pk", [64, TT], BF16)
    vt = p.sb("vt", [128, (TT // 128) * 64], BF16)
    ll = p.sb("ll", [1, TT], F32)
    res = p.sb("res", [65, TT], F32)
    rstore = p.sb("rstore", [64, 65 * nchk], BF16)
    pkt = p.sb("pkt", [128, 64 * nchk], BF16)
    llc = p.sb("llc", [1, CH], F32)
    lt_col = p.sb("lt_col", [128, 1], F32)
    nlt_col = p.sb("nlt_col", [128, 1], F32)
    lc128 = p.sb("lc128", [128, 1], F32)
    acol = p.sb("acol", [128, 1], F32)
    wcol = p.sb("wcol", [128, 1], F32)
    ecall = p.sb("ecall", [128, 1], F32)
    bcsb = p.sb("bcsb", [128, 128], F32)
    e1in = p.sb("e1in", [128, 128], F32)
    e2in = p.sb("e2in", [128, 128], F32)
    E1 = p.sb("E1", [128, 128], BF16)
    E2 = p.sb("E2", [128, 128], BF16)
    Etot = p.sb("Etot", [128, 128], BF16)
    Asb = p.sb("Asb", [128, 128], BF16)
    expb = p.sb("expb", [64, 128], BF16)
    wrow = p.sb("wrow", [64, 128], BF16)
    qs = p.sb("qs", [64, 128], BF16)
    qsr = p.sb("qsr", [64, 128], BF16)
    Vsc = p.sb("Vsc", [128, 65], BF16)
    S_f = p.sb("S_f", [64, 65], F32)
    R_f = p.sb("R_f", [64, 65], F32)
    Sb = p.sb("Sb", [64, 65], BF16)
    oattn = p.sb("oattn", [64, 512], BF16)

    psb = p.ps("psb", [128, 128], F32)
    ps2c = p.ps("ps2c", [128, 2], F32)
    psG = p.ps("psG", [128, 128], F32)
    psO = p.ps("psO", [128, 128], F32)
    psO2 = p.ps("psO2", [128, 128], F32)
    psPT = p.ps("psPT", [128, 64], BF16)
    psS = p.ps("psS", [64, 65], F32)
    psN = p.ps("psN", [64, 512], F32)

    identf = p.sb("identf", [128, 128], F32)
    p.sop("sync", lambda e: e.dma_start(out=mask_sb[:, :], in_=maskL), dma=True)
    p.sop("sync", lambda e: e.dma_start(out=identf[:, :], in_=ident), dma=True)
    p.sop("vector", lambda e: e.tensor_copy(p._identb[:, :], identf[:, :]))

    def chunk_common(pr, ch, need_bcast):
        p.sop("vector", lambda e, ch=ch: e.tensor_tensor_scan(
            llc[0:1, :], ll[0:1, ch * CH:(ch + 1) * CH],
            p._onesrow_f[0:1, 0:CH], 0.0, ALU.add, ALU.mult))
        p.sop("tensor", lambda e: e.transpose(
            ps2c[:, 0:1], llc[0:1, :], p._onesrow_f[0:1, 0:1]))
        p.sop("vector", lambda e: e.tensor_copy(lt_col[:, :], ps2c[:, 0:1]))
        p.sop("vector", lambda e: e.tensor_scalar_mul(
            nlt_col[:, :], lt_col[:, :], -1.0))
        p.sop("tensor", lambda e: e.matmul(
            ps2c[:, 1:2], p._onesrow_f[0:1, 0:128], llc[0:1, 127:128],
            start=True, stop=True, skip_group_check=True))
        p.sop("vector", lambda e: e.tensor_copy(lc128[:, :], ps2c[:, 1:2]))
        p.sop("scalar", lambda e: e.activation(acol[:, :], lt_col[:, :], AF.Exp))
        p.sop("scalar", lambda e: e.activation(
            wcol[:, :], lt_col[:, :], AF.Exp, bias=lc128[:, 0:1], scale=-1.0))
        p.sop("scalar", lambda e: e.activation(ecall[:, :], lc128[:, :], AF.Exp))
        if need_bcast:
            p.sop("tensor", lambda e: e.matmul(
                psb[:, :], p._onesrow_f[0:1, 0:128], llc[0:1, :],
                start=True, stop=True))
            p.sop("vector", lambda e: e.tensor_copy(bcsb[:, :], psb[:, :]))
            p.sop("vector", lambda e: e.tensor_add(
                e1in[:, :], bcsb[:, :], mask_sb[:, :]))
            p.sop("vector", lambda e: e.tensor_scalar_add(
                e2in[:, :], e1in[:, :], 1e4))

    for pr in range(NP):
        p.sop("sync", lambda e, pr=pr: e.dma_start(
            out=pq[:, :], in_=phiq[pr * 64:(pr + 1) * 64, :]), dma=True)
        p.sop("sync", lambda e, pr=pr: e.dma_start(
            out=pk[:, :], in_=phik[pr * 64:(pr + 1) * 64, :]), dma=True)
        p.sop("sync", lambda e, pr=pr: e.dma_start(
            out=vt[:, :].rearrange("p (c n) -> p c n", c=TT // 128),
            in_=vtok[pr * TT:(pr + 1) * TT, :].rearrange(
                "(c p) n -> p c n", p=128)), dma=True)
        p.sop("sync", lambda e, pr=pr: e.dma_start(
            out=ll[0:1, :], in_=loglam[pr:pr + 1, :]), dma=True)

        p.sop("vector", lambda e: e.memset(R_f[:, :], 0.0))
        p.sop("vector", lambda e: e.memset(S_f[:, :], 0.0))

        for ch in range(nchk - 1, -1, -1):
            chunk_common(pr, ch, need_bcast=False)
            p.sop("vector", lambda e, ch=ch: e.tensor_copy(
                rstore[:, ch * 65:(ch + 1) * 65], R_f[:, :]))
            p.sop("tensor", lambda e, ch=ch: e.transpose(
                psPT[:, :], pk[:, ch * CH:(ch + 1) * CH], p._identb[0:64, 0:64]))
            p.sop("vector", lambda e, ch=ch: e.tensor_copy(
                pkt[:, ch * 64:(ch + 1) * 64], psPT[:, :]))
            p.sop("vector", lambda e, ch=ch: e.tensor_scalar_mul(
                Vsc[:, 0:64], vt[:, ch * 64:(ch + 1) * 64], acol[:, 0:1]))
            p.sop("vector", lambda e: e.tensor_copy(Vsc[:, 64:65], acol[:, :]))
            p.sop("tensor", lambda e, ch=ch: e.matmul(
                psS[:, :], pkt[:, ch * 64:(ch + 1) * 64], Vsc[:, :],
                start=True, stop=True))
            p.sop("vector", lambda e: e.tensor_scalar_mul(
                R_f[:, :], R_f[:, :], ecall[0:64, 0:1]))
            p.sop("vector", lambda e: e.tensor_add(R_f[:, :], R_f[:, :],
                                                   psS[:, :]))

        for ch in range(nchk):
            chunk_common(pr, ch, need_bcast=True)
            p.sop("scalar", lambda e: e.activation(
                E1[:, :], e1in[:, :], AF.Exp, bias=nlt_col[:, 0:1]))
            p.sop("scalar", lambda e: e.activation(
                E2[:, :], e2in[:, :], AF.Exp, bias=lt_col[:, 0:1], scale=-1.0))
            p.sop("vector", lambda e: e.tensor_add(Etot[:, :], E1[:, :],
                                                   E2[:, :]))
            p.sop("scalar", lambda e: e.activation(
                expb[:, :], bcsb[0:64, :], AF.Exp))
            p.sop("scalar", lambda e: e.activation(
                wrow[:, :], bcsb[0:64, :], AF.Exp, bias=lc128[0:64, 0:1],
                scale=-1.0))
            p.sop("tensor", lambda e, ch=ch: e.matmul(
                psG[:, :], pk[:, ch * CH:(ch + 1) * CH],
                pq[:, ch * CH:(ch + 1) * CH], start=True, stop=True))
            if dbg and pr == 0 and ch == nchk - 1:
                p.sop("vector", lambda e: e.tensor_copy(e1in[:, :], psG[:, :]))
                p.sop("sync", lambda e: e.dma_start(out=dG, in_=e1in[:, :]),
                      dma=True)
            p.sop("vector", lambda e: e.tensor_tensor(
                Asb[:, :], psG[:, :], Etot[:, :], ALU.mult))
            p.sop("vector", lambda e, ch=ch: e.tensor_mul(
                qs[:, :], pq[:, ch * CH:(ch + 1) * CH], expb[:, :]))
            p.sop("vector", lambda e, ch=ch: e.tensor_mul(
                qsr[:, :], pq[:, ch * CH:(ch + 1) * CH], wrow[:, :]))
            p.sop("vector", lambda e: e.tensor_copy(Sb[:, :], S_f[:, :]))
            p.sop("tensor", lambda e, ch=ch: e.matmul(
                psO[0:64, :], vt[:, ch * 64:(ch + 1) * 64], Asb[:, :],
                start=True, stop=True, skip_group_check=True))
            p.sop("tensor", lambda e: e.matmul(
                psO[64:65, :], p._onescol_b[:, 0:1], Asb[:, :],
                start=True, stop=True, skip_group_check=True))
            p.sop("tensor", lambda e: e.matmul(
                psO2[0:65, :], Sb[:, :], qs[:, :],
                start=True, stop=False, skip_group_check=True))
            p.sop("tensor", lambda e, ch=ch: e.matmul(
                psO2[0:65, :], rstore[:, ch * 65:(ch + 1) * 65], qsr[:, :],
                start=False, stop=True, skip_group_check=True))
            p.sop("vector", lambda e, ch=ch: e.tensor_copy(
                res[:, ch * CH:(ch + 1) * CH], psO[0:65, :]))
            p.sop("vector", lambda e, ch=ch: e.tensor_add(
                res[:, ch * CH:(ch + 1) * CH],
                res[:, ch * CH:(ch + 1) * CH], psO2[0:65, :]))
            p.sop("vector", lambda e, ch=ch: e.tensor_scalar_mul(
                Vsc[:, 0:64], vt[:, ch * 64:(ch + 1) * 64], wcol[:, 0:1]))
            p.sop("vector", lambda e: e.tensor_copy(Vsc[:, 64:65], wcol[:, :]))
            p.sop("tensor", lambda e, ch=ch: e.matmul(
                psS[:, :], pkt[:, ch * 64:(ch + 1) * 64], Vsc[:, :],
                start=True, stop=True))
            p.sop("vector", lambda e: e.tensor_scalar_mul(
                S_f[:, :], S_f[:, :], ecall[0:64, 0:1]))
            p.sop("vector", lambda e: e.tensor_add(S_f[:, :], S_f[:, :],
                                                   psS[:, :]))

        if dbg and pr == 0:
            p.sop("sync", lambda e: e.dma_start(out=dres, in_=res[:, :]),
                  dma=True)
            p.sop("vector", lambda e: e.tensor_copy(e1in[:, :], E1[:, :]))
            p.sop("sync", lambda e: e.dma_start(out=dE[0:128, :],
                                                in_=e1in[:, :]), dma=True)
            p.sop("vector", lambda e: e.tensor_copy(e1in[:, :], E2[:, :]))
            p.sop("sync", lambda e: e.dma_start(out=dE[128:256, :],
                                                in_=e1in[:, :]), dma=True)
            p.sop("vector", lambda e: e.tensor_copy(e1in[:, :], Etot[:, :]))
            p.sop("sync", lambda e: e.dma_start(out=dE[256:384, :],
                                                in_=e1in[:, :]), dma=True)
            p.sop("vector", lambda e: e.tensor_copy(e1in[:, :], Asb[:, :]))
            p.sop("sync", lambda e: e.dma_start(out=dE[384:512, :],
                                                in_=e1in[:, :]), dma=True)
            for cc in range(min(8, nchk)):
                p.sop("vector", lambda e, cc=cc: e.tensor_copy(
                    Vsc[:, :] if False else S_f[:, :], S_f[:, :]))
                p.sop("sync", lambda e, cc=cc: e.dma_start(
                    out=dst8[:, cc * 65:(cc + 1) * 65],
                    in_=rstore[:, cc * 65:(cc + 1) * 65]), dma=True)

        p.sop("vector", lambda e: e.tensor_scalar_max(
            ll[0:1, :], res[64:65, :], 1e-6))
        p.sop("vector", lambda e: e.reciprocal(ll[0:1, :], ll[0:1, :]))
        for tb in range(TT // 512):
            p.sop("tensor", lambda e, tb=tb: e.matmul(
                psN[:, :], p._onesrow_f[0:1, 0:64],
                ll[0:1, tb * 512:(tb + 1) * 512], start=True, stop=True))
            p.sop("vector", lambda e, tb=tb: e.tensor_mul(
                oattn[:, :], res[0:64, tb * 512:(tb + 1) * 512], psN[:, :]))
            p.sop("sync", lambda e, pr=pr, tb=tb: e.dma_start(
                out=attn[pr * 64:(pr + 1) * 64, tb * 512:(tb + 1) * 512],
                in_=oattn[:, :]), dma=True)

    p.wait("sync", "seq")
    return p.build()



def ld_bias_cols(p, dst_sb, vec_ap, nch):
    p.op("gpsimd", lambda e: e.dma_start(
        out=dst_sb[:, 0:nch], in_=vec_ap.rearrange("(c p) -> p c", p=128),
        allow_slow_non_contiguous=True), inc="ld", incn=16)


def srows_meanvar_rstd(p, sum_row, sq_row, ntok, nd, rstd_row, negms_row,
                      eps=LN_EPS):
    p.sop("vector", lambda e: e.tensor_scalar_mul(
        sum_row[0:1, 0:ntok], sum_row[0:1, 0:ntok], 1.0 / nd))
    p.sop("vector", lambda e: e.tensor_scalar_mul(
        sq_row[0:1, 0:ntok], sq_row[0:1, 0:ntok], 1.0 / nd))
    p.sop("vector", lambda e: e.tensor_mul(
        p._rtmp[0:1, 0:ntok], sum_row[0:1, 0:ntok], sum_row[0:1, 0:ntok]))
    p.sop("vector", lambda e: e.tensor_sub(
        sq_row[0:1, 0:ntok], sq_row[0:1, 0:ntok], p._rtmp[0:1, 0:ntok]))
    p.sop("scalar", lambda e: e.activation(
        p._rtmp[0:1, 0:ntok], sq_row[0:1, 0:ntok], AF.Sqrt,
        bias=p._eps_col[0:1, 0:1]))
    p.sop("vector", lambda e: e.reciprocal(
        rstd_row[0:1, 0:ntok], p._rtmp[0:1, 0:ntok]))
    p.sop("vector", lambda e: e.tensor_mul(
        negms_row[0:1, 0:ntok], sum_row[0:1, 0:ntok], rstd_row[0:1, 0:ntok]))
    p.sop("vector", lambda e: e.tensor_scalar_mul(
        negms_row[0:1, 0:ntok], negms_row[0:1, 0:ntok], -1.0))


def sln_apply(p, x_sb_or_stream, scale_row, negms_row, g_row, b_row,
              out_dram, ntok, nch, psA, psB, tile_f, tile_o, src_dram=None,
              src_dt=BF16, xbuf=None):
    for tb in range(cdiv(ntok, 512)):
        n = min(512, ntok - tb * 512)
        if src_dram is not None:
            src = src_dram.rearrange("(c p) n -> p c n", p=128)[
                :, :, tb * 512:tb * 512 + n]
            p.sop("sync", lambda e, src=src, n=n: e.dma_start(
                out=xbuf[:, 0:nch * n].rearrange("p (c n) -> p c n", c=nch),
                in_=src), dma=True)
            xsb, XN = xbuf, n
            xoff = 0
        else:
            xsb, XN = x_sb_or_stream
            xoff = tb * 512
        for ci in range(nch):
            grow = (g_row[0:1, ci * 128:(ci + 1) * 128] if g_row is not None
                    else p._onesrow_f[0:1, 0:128])
            p.sop("tensor", lambda e, grow=grow, tb=tb, n=n: e.matmul(
                psA[:, 0:n], grow, scale_row[0:1, tb * 512:tb * 512 + n],
                start=True, stop=True))
            if negms_row is not None:
                p.sop("tensor", lambda e, grow=grow, tb=tb, n=n: e.matmul(
                    psB[:, 0:n], grow,
                    negms_row[0:1, tb * 512:tb * 512 + n],
                    start=True, stop=False, skip_group_check=True))
            p.sop("tensor", lambda e, ci=ci, n=n: e.matmul(
                psB[:, 0:n], b_row[0:1, ci * 128:(ci + 1) * 128],
                p._onesrow_f[0:1, 0:n],
                start=(negms_row is None), stop=True, skip_group_check=True))
            p.sop("vector", lambda e, ci=ci, n=n, xsb=xsb, XN=XN, xoff=xoff:
                  e.tensor_mul(tile_f[:, 0:n],
                               xsb[:, ci * XN + xoff:ci * XN + xoff + n],
                               psA[:, 0:n]))
            p.sop("vector", lambda e, n=n: e.tensor_add(
                tile_o[:, 0:n], tile_f[:, 0:n], psB[:, 0:n]))
            p.sop("sync", lambda e, ci=ci, tb=tb, n=n: e.dma_start(
                out=out_dram[ci * 128:(ci + 1) * 128, tb * 512:tb * 512 + n],
                in_=tile_o[:, 0:n]), dma=True)


def build_aoln(dd=D, ow_name="o_w"):
    p = Prog()
    attn = p.dram("attn", [dd, TH], BF16, kind="ExternalInput")
    xn0 = p.dram("xn0", [dd, TH], F32, kind="ExternalInput")
    ow = p.dram("o_w", [dd, dd], F32, kind="ExternalInput")
    ob = p.dram("o_b", [dd], F32, kind="ExternalInput")
    ln2w = p.dram("ln2_w", [dd], F32, kind="ExternalInput")
    ln2b = p.dram("ln2_b", [dd], F32, kind="ExternalInput")
    xa = p.dram("xa", [dd, TH], BF16, kind="ExternalOutput")
    h2a = p.dram("h2a", [dd, TH], BF16, kind="ExternalOutput")

    nch = dd // 128
    common_tiles(p)
    at_sb = p.sb("at_sb", [128, nch * 512], BF16)
    xn_sb = p.sb("xn_sb", [128, nch * 512], BF16)
    ow_sb = p.sb("ow_sb", [128, nch * dd], BF16)
    ob_c = p.sb("ob_c", [128, nch], F32)
    ln2w_r = p.sb("ln2w_r", [1, dd], F32)
    ln2b_r = p.sb("ln2b_r", [1, dd], F32)
    sum_row = p.sb("sum_row", [1, TH], F32)
    sq_row = p.sb("sq_row", [1, TH], F32)
    rstd_row = p.sb("rstd_row", [1, TH], F32)
    negms_row = p.sb("negms_row", [1, TH], F32)
    t1 = p.sb("t1", [128, 512], F32)
    t2 = p.sb("t2", [128, 512], BF16)
    sqt = p.sb("sqt", [128, 512], BF16)
    ps = p.ps("ps", [128, 512], F32)
    psA = p.ps("psA", [128, 512], F32)
    psB = p.ps("psB", [128, 512], F32)
    psS1 = p.ps("psS1", [1, 512], F32)
    psS2 = p.ps("psS2", [1, 512], F32)

    p.sop("gpsimd", lambda e: e.dma_start(
        out=ow_sb[:, :].rearrange("p (c n) -> p c n", c=nch),
        in_=ow.rearrange("(c p) n -> p c n", p=128)), dma=True)
    p.sop("gpsimd", lambda e: e.dma_start(
        out=ob_c[:, 0:nch], in_=ob.rearrange("(c p) -> p c", p=128),
        allow_slow_non_contiguous=True), dma=True)
    p.sop("sync", lambda e: e.dma_start(out=ln2w_r[0:1, 0:dd],
                                        in_=ln2w[None, :]), dma=True)
    p.sop("sync", lambda e: e.dma_start(out=ln2b_r[0:1, 0:dd],
                                        in_=ln2b[None, :]), dma=True)

    for tb in range(TH // 512):
        p.sop("sync", lambda e, tb=tb: e.dma_start(
            out=at_sb[:, :].rearrange("p (c n) -> p c n", c=nch),
            in_=attn.rearrange("(c p) n -> p c n", p=128)[
                :, :, tb * 512:(tb + 1) * 512]), dma=True)
        p.sop("gpsimd", lambda e, tb=tb: e.dma_start(
            out=xn_sb[:, :].rearrange("p (c n) -> p c n", c=nch),
            in_=xn0.rearrange("(c p) n -> p c n", p=128)[
                :, :, tb * 512:(tb + 1) * 512]), dma=True)
        for co in range(nch):
            for ci in range(nch):
                p.sop("tensor", lambda e, co=co, ci=ci, tb=tb: e.matmul(
                    ps[:, :], ow_sb[:, ci * dd + co * 128:ci * dd + co * 128 + 128],
                    at_sb[:, ci * 512:(ci + 1) * 512],
                    start=(ci == 0), stop=(ci == nch - 1)))
            p.sop("scalar", lambda e, co=co: e.activation(
                t1[:, :], ps[:, :], AF.Identity, bias=ob_c[:, co:co + 1]))
            p.sop("vector", lambda e, co=co, tb=tb: e.tensor_add(
                t2[:, :], t1[:, :],
                xn_sb[:, co * 512:(co + 1) * 512]))
            p.sop("sync", lambda e, co=co, tb=tb: e.dma_start(
                out=xa[co * 128:(co + 1) * 128, tb * 512:tb * 512 + 512],
                in_=t2[:, :]), dma=True)
            p.sop("scalar", lambda e: e.activation(sqt[:, :], t2[:, :],
                                                   AF.Square))
            p.sop("tensor", lambda e, co=co: e.matmul(
                psS1[0:1, :], p._onescol_b[:, 0:1], t2[:, :],
                start=(co == 0), stop=(co == nch - 1)))
            p.sop("tensor", lambda e, co=co: e.matmul(
                psS2[0:1, :], p._onescol_b[:, 0:1], sqt[:, :],
                start=(co == 0), stop=(co == nch - 1)))
        p.sop("vector", lambda e, tb=tb: e.tensor_copy(
            sum_row[0:1, tb * 512:(tb + 1) * 512], psS1[0:1, :]))
        p.sop("vector", lambda e, tb=tb: e.tensor_copy(
            sq_row[0:1, tb * 512:(tb + 1) * 512], psS2[0:1, :]))

    srows_meanvar_rstd(p, sum_row, sq_row, TH, dd, rstd_row, negms_row)
    xbuf = p.sb("xbuf", [128, nch * 512], BF16)
    sln_apply(p, None, rstd_row, negms_row, ln2w_r, ln2b_r, h2a, TH, nch,
              psA, psB, t1, t2, src_dram=xa, xbuf=xbuf)
    p.wait("sync", "seq")
    return p.build()


def build_affn(dff=4096):
    p = Prog()
    h2a = p.dram("h2a", [D, TH], BF16, kind="ExternalInput")
    xa = p.dram("xa", [D, TH], BF16, kind="ExternalInput")
    f1w = p.dram("f1_w", [D, dff], F32, kind="ExternalInput")
    f1b = p.dram("f1_b", [dff], F32, kind="ExternalInput")
    f2w = p.dram("f2_w", [dff, D], F32, kind="ExternalInput")
    f2b = p.dram("f2_b", [D], F32, kind="ExternalInput")
    a0 = p.dram("a0", [D, TH], BF16, kind="ExternalOutput")

    nh = dff // 128
    NTB = TH // 512
    NG = nh + NCH
    common_tiles(p)
    w1_sb = p.sb("w1_sb", [128, NCH * dff], BF16)
    w2_sb = p.sb("w2_sb", [128, nh * D], BF16)
    b1_c = p.sb("b1_c", [128, nh], F32)
    b2_c = p.sb("b2_c", [128, NCH], F32)
    hb = p.sb("hb", [128, nh * 512], BF16)
    xab = p.sb("xab", [128, NCH * 512], BF16)
    h2b = p.sb("h2b", [128, NCH * 512], BF16)
    t1 = p.sb("t1", [128, 512], F32)
    t2 = [p.sb(f"t2_{i}", [128, 512], BF16) for i in range(2)]
    ps = [p.ps(f"ps{i}", [128, 512], F32) for i in range(2)]

    def W(*pairs):
        return [(n, v) for (n, v) in pairs if v > 0] or None

    p.op("gpsimd", lambda e: e.dma_start(
        out=w1_sb[:, :].rearrange("p (c n) -> p c n", c=NCH),
        in_=f1w.rearrange("(c p) n -> p c n", p=128)), inc="wl", incn=16)
    p.op("gpsimd", lambda e: e.dma_start(
        out=w2_sb[:, :].rearrange("p (c n) -> p c n", c=nh),
        in_=f2w.rearrange("(c p) n -> p c n", p=128)), inc="wl", incn=16)
    p.op("gpsimd", lambda e: e.dma_start(
        out=b1_c[:, 0:nh], in_=f1b.rearrange("(c p) -> p c", p=128),
        allow_slow_non_contiguous=True), inc="wl", incn=16)
    p.op("gpsimd", lambda e: e.dma_start(
        out=b2_c[:, 0:NCH], in_=f2b.rearrange("(c p) -> p c", p=128),
        allow_slow_non_contiguous=True), inc="wl", incn=16)

    for tb in range(NTB):
        lw = []
        if tb >= 1:
            lw = [("mg", (tb - 1) * NG + nh)]
        p.op("sync", lambda e, tb=tb: e.dma_start(
            out=h2b[:, :].rearrange("p (c n) -> p c n", c=NCH),
            in_=h2a.rearrange("(c p) n -> p c n", p=128)[
                :, :, tb * 512:(tb + 1) * 512]),
            inc="xl", incn=16, wait=W(*lw) if lw else None)
        p.op("sync", lambda e, tb=tb: e.dma_start(
            out=xab[:, :].rearrange("p (c n) -> p c n", c=NCH),
            in_=xa.rearrange("(c p) n -> p c n", p=128)[
                :, :, tb * 512:(tb + 1) * 512]),
            inc="xs", incn=16, wait=W(("va", 8 * tb)) if tb >= 1 else None)
    for tb in range(NTB):
        base = tb * NG
        for co in range(nh):
            g = base + co
            for ci in range(NCH):
                w_ = [("ae", g - 1)]
                if co == 0 and ci == 0:
                    w_.append(("xl", 16 * (tb + 1)))
                    if tb == 0:
                        w_.append(("wl", 64))
                p.op("tensor", lambda e, tb=tb, co=co, ci=ci, g=g: e.matmul(
                    ps[g % 2][:, :],
                    w1_sb[:, ci * dff + co * 128:ci * dff + co * 128 + 128],
                    h2b[:, ci * 512:(ci + 1) * 512],
                    start=(ci == 0), stop=(ci == NCH - 1),
                    skip_group_check=True),
                    inc="mg" if ci == NCH - 1 else None,
                    wait=W(*w_) if ci == 0 else None)
            p.op("scalar", lambda e, co=co, g=g: e.activation(
                hb[:, co * 512:(co + 1) * 512], ps[g % 2][:, :], AF.Gelu,
                bias=b1_c[:, co:co + 1]), inc="ae", wait=W(("mg", g + 1)))
        for co in range(NCH):
            g = base + nh + co
            o = tb * NCH + co
            for ci in range(nh):
                w_ = [("ae", g if co == 0 else g - 1)]
                p.op("tensor", lambda e, co=co, ci=ci, g=g: e.matmul(
                    ps[g % 2][:, :],
                    w2_sb[:, ci * D + co * 128:ci * D + co * 128 + 128],
                    hb[:, ci * 512:(ci + 1) * 512],
                    start=(ci == 0), stop=(ci == nh - 1),
                    skip_group_check=True),
                    inc="mg" if ci == nh - 1 else None,
                    wait=W(*w_) if ci == 0 else None)
            p.op("scalar", lambda e, co=co, g=g, o=o: e.activation(
                t1[:, :], ps[g % 2][:, :], AF.Identity,
                bias=b2_c[:, co:co + 1]), inc="ae",
                wait=W(("mg", g + 1), ("va", o)))
            aw = [("ae", g + 1), ("st%d" % (o % 2), 16 * (o // 2))]
            if co == 0:
                aw.append(("xs", 16 * (tb + 1)))
            p.op("vector", lambda e, tb=tb, co=co, o=o: e.tensor_add(
                t2[o % 2][:, :], t1[:, :],
                xab[:, co * 512:(co + 1) * 512]), inc="va",
                wait=W(*aw))
            p.op("gpsimd", lambda e, co=co, tb=tb, o=o: e.dma_start(
                out=a0[co * 128:(co + 1) * 128, tb * 512:tb * 512 + 512],
                in_=t2[o % 2][:, :]), inc="st%d" % (o % 2), incn=16,
                wait=W(("va", o + 1)))
    p.wait("gpsimd", "st0", 16 * (NTB * NCH // 2))
    p.wait("gpsimd", "st1", 16 * (NTB * NCH // 2))
    return p.build()


def run_launch(nc, in_maps):
    res = run_bass_kernel_spmd(nc, in_maps, core_ids=list(range(NCORE)))
    return res.results


def shard_rows(x):
    out = []
    for c in range(NCORE):
        b, h = c // 2, c % 2
        sl = np.ascontiguousarray(x[b, h * TH:(h + 1) * TH, :].T)
        out.append(sl)
    return out



def build_fqkv():
    p = Prog()
    xn0 = p.dram("xn0", [D, TH], F32, kind="ExternalInput")
    vyd = p.dram("vy", [1, TH], F32, kind="ExternalInput")
    wq = p.dram("wq", [D, D], F32, kind="ExternalInput")
    wk = p.dram("wk", [D, D], F32, kind="ExternalInput")
    wv = p.dram("wv", [D, D], F32, kind="ExternalInput")
    bq = p.dram("bq", [D], F32, kind="ExternalInput")
    bk = p.dram("bk", [D], F32, kind="ExternalInput")
    bv = p.dram("bv", [D], F32, kind="ExternalInput")
    ln1g = p.dram("ln1_g", [D], F32, kind="ExternalInput")
    ln1b = p.dram("ln1_b", [D], F32, kind="ExternalInput")
    qF = p.dram("qF", [D, TH], BF16, kind="ExternalOutput")
    kF = p.dram("kF", [D, TH], BF16, kind="ExternalOutput")
    vF = p.dram("vF", [TH, D], BF16, kind="ExternalOutput")

    common_tiles(p)
    wq_sb = p.sb("wq_sb", [128, NCH * D], BF16)
    wk_sb = p.sb("wk_sb", [128, NCH * D], BF16)
    wv_sb = p.sb("wv_sb", [128, NCH * D], BF16)
    qb_c = p.sb("qb_c", [128, NCH], F32)
    kb_c = p.sb("kb_c", [128, NCH], F32)
    vb_r = p.sb("vb_r", [1, D], BF16)
    g_r = p.sb("g_r", [1, D], F32)
    b_r = p.sb("b_r", [1, D], F32)
    vy_r = p.sb("vy_r", [1, TH], F32)
    hf = p.sb("hf", [128, NCH * TH], BF16)
    xbuf = p.sb("xbuf", [128, NCH * 512], F32)
    t1 = p.sb("t1", [128, 512], F32)
    t2 = p.sb("t2", [128, 512], BF16)
    ps = p.ps("ps", [128, 512], F32)
    psA = p.ps("psA", [128, 512], F32)
    psB = p.ps("psB", [128, 512], F32)

    p.sop("gpsimd", lambda e: e.dma_start(
        out=wq_sb[:, :].rearrange("p (c n) -> p c n", c=NCH),
        in_=wq.rearrange("(c p) n -> p c n", p=128)), dma=True)
    p.sop("gpsimd", lambda e: e.dma_start(
        out=wk_sb[:, :].rearrange("p (c n) -> p c n", c=NCH),
        in_=wk.rearrange("(c p) n -> p c n", p=128)), dma=True)
    p.sop("gpsimd", lambda e: e.dma_start(
        out=wv_sb[:, :].rearrange("p (c n) -> p c n", c=NCH),
        in_=wv.rearrange("(c p) n -> p c n", p=128)), dma=True)
    p.sop("gpsimd", lambda e: e.dma_start(
        out=qb_c[:, 0:NCH], in_=bq.rearrange("(c p) -> p c", p=128),
        allow_slow_non_contiguous=True), dma=True)
    p.sop("gpsimd", lambda e: e.dma_start(
        out=kb_c[:, 0:NCH], in_=bk.rearrange("(c p) -> p c", p=128),
        allow_slow_non_contiguous=True), dma=True)
    p.sop("gpsimd", lambda e: e.dma_start(out=vb_r[0:1, 0:D],
                                          in_=bv[None, :]), dma=True)
    p.sop("sync", lambda e: e.dma_start(out=g_r[0:1, 0:D],
                                        in_=ln1g[None, :]), dma=True)
    p.sop("sync", lambda e: e.dma_start(out=b_r[0:1, 0:D],
                                        in_=ln1b[None, :]), dma=True)
    p.sop("sync", lambda e: e.dma_start(out=vy_r[0:1, :],
                                        in_=vyd[0:1, :]), dma=True)
    p.sop("vector", lambda e: e.tensor_scalar_mul(qb_c[:, :], qb_c[:, :],
                                                  0.125))
    p.sop("scalar", lambda e: e.activation(p._rtmp[0:1, 0:TH],
                                           vy_r[0:1, 0:TH], AF.Sqrt,
                                           bias=p._eps_col[0:1, 0:1]))
    p.sop("vector", lambda e: e.reciprocal(vy_r[0:1, 0:TH],
                                           p._rtmp[0:1, 0:TH]))
    for tb in range(TH // 512):
        p.sop("sync", lambda e, tb=tb: e.dma_start(
            out=xbuf[:, :].rearrange("p (c n) -> p c n", c=NCH),
            in_=xn0.rearrange("(c p) n -> p c n", p=128)[
                :, :, tb * 512:(tb + 1) * 512]), dma=True)
        for ci in range(NCH):
            p.sop("tensor", lambda e, ci=ci, tb=tb: e.matmul(
                psA[:, :], g_r[0:1, ci * 128:(ci + 1) * 128],
                vy_r[0:1, tb * 512:(tb + 1) * 512], start=True, stop=True))
            p.sop("tensor", lambda e, ci=ci: e.matmul(
                psB[:, :], b_r[0:1, ci * 128:(ci + 1) * 128],
                p._onesrow_f[0:1, 0:512], start=True, stop=True))
            p.sop("vector", lambda e, ci=ci: e.tensor_mul(
                t1[:, :], xbuf[:, ci * 512:(ci + 1) * 512], psA[:, :]))
            p.sop("vector", lambda e, ci=ci, tb=tb: e.tensor_add(
                hf[:, ci * TH + tb * 512:ci * TH + tb * 512 + 512],
                t1[:, :], psB[:, :]))
    for which, wsb, dst in (("q", wq_sb, qF), ("k", wk_sb, kF)):
        for co in range(NCH):
            for tb in range(TH // 512):
                for ci in range(NCH):
                    p.sop("tensor", lambda e, wsb=wsb, co=co, ci=ci, tb=tb:
                          e.matmul(ps[:, :],
                                   wsb[:, ci * D + co * 128:ci * D + co * 128 + 128],
                                   hf[:, ci * TH + tb * 512:ci * TH + tb * 512 + 512],
                                   start=(ci == 0), stop=(ci == NCH - 1)))
                if which == "q":
                    p.sop("scalar", lambda e, co=co: e.activation(
                        t2[:, :], ps[:, :], AF.Identity,
                        bias=qb_c[:, co:co + 1], scale=0.125))
                else:
                    p.sop("scalar", lambda e, co=co: e.activation(
                        t2[:, :], ps[:, :], AF.Identity,
                        bias=kb_c[:, co:co + 1]))
                p.sop("sync", lambda e, dst=dst, co=co, tb=tb: e.dma_start(
                    out=dst[co * 128:(co + 1) * 128, tb * 512:tb * 512 + 512],
                    in_=t2[:, :]), dma=True)
    for tt in range(TH // 128):
        for ob in range(D // 512):
            for ci in range(NCH):
                p.sop("tensor", lambda e, tt=tt, ob=ob, ci=ci: e.matmul(
                    ps[:, :],
                    hf[:, ci * TH + tt * 128:ci * TH + tt * 128 + 128],
                    wv_sb[:, ci * D + ob * 512:ci * D + ob * 512 + 512],
                    start=(ci == 0), stop=False))
            p.sop("tensor", lambda e, tt=tt, ob=ob: e.matmul(
                ps[:, :], p._ones_b[0:1, 0:128],
                vb_r[0:1, ob * 512:ob * 512 + 512],
                start=False, stop=True))
            p.sop("scalar", lambda e: e.activation(t2[:, :], ps[:, :],
                                                   AF.Copy))
            p.sop("sync", lambda e, tt=tt, ob=ob: e.dma_start(
                out=vF[tt * 128:(tt + 1) * 128, ob * 512:ob * 512 + 512],
                in_=t2[:, :]), dma=True)
    p.wait("sync", "seq")
    return p.build()



def build_fattn(NP=8, TT=T):
    p = Prog()
    qF = p.dram("qF", [NP * 64, TT], BF16, kind="ExternalInput")
    kF = p.dram("kF", [NP * 64, TT], BF16, kind="ExternalInput")
    vF = p.dram("vF", [NP * TT, 64], BF16, kind="ExternalInput")
    attnF = p.dram("attnF", [NP * 64, TT], BF16, kind="ExternalOutput")

    common_tiles(p)
    NKC = TT // 128
    NQB = TT // 512
    NIT = NQB * NKC
    qsb = [p.sb(f"qsb{i}", [64, TT], BF16) for i in range(2)]
    ksb = [p.sb(f"ksb{i}", [64, TT], BF16) for i in range(2)]
    vsb = [p.sb(f"vsb{i}", [128, NKC * 65], BF16) for i in range(2)]
    esb = [p.sb(f"esb{i}", [128, 512], BF16) for i in range(2)]
    negc_col = [p.sb(f"negcc{i}", [128, 1], F32) for i in range(2)]
    oatt = [p.sb(f"oatt{i}", [64, 512], BF16) for i in range(2)]
    tt1 = p.sb("tt1", [64, 512], F32)
    denr = p.sb("denr", [1, 512], F32)
    sqf = p.sb("sqf", [64, 512], BF16)
    qmax = p.sb("qmax", [1, 1], F32)
    kmax = p.sb("kmax", [1, 1], F32)
    tmax = p.sb("tmax", [1, 1], F32)
    negc = p.sb("negc", [1, 1], F32)
    psSc = [p.ps(f"psSc{i}", [128, 512], F32) for i in range(2)]
    psO = [p.ps(f"psO{i}", [65, 512], F32) for i in range(2)]
    psB = p.ps("psB", [64, 512], F32)
    psDC = p.ps("psDC", [128, 512], F32)
    psD = psDC
    psC = psDC

    def W(*pairs):
        return [(n, v) for (n, v) in pairs if v > 0] or None

    p.op("vector", lambda e: e.memset(vsb[0][:, :], 1.0), inc="ms")
    p.op("vector", lambda e: e.memset(vsb[1][:, :], 1.0), inc="ms")

    def emit_loads(pr):
        par = pr % 2
        ldn = "ld%d" % par
        p.op("sync", lambda e, pr=pr, par=par: e.dma_start(
            out=qsb[par][:, :], in_=qF[pr * 64:(pr + 1) * 64, :]),
            inc=ldn, incn=16,
            wait=W(("qd", 16 * (pr - 1))) if pr >= 2 else None)
        p.op("sync", lambda e, pr=pr, par=par: e.dma_start(
            out=ksb[par][:, :], in_=kF[pr * 64:(pr + 1) * 64, :]),
            inc=ldn, incn=16)
        p.op("sync", lambda e, pr=pr, par=par: e.dma_start(
            out=vsb[par][:, :].rearrange("p (c n) -> p c n", n=65)[:, :, 0:64],
            in_=vF[pr * TT:(pr + 1) * TT, :].rearrange("(c p) n -> p c n",
                                                       p=128)),
            inc=ldn, incn=16, wait=W(("ms", 2)) if pr < 2 else None)

    pq_c = [0]

    def prologue_ops(pr):
        par = pr % 2
        ops = []

        def ch(eng, fn, extra=None, inc=None):
            ops.append((eng, fn, extra, inc))

        nld = 48 * (pr // 2 + 1)
        ch("vector", lambda e: e.memset(qmax[:, :], 0.0),
           extra=[("ld%d" % par, nld)])
        ch("vector", lambda e: e.memset(kmax[:, :], 0.0))
        for sb_, mx in ((qsb[par], qmax), (ksb[par], kmax)):
            for tb in range(TT // 512):
                ch("vector", lambda e, sb_=sb_, tb=tb: e.tensor_tensor(
                    sqf[:, :], sb_[:, tb * 512:(tb + 1) * 512],
                    sb_[:, tb * 512:(tb + 1) * 512], ALU.mult))
                ch("tensor", lambda e: e.matmul(
                    psD[0:1, :], p._onescol_b[0:64, 0:1], sqf[:, :],
                    start=True, stop=True, skip_group_check=True))
                ch("vector", lambda e: e.tensor_reduce(
                    tmax[0:1, 0:1], psD[0:1, :], AX.X, ALU.max))
                ch("vector", lambda e, mx=mx: e.tensor_max(
                    mx[0:1, 0:1], mx[0:1, 0:1], tmax[0:1, 0:1]))
        ch("vector", lambda e: e.tensor_mul(negc[0:1, 0:1], qmax[0:1, 0:1],
                                            kmax[0:1, 0:1]))
        ch("scalar", lambda e: e.activation(negc[0:1, 0:1], negc[0:1, 0:1],
                                            AF.Ln))
        ch("scalar", lambda e: e.activation(negc[0:1, 0:1], negc[0:1, 0:1],
                                            AF.Exp, scale=0.5))
        ch("vector", lambda e: e.tensor_scalar_mul(negc[0:1, 0:1],
                                                   negc[0:1, 0:1], -1.0))
        ch("tensor", lambda e: e.matmul(
            psC[:, 0:1], p._onesrow_f[0:1, 0:128], negc[0:1, 0:1],
            start=True, stop=True, skip_group_check=True),
           extra=[("pd", pr)])
        ch("vector", lambda e, par=par: e.tensor_copy(negc_col[par][:, :],
                                                      psC[:, 0:1]), inc="pd")
        return ops

    def emit_pro(op):
        eng, fn, extra, inc = op
        w = W(("pq", pq_c[0]), *(extra or []))
        if inc is None:
            p.op(eng, fn, inc="pq", wait=w)
            pq_c[0] += 1
        else:
            p.op(eng, fn, inc=inc, wait=w)

    def epi_pe(qq):
        p.op("tensor", lambda e: e.matmul(
            psB[:, :], p._onesrow_f[0:1, 0:64], denr[0:1, :],
            start=True, stop=True, skip_group_check=True),
            inc="eb", wait=W(("ev", qq + 1), ("eo", qq)))

    def epi_mul_store(qq, pr):
        p.op("vector", lambda e, qq=qq: e.tensor_mul(
            oatt[qq % 2][:, :], tt1[:, :], psB[:, :]), inc="eo",
            wait=W(("eb", qq + 1), ("st%d" % (qq % 2), 16 * (qq // 2))))
        qb = qq % NQB
        p.op("gpsimd", lambda e, pr=pr, qb=qb, qq=qq: e.dma_start(
            out=attnF[pr * 64:(pr + 1) * 64, qb * 512:(qb + 1) * 512],
            in_=oatt[qq % 2][:, :]), inc="st%d" % (qq % 2), incn=16,
            wait=W(("eo", qq + 1)))

    def epi_evac(qbg):
        p.op("vector", lambda e, qbg=qbg: e.tensor_copy(
            tt1[:, :], psO[qbg % 2][0:64, :]), wait=W(("qd", qbg + 1)))
        p.op("vector", lambda e, qbg=qbg: e.reciprocal(
            denr[0:1, :], psO[qbg % 2][64:65, :]), inc="ev")

    def emit_av(it2):
        par2 = (it2 // NIT) % 2
        kc2 = it2 % NKC
        qbg2 = it2 // NKC
        aw = [("fS", it2 + 1)]
        if kc2 == 0:
            aw.append(("ev", qbg2 - 1))
        p.op("tensor", lambda e, it2=it2, par2=par2, kc2=kc2, qbg2=qbg2:
             e.matmul(psO[qbg2 % 2][:, :],
                      vsb[par2][:, kc2 * 65:(kc2 + 1) * 65],
                      esb[it2 % 2][:, :],
                      start=(kc2 == 0), stop=(kc2 == NKC - 1),
                      skip_group_check=True),
             inc="qd" if kc2 == NKC - 1 else None, wait=W(*aw))

    emit_loads(0)
    emit_loads(1)
    for op in prologue_ops(0):
        emit_pro(op)
    pending = []
    pi = 0
    for pr in range(NP):
        par = pr % 2
        if pr + 2 < NP:
            emit_loads(pr + 2)
        pending = prologue_ops(pr + 1) if pr + 1 < NP else []
        pi = 0
        for i in range(NIT):
            it = pr * NIT + i
            qbg = it // NKC
            kc = it % NKC
            qb = i // NKC
            if i % 7 == 3 and pi < len(pending):
                emit_pro(pending[pi])
                pi += 1
            if kc == 3 and qbg >= 1:
                epi_pe(qbg - 1)
            if kc == 5 and qbg >= 1:
                epi_mul_store(qbg - 1, pr if qb > 0 else pr - 1)
            sw = W(("ld%d" % par, 48 * (pr // 2 + 1))) if i == 0 else None
            p.op("tensor", lambda e, it=it, par=par, qb=qb, kc=kc: e.matmul(
                psSc[it % 2][:, :], ksb[par][:, kc * 128:(kc + 1) * 128],
                qsb[par][:, qb * 512:(qb + 1) * 512],
                start=True, stop=True, skip_group_check=True),
                inc="fT", wait=sw)
            ew = [("fT", it + 1)]
            if i == 0:
                ew.append(("pd", pr + 1))
            p.op("scalar", lambda e, it=it, par=par: e.activation(
                esb[it % 2][:, :], psSc[it % 2][:, :], AF.Exp,
                bias=negc_col[par][:, 0:1]), inc="fS", wait=W(*ew))
            if it >= 1:
                emit_av(it - 1)
            if kc == NKC - 2:
                epi_evac(qbg)
    last_it = NP * NIT - 1
    emit_av(last_it)
    last = NP * NQB - 1
    epi_pe(last)
    epi_mul_store(last, NP - 1)
    p.wait("gpsimd", "st0", 16 * (NP * NQB // 2))
    p.wait("gpsimd", "st1", 16 * (NP * NQB // 2))
    return p.build()



def build_route():
    p = Prog()
    h2f = p.dram("h2f", [D, TH], BF16, kind="ExternalInput")
    wr = p.dram("wr", [D, 8], F32, kind="ExternalInput")
    p1d = p.dram("p1", [1, TH], F32, kind="ExternalOutput")
    p2d = p.dram("p2", [1, TH], F32, kind="ExternalOutput")
    e1d = p.dram("e1", [1, TH], F32, kind="ExternalOutput")
    e2d = p.dram("e2", [1, TH], F32, kind="ExternalOutput")
    s1d = p.dram("s1", [1, TH], F32, kind="ExternalOutput")
    s2d = p.dram("s2", [1, TH], F32, kind="ExternalOutput")
    cntd = p.dram("cnt", [2, 8], F32, kind="ExternalOutput")
    ident = p.dram("ident", [128, 128], F32, kind="ExternalInput")

    common_tiles(p)
    identf = p.sb("identf", [128, 128], F32)
    wr_sb = p.sb("wr_sb", [128, NCH * 8], BF16)
    hbuf = p.sb("hbuf", [128, NCH * 128], BF16)
    lg = p.sb("lg", [128, 8], F32)
    m1c = p.sb("m1c", [128, 1], F32)
    oh = p.sb("oh", [128, 8], F32)
    cum = p.sb("cum", [128, 8], F32)
    oh1 = p.sb("oh1", [128, 8], F32)
    oh2 = p.sb("oh2", [128, 8], F32)
    msk = p.sb("msk", [128, 8], F32)
    exps = p.sb("exps", [128, 8], F32)
    sumc = p.sb("sumc", [128, 1], F32)
    pc1 = p.sb("pc1", [128, 1], F32)
    pc2 = p.sb("pc2", [128, 1], F32)
    iota8 = p.sb("iota8", [128, 8], mybir.dt.int32)
    iota8f = p.sb("iota8f", [128, 8], F32)
    oh1f = p.sb("oh1f", [8, TH], F32)
    oh2f = p.sb("oh2f", [8, TH], F32)
    cum1f = p.sb("cum1f", [8, TH], F32)
    ones8 = p.sb("ones8", [8, TH], F32)
    prod8 = p.sb("prod8", [8, 512], F32)
    rowo = p.sb("rowo", [1, 512], F32)
    pst = p.ps("pst", [128, 8], F32)
    psr = p.ps("psr", [1, 512], F32)
    pstr = p.ps("pstr", [128, 128], F32)

    p.sop("gpsimd", lambda e: e.dma_start(
        out=wr_sb[:, :].rearrange("p (c n) -> p c n", c=NCH),
        in_=wr.rearrange("(c p) n -> p c n", p=128)), dma=True)
    p.sop("sync", lambda e: e.dma_start(out=identf[:, :], in_=ident),
          dma=True)
    p.sop("gpsimd", lambda e: e.iota(iota8[:, :], pattern=[[1, 8]], base=0,
                                     channel_multiplier=0))
    p.sop("vector", lambda e: e.tensor_copy(iota8f[:, :], iota8[:, :]))
    p.sop("vector", lambda e: e.memset(ones8[:, :], 1.0))

    for tt in range(TH // 128):
        p.sop("sync", lambda e, tt=tt: e.dma_start(
            out=hbuf[:, :].rearrange("p (c n) -> p c n", c=NCH),
            in_=h2f.rearrange("(c p) n -> p c n", p=128)[
                :, :, tt * 128:(tt + 1) * 128]), dma=True)
        for ci in range(NCH):
            p.sop("tensor", lambda e, ci=ci: e.matmul(
                pst[:, :], hbuf[:, ci * 128:(ci + 1) * 128],
                wr_sb[:, ci * 8:(ci + 1) * 8],
                start=(ci == 0), stop=(ci == NCH - 1)))
        p.sop("vector", lambda e: e.tensor_copy(lg[:, :], pst[:, :]))
        p.sop("vector", lambda e: e.tensor_reduce(m1c[:, :], lg[:, :], AX.X,
                                                  ALU.max))
        p.sop("vector", lambda e: e.tensor_scalar(
            oh[:, :], lg[:, :], m1c[:, 0:1], None, ALU.is_ge))
        p.sop("vector", lambda e: e.tensor_tensor_scan(
            cum[:, :], oh[:, :], p._ones_f[:, 0:8], 0.0, ALU.add, ALU.mult))
        p.sop("vector", lambda e: e.tensor_scalar(
            oh1[:, :], cum[:, :], 1.0, None, ALU.is_equal))
        p.sop("vector", lambda e: e.tensor_mul(oh1[:, :], oh1[:, :],
                                               oh[:, :]))
        p.sop("vector", lambda e: e.scalar_tensor_tensor(
            msk[:, :], oh1[:, :], -1e30, lg[:, :], ALU.mult, ALU.add))
        p.sop("vector", lambda e: e.tensor_reduce(pc1[:, :], msk[:, :], AX.X,
                                                  ALU.max))
        p.sop("vector", lambda e: e.tensor_scalar(
            oh[:, :], msk[:, :], pc1[:, 0:1], None, ALU.is_ge))
        p.sop("vector", lambda e: e.tensor_tensor_scan(
            cum[:, :], oh[:, :], p._ones_f[:, 0:8], 0.0, ALU.add, ALU.mult))
        p.sop("vector", lambda e: e.tensor_scalar(
            oh2[:, :], cum[:, :], 1.0, None, ALU.is_equal))
        p.sop("vector", lambda e: e.tensor_mul(oh2[:, :], oh2[:, :],
                                               oh[:, :]))
        p.sop("vector", lambda e: e.tensor_scalar_mul(m1c[:, :], m1c[:, :],
                                                      -1.0))
        p.sop("scalar", lambda e: e.activation(
            exps[:, :], lg[:, :], AF.Exp, bias=m1c[:, 0:1],
            accum_out=sumc[:, 0:1]))
        p.sop("vector", lambda e: e.reciprocal(sumc[:, :], sumc[:, :]))
        p.sop("vector", lambda e: e.tensor_mul(msk[:, :], exps[:, :],
                                               oh1[:, :]))
        p.sop("vector", lambda e: e.tensor_reduce(pc1[:, :], msk[:, :], AX.X,
                                                  ALU.add))
        p.sop("vector", lambda e: e.tensor_mul(pc1[:, :], pc1[:, :],
                                               sumc[:, :]))
        p.sop("vector", lambda e: e.tensor_mul(msk[:, :], exps[:, :],
                                               oh2[:, :]))
        p.sop("vector", lambda e: e.tensor_reduce(pc2[:, :], msk[:, :], AX.X,
                                                  ALU.add))
        p.sop("vector", lambda e: e.tensor_mul(pc2[:, :], pc2[:, :],
                                               sumc[:, :]))
        p.sop("vector", lambda e: e.tensor_mul(msk[:, :], oh1[:, :],
                                               iota8f[:, :]))
        p.sop("vector", lambda e: e.tensor_reduce(m1c[:, :], msk[:, :], AX.X,
                                                  ALU.add))
        p.sop("vector", lambda e: e.tensor_mul(exps[:, :], oh2[:, :],
                                               iota8f[:, :]))
        p.sop("vector", lambda e: e.tensor_reduce(sumc[:, :], exps[:, :],
                                                  AX.X, ALU.add))
        for src, dstf in ((oh1, oh1f), (oh2, oh2f)):
            p.sop("tensor", lambda e, src=src: e.transpose(
                pstr[0:8, 0:128], src[:, :], identf[:, :]))
            p.sop("vector", lambda e, dstf=dstf, tt=tt: e.tensor_copy(
                dstf[0:8, tt * 128:(tt + 1) * 128], pstr[0:8, 0:128]))
        for src, dstd in ((pc1, p1d), (pc2, p2d), (m1c, e1d), (sumc, e2d)):
            p.sop("tensor", lambda e, src=src: e.transpose(
                pstr[0:1, 0:128], src[:, :], identf[:, :]))
            p.sop("vector", lambda e: e.tensor_copy(rowo[0:1, 0:128],
                                                    pstr[0:1, 0:128]))
            p.sop("sync", lambda e, dstd=dstd, tt=tt: e.dma_start(
                out=dstd[0:1, tt * 128:(tt + 1) * 128],
                in_=rowo[0:1, 0:128]), dma=True)
    for ohf, sd, row in ((oh1f, s1d, 0), (oh2f, s2d, 1)):
        p.sop("vector", lambda e, ohf=ohf: e.tensor_tensor_scan(
            cum1f[0:8, :], ohf[0:8, :], ones8[0:8, :], 0.0,
            ALU.add, ALU.mult))
        for tb in range(TH // 512):
            p.sop("vector", lambda e, ohf=ohf, tb=tb: e.tensor_mul(
                prod8[0:8, :], ohf[0:8, tb * 512:(tb + 1) * 512],
                cum1f[0:8, tb * 512:(tb + 1) * 512]))
            p.sop("tensor", lambda e: e.matmul(
                psr[0:1, :], p._ones_f[0:8, 0:1], prod8[0:8, :],
                start=True, stop=True))
            p.sop("vector", lambda e: e.tensor_copy(rowo[0:1, :],
                                                    psr[0:1, :]))
            p.sop("sync", lambda e, sd=sd, tb=tb: e.dma_start(
                out=sd[0:1, tb * 512:(tb + 1) * 512], in_=rowo[0:1, 0:512]),
                dma=True)
        p.sop("tensor", lambda e: e.transpose(
            pstr[0:1, 0:8], cum1f[0:8, TH - 1:TH], identf[0:8, 0:8]))
        p.sop("vector", lambda e: e.tensor_copy(rowo[0:1, 0:8],
                                                pstr[0:1, 0:8]))
        p.sop("sync", lambda e, row=row: e.dma_start(
            out=cntd[row:row + 1, 0:8], in_=rowo[0:1, 0:8]), dma=True)
    p.wait("sync", "seq")
    return p.build()



def build_moegates(CAP=8192):
    p = Prog()
    cnt_all = p.dram("cnt_all", [16, 8], F32, kind="ExternalInput")
    pref = p.dram("pref", [8], F32, kind="ExternalInput")
    p1d = p.dram("p1", [1, TH], F32, kind="ExternalInput")
    p2d = p.dram("p2", [1, TH], F32, kind="ExternalInput")
    e1d = p.dram("e1", [1, TH], F32, kind="ExternalInput")
    e2d = p.dram("e2", [1, TH], F32, kind="ExternalInput")
    s1d = p.dram("s1", [1, TH], F32, kind="ExternalInput")
    s2d = p.dram("s2", [1, TH], F32, kind="ExternalInput")
    g1d = p.dram("g1", [1, TH], F32, kind="ExternalOutput")
    g2d = p.dram("g2", [1, TH], F32, kind="ExternalOutput")
    l1d = p.dram("loc1", [1, TH], F32, kind="ExternalOutput")
    l2d = p.dram("loc2", [1, TH], F32, kind="ExternalOutput")

    common_tiles(p)
    cnt_sb = p.sb("cnt_sb", [8, 8], F32)
    cnt2_sb = p.sb("cnt2_sb", [8, 8], F32)
    pref_c = p.sb("pref_c", [8, 1], F32)
    rows = {}
    for nm in ("p1r", "p2r", "e1r", "e2r", "l1r", "l2r", "t1r", "t2r", "mr"):
        rows[nm] = p.sb(nm, [1, TH], F32)
    n1row = p.sb("n1row", [1, 8], F32)
    off1 = p.sb("off1", [1, 8], F32)
    off2 = p.sb("off2", [1, 8], F32)
    psr = p.ps("psr", [1, 8], F32)

    p.sop("sync", lambda e: e.dma_start(out=cnt_sb[:, :],
                                        in_=cnt_all[0:8, :]), dma=True)
    p.sop("sync", lambda e: e.dma_start(out=cnt2_sb[:, :],
                                        in_=cnt_all[8:16, :]), dma=True)
    p.sop("sync", lambda e: e.dma_start(out=pref_c[:, 0:1],
                                        in_=pref[:][:, None]), dma=True)
    for nm, src in (("p1r", p1d), ("p2r", p2d), ("e1r", e1d), ("e2r", e2d),
                    ("l1r", s1d), ("l2r", s2d)):
        p.sop("sync", lambda e, nm=nm, src=src: e.dma_start(
            out=rows[nm][0:1, :], in_=src[0:1, :]), dma=True)
    p.sop("tensor", lambda e: e.matmul(psr[0:1, :], p._ones_f[0:8, 0:1],
                                       cnt_sb[0:8, :], start=True, stop=True))
    p.sop("vector", lambda e: e.tensor_copy(n1row[0:1, :], psr[0:1, :]))
    p.sop("tensor", lambda e: e.matmul(psr[0:1, :], pref_c[0:8, 0:1],
                                       cnt_sb[0:8, :], start=True, stop=True))
    p.sop("vector", lambda e: e.tensor_copy(off1[0:1, :], psr[0:1, :]))
    p.sop("tensor", lambda e: e.matmul(psr[0:1, :], pref_c[0:8, 0:1],
                                       cnt2_sb[0:8, :], start=True, stop=True))
    p.sop("vector", lambda e: e.tensor_copy(off2[0:1, :], psr[0:1, :]))
    p.sop("vector", lambda e: e.tensor_add(off2[0:1, :], off2[0:1, :],
                                           n1row[0:1, :]))
    for lr, er, offr in (("l1r", "e1r", off1), ("l2r", "e2r", off2)):
        p.sop("vector", lambda e, lr=lr: e.tensor_scalar_add(
            rows[lr][0:1, :], rows[lr][0:1, :], -1.0))
        for ex in range(8):
            p.sop("vector", lambda e, er=er, ex=ex: e.tensor_scalar(
                rows["mr"][0:1, :], rows[er][0:1, :], float(ex), None,
                ALU.is_equal))
            p.sop("vector", lambda e, lr=lr, offr=offr, ex=ex:
                  e.scalar_tensor_tensor(rows[lr][0:1, :], rows["mr"][0:1, :],
                                         offr[0:1, ex:ex + 1],
                                         rows[lr][0:1, :], ALU.mult, ALU.add))
    p.sop("vector", lambda e: e.tensor_scalar(
        rows["t1r"][0:1, :], rows["l1r"][0:1, :], float(CAP), None,
        ALU.is_lt))
    p.sop("vector", lambda e: e.tensor_scalar(
        rows["t2r"][0:1, :], rows["l2r"][0:1, :], float(CAP), None,
        ALU.is_lt))
    p.sop("vector", lambda e: e.tensor_mul(rows["t1r"][0:1, :],
                                           rows["t1r"][0:1, :],
                                           rows["p1r"][0:1, :]))
    p.sop("vector", lambda e: e.tensor_mul(rows["t2r"][0:1, :],
                                           rows["t2r"][0:1, :],
                                           rows["p2r"][0:1, :]))
    p.sop("vector", lambda e: e.tensor_add(rows["mr"][0:1, :],
                                           rows["t1r"][0:1, :],
                                           rows["t2r"][0:1, :]))
    p.sop("vector", lambda e: e.tensor_scalar_max(rows["mr"][0:1, :],
                                                  rows["mr"][0:1, :],
                                                  1.1920929e-07))
    p.sop("vector", lambda e: e.reciprocal(rows["mr"][0:1, :],
                                           rows["mr"][0:1, :]))
    p.sop("vector", lambda e: e.tensor_mul(rows["t1r"][0:1, :],
                                           rows["t1r"][0:1, :],
                                           rows["mr"][0:1, :]))
    p.sop("vector", lambda e: e.tensor_mul(rows["t2r"][0:1, :],
                                           rows["t2r"][0:1, :],
                                           rows["mr"][0:1, :]))
    for src, dst in (("t1r", g1d), ("t2r", g2d), ("l1r", l1d), ("l2r", l2d)):
        p.sop("sync", lambda e, src=src, dst=dst: e.dma_start(
            out=dst[0:1, :], in_=rows[src][0:1, :]), dma=True)
    p.wait("sync", "seq")
    return p.build()



def build_expert(RC=8192, dff=4096):
    p = Prog()
    Xe = p.dram("Xe", [D, RC], BF16, kind="ExternalInput")
    w1 = p.dram("w1", [D, dff], F32, kind="ExternalInput")
    b1 = p.dram("b1", [dff], F32, kind="ExternalInput")
    w2 = p.dram("w2", [dff, D], F32, kind="ExternalInput")
    b2 = p.dram("b2", [D], F32, kind="ExternalInput")
    Ye = p.dram("Ye", [D, RC], BF16, kind="ExternalOutput")

    nh = dff // 128
    NTB = RC // 512
    NG = nh + NCH
    common_tiles(p)
    w1_sb = p.sb("w1_sb", [128, NCH * dff], BF16)
    w2_sb = p.sb("w2_sb", [128, nh * D], BF16)
    b1_c = p.sb("b1_c", [128, nh], F32)
    b2_c = p.sb("b2_c", [128, NCH], F32)
    hb = p.sb("hb", [128, nh * 512], BF16)
    xb_ = [p.sb(f"xb{i}", [128, NCH * 512], BF16) for i in range(2)]
    t2 = [p.sb(f"t2_{i}", [128, 512], BF16) for i in range(2)]
    ps = [p.ps(f"ps{i}", [128, 512], F32) for i in range(2)]

    def W(*pairs):
        return [(n, v) for (n, v) in pairs if v > 0] or None

    p.op("gpsimd", lambda e: e.dma_start(
        out=w1_sb[:, :].rearrange("p (c n) -> p c n", c=NCH),
        in_=w1.rearrange("(c p) n -> p c n", p=128)), inc="wl", incn=16)
    p.op("gpsimd", lambda e: e.dma_start(
        out=w2_sb[:, :].rearrange("p (c n) -> p c n", c=nh),
        in_=w2.rearrange("(c p) n -> p c n", p=128)), inc="wl", incn=16)
    p.op("gpsimd", lambda e: e.dma_start(
        out=b1_c[:, 0:nh], in_=b1.rearrange("(c p) -> p c", p=128),
        allow_slow_non_contiguous=True), inc="wl", incn=16)
    p.op("gpsimd", lambda e: e.dma_start(
        out=b2_c[:, 0:NCH], in_=b2.rearrange("(c p) -> p c", p=128),
        allow_slow_non_contiguous=True), inc="wl", incn=16)
    p.op("vector", lambda e: e.tensor_scalar_mul(b2_c[:, :], b2_c[:, :],
                                                 0.8), inc="bs",
         wait=W(("wl", 64)))
    for tb in range(NTB):
        p.op("sync", lambda e, tb=tb: e.dma_start(
            out=xb_[tb % 2][:, :].rearrange("p (c n) -> p c n", c=NCH),
            in_=Xe.rearrange("(c p) n -> p c n", p=128)[
                :, :, tb * 512:(tb + 1) * 512]),
            inc="xl%d" % (tb % 2), incn=16,
            wait=W(("mg", (tb - 2) * NG + nh)) if tb >= 2 else None)
    for tb in range(NTB):
        base = tb * NG
        for co in range(nh):
            g = base + co
            for ci in range(NCH):
                w_ = [("ae", g - 1)]
                if co == 0 and ci == 0:
                    w_.append(("xl%d" % (tb % 2), 16 * (tb // 2 + 1)))
                    if tb == 0:
                        w_.append(("wl", 64))
                p.op("tensor", lambda e, tb=tb, co=co, ci=ci, g=g: e.matmul(
                    ps[g % 2][:, :],
                    w1_sb[:, ci * dff + co * 128:ci * dff + co * 128 + 128],
                    xb_[tb % 2][:, ci * 512:(ci + 1) * 512],
                    start=(ci == 0), stop=(ci == NCH - 1),
                    skip_group_check=True),
                    inc="mg" if ci == NCH - 1 else None,
                    wait=W(*w_) if ci == 0 else None)
            p.op("scalar", lambda e, co=co, g=g: e.activation(
                hb[:, co * 512:(co + 1) * 512], ps[g % 2][:, :], AF.Relu,
                bias=b1_c[:, co:co + 1]), inc="ae", wait=W(("mg", g + 1)))
        for co in range(NCH):
            g = base + nh + co
            o = tb * NCH + co
            for ci in range(nh):
                w_ = [("ae", g if co == 0 else g - 1)]
                p.op("tensor", lambda e, co=co, ci=ci, g=g: e.matmul(
                    ps[g % 2][:, :],
                    w2_sb[:, ci * D + co * 128:ci * D + co * 128 + 128],
                    hb[:, ci * 512:(ci + 1) * 512],
                    start=(ci == 0), stop=(ci == nh - 1),
                    skip_group_check=True),
                    inc="mg" if ci == nh - 1 else None,
                    wait=W(*w_) if ci == 0 else None)
            aw = [("mg", g + 1), ("st%d" % (o % 2), 16 * (o // 2))]
            if o == 0:
                aw.append(("bs", 1))
            p.op("scalar", lambda e, co=co, g=g, o=o: e.activation(
                t2[o % 2][:, :], ps[g % 2][:, :], AF.Identity,
                bias=b2_c[:, co:co + 1], scale=0.8), inc="ae", wait=W(*aw))
            p.op("gpsimd", lambda e, co=co, tb=tb, o=o, g=g: e.dma_start(
                out=Ye[co * 128:(co + 1) * 128, tb * 512:tb * 512 + 512],
                in_=t2[o % 2][:, :]), inc="st%d" % (o % 2), incn=16,
                wait=W(("ae", g + 1)))
    p.wait("gpsimd", "st0", 16 * (NTB * NCH // 2))
    p.wait("gpsimd", "st1", 16 * (NTB * NCH // 2))
    return p.build()



def build_combine():
    p = Prog()
    xT = p.dram("xT", [D, TH], F32, kind="ExternalInput")
    a0 = p.dram("a0", [D, TH], BF16, kind="ExternalInput")
    xf = p.dram("xf", [D, TH], BF16, kind="ExternalInput")
    y1 = p.dram("y1", [D, TH], BF16, kind="ExternalInput")
    y2 = p.dram("y2", [D, TH], BF16, kind="ExternalInput")
    g1 = p.dram("g1", [1, TH], F32, kind="ExternalInput")
    g2 = p.dram("g2", [1, TH], F32, kind="ExternalInput")
    x1d = p.dram("x1", [D, TH], F32, kind="ExternalOutput")
    xn1d = p.dram("xn1", [D, TH], F32, kind="ExternalOutput")
    vy1d = p.dram("vy1", [1, TH], F32, kind="ExternalOutput")

    common_tiles(p)
    bufs = {}
    for nm, dt_ in (("bxT", F32), ("ba0", BF16), ("bxf", BF16),
                    ("by1", BF16), ("by2", BF16)):
        bufs[nm] = p.sb(nm, [128, NCH * 512], dt_)
    g1r = p.sb("g1r", [1, TH], F32)
    g2r = p.sb("g2r", [1, TH], F32)
    sum_row = p.sb("sum_row", [1, TH], F32)
    sq_row = p.sb("sq_row", [1, TH], F32)
    rstd_row = p.sb("rstd_row", [1, TH], F32)
    negms_row = p.sb("negms_row", [1, TH], F32)
    t1 = p.sb("t1", [128, 512], F32)
    t2 = p.sb("t2", [128, 512], F32)
    sqt = p.sb("sqt", [128, 512], F32)
    psA = p.ps("psA", [128, 512], F32)
    psB = p.ps("psB", [128, 512], F32)
    psS1 = p.ps("psS1", [1, 512], F32)
    psS2 = p.ps("psS2", [1, 512], F32)

    p.sop("sync", lambda e: e.dma_start(out=g1r[0:1, :], in_=g1[0:1, :]),
          dma=True)
    p.sop("sync", lambda e: e.dma_start(out=g2r[0:1, :], in_=g2[0:1, :]),
          dma=True)
    srcs = {"bxT": xT, "ba0": a0, "bxf": xf, "by1": y1, "by2": y2}
    for tb in range(TH // 512):
        for nm, src in srcs.items():
            p.sop("sync", lambda e, nm=nm, src=src, tb=tb: e.dma_start(
                out=bufs[nm][:, :].rearrange("p (c n) -> p c n", c=NCH),
                in_=src.rearrange("(c p) n -> p c n", p=128)[
                    :, :, tb * 512:(tb + 1) * 512]), dma=True)
        p.sop("tensor", lambda e, tb=tb: e.matmul(
            psA[:, :], p._onesrow_f[0:1, 0:128],
            g1r[0:1, tb * 512:(tb + 1) * 512], start=True, stop=True))
        p.sop("tensor", lambda e, tb=tb: e.matmul(
            psB[:, :], p._onesrow_f[0:1, 0:128],
            g2r[0:1, tb * 512:(tb + 1) * 512], start=True, stop=True))
        for ci in range(NCH):
            p.sop("vector", lambda e, ci=ci: e.tensor_mul(
                t1[:, :], bufs["by1"][:, ci * 512:(ci + 1) * 512], psA[:, :]))
            p.sop("vector", lambda e, ci=ci: e.tensor_mul(
                t2[:, :], bufs["by2"][:, ci * 512:(ci + 1) * 512], psB[:, :]))
            p.sop("vector", lambda e: e.tensor_add(t1[:, :], t1[:, :],
                                                   t2[:, :]))
            p.sop("vector", lambda e, ci=ci: e.tensor_add(
                t1[:, :], t1[:, :], bufs["bxf"][:, ci * 512:(ci + 1) * 512]))
            p.sop("vector", lambda e, ci=ci: e.tensor_add(
                t1[:, :], t1[:, :], bufs["ba0"][:, ci * 512:(ci + 1) * 512]))
            p.sop("vector", lambda e, ci=ci: e.tensor_add(
                t1[:, :], t1[:, :], bufs["bxT"][:, ci * 512:(ci + 1) * 512]))
            p.sop("sync", lambda e, ci=ci, tb=tb: e.dma_start(
                out=x1d[ci * 128:(ci + 1) * 128, tb * 512:tb * 512 + 512],
                in_=t1[:, :]), dma=True)
            p.sop("scalar", lambda e: e.activation(sqt[:, :], t1[:, :],
                                                   AF.Square))
            p.sop("tensor", lambda e, ci=ci: e.matmul(
                psS1[0:1, :], p._ones_f[:, 0:1], t1[:, :],
                start=(ci == 0), stop=(ci == NCH - 1)))
            p.sop("tensor", lambda e, ci=ci: e.matmul(
                psS2[0:1, :], p._ones_f[:, 0:1], sqt[:, :],
                start=(ci == 0), stop=(ci == NCH - 1)))
        p.sop("vector", lambda e, tb=tb: e.tensor_copy(
            sum_row[0:1, tb * 512:(tb + 1) * 512], psS1[0:1, :]))
        p.sop("vector", lambda e, tb=tb: e.tensor_copy(
            sq_row[0:1, tb * 512:(tb + 1) * 512], psS2[0:1, :]))
    srows_meanvar_rstd(p, sum_row, sq_row, TH, D, rstd_row, negms_row)
    p.sop("vector", lambda e: e.tensor_mul(p._rtmp[0:1, :], sq_row[0:1, :],
                                           rstd_row[0:1, :]))
    p.sop("vector", lambda e: e.tensor_mul(p._rtmp[0:1, :], p._rtmp[0:1, :],
                                           rstd_row[0:1, :]))
    p.sop("sync", lambda e: e.dma_start(out=vy1d[0:1, :],
                                        in_=p._rtmp[0:1, :]), dma=True)
    for tb in range(TH // 512):
        p.sop("sync", lambda e, tb=tb: e.dma_start(
            out=bufs["bxT"][:, :].rearrange("p (c n) -> p c n", c=NCH),
            in_=x1d.rearrange("(c p) n -> p c n", p=128)[
                :, :, tb * 512:(tb + 1) * 512]), dma=True)
        p.sop("tensor", lambda e, tb=tb: e.matmul(
            psA[:, :], p._onesrow_f[0:1, 0:128],
            rstd_row[0:1, tb * 512:(tb + 1) * 512], start=True, stop=True))
        p.sop("tensor", lambda e, tb=tb: e.matmul(
            psB[:, :], p._onesrow_f[0:1, 0:128],
            negms_row[0:1, tb * 512:(tb + 1) * 512], start=True, stop=True))
        for ci in range(NCH):
            p.sop("vector", lambda e, ci=ci: e.tensor_mul(
                t1[:, :], bufs["bxT"][:, ci * 512:(ci + 1) * 512], psA[:, :]))
            p.sop("vector", lambda e: e.tensor_add(t1[:, :], t1[:, :],
                                                   psB[:, :]))
            p.sop("sync", lambda e, ci=ci, tb=tb: e.dma_start(
                out=xn1d[ci * 128:(ci + 1) * 128, tb * 512:tb * 512 + 512],
                in_=t1[:, :]), dma=True)
    p.wait("sync", "seq")
    return p.build()



def build_mpre():
    p = Prog()
    DIP = 4384
    xn1 = p.dram("xn1", [D, TH], F32, kind="ExternalInput")
    vy1 = p.dram("vy1", [1, TH], F32, kind="ExternalInput")
    inwT = p.dram("in_wT", [D, DIP], F32, kind="ExternalInput")
    normw = p.dram("norm_w", [D], F32, kind="ExternalInput")
    dtb = p.dram("dt_bias", [32], F32, kind="ExternalInput")
    alog = p.dram("A_log", [32], F32, kind="ExternalInput")
    zd = p.dram("z", [2048, TH], BF16, kind="ExternalOutput")
    xbcd = p.dram("xbc", [2304, TH], BF16, kind="ExternalOutput")
    dtd = p.dram("dtr", [32, TH], F32, kind="ExternalOutput")
    lld = p.dram("llm", [32, TH], F32, kind="ExternalOutput")

    common_tiles(p)
    w_sb = p.sb("w_sb", [128, NCH * DIP], BF16)
    nw_r = p.sb("nw_r", [1, D], F32)
    zero_r = p.sb("zero_r", [1, D], F32)
    vy_r = p.sb("vy_r", [1, TH], F32)
    dtb_c = p.sb("dtb_c", [32, 1], F32)
    negA_c = p.sb("negA_c", [32, 1], F32)
    xr = p.sb("xr", [128, NCH * TH], BF16)
    xbuf = p.sb("xbuf", [128, NCH * 512], F32)
    t1 = p.sb("t1", [128, 512], F32)
    t2 = p.sb("t2", [128, 512], BF16)
    v1 = p.sb("v1", [32, 512], F32)
    v2 = p.sb("v2", [32, 512], F32)
    v3 = p.sb("v3", [32, 512], F32)
    ps = p.ps("ps", [128, 512], F32)
    psA = p.ps("psA", [128, 512], F32)
    psB = p.ps("psB", [128, 512], F32)

    p.sop("gpsimd", lambda e: e.dma_start(
        out=w_sb[:, :].rearrange("p (c n) -> p c n", c=NCH),
        in_=inwT.rearrange("(c p) n -> p c n", p=128)), dma=True)
    p.sop("sync", lambda e: e.dma_start(out=nw_r[0:1, 0:D],
                                        in_=normw[None, :]), dma=True)
    p.sop("sync", lambda e: e.dma_start(out=vy_r[0:1, :], in_=vy1[0:1, :]),
          dma=True)
    p.sop("sync", lambda e: e.dma_start(out=dtb_c[:, 0:1],
                                        in_=dtb[:][:, None]), dma=True)
    p.sop("sync", lambda e: e.dma_start(out=negA_c[:, 0:1],
                                        in_=alog[:][:, None]), dma=True)
    p.sop("vector", lambda e: e.memset(zero_r[:, :], 0.0))
    p.sop("scalar", lambda e: e.activation(negA_c[:, :], negA_c[:, :],
                                           AF.Exp))
    p.sop("vector", lambda e: e.tensor_scalar_mul(negA_c[:, :],
                                                  negA_c[:, :], -1.0))
    p.sop("scalar", lambda e: e.activation(p._rtmp[0:1, 0:TH],
                                           vy_r[0:1, 0:TH], AF.Sqrt,
                                           bias=p._eps_col[0:1, 0:1]))
    p.sop("vector", lambda e: e.reciprocal(vy_r[0:1, 0:TH],
                                           p._rtmp[0:1, 0:TH]))
    for tb in range(TH // 512):
        p.sop("sync", lambda e, tb=tb: e.dma_start(
            out=xbuf[:, :].rearrange("p (c n) -> p c n", c=NCH),
            in_=xn1.rearrange("(c p) n -> p c n", p=128)[
                :, :, tb * 512:(tb + 1) * 512]), dma=True)
        for ci in range(NCH):
            p.sop("tensor", lambda e, ci=ci, tb=tb: e.matmul(
                psA[:, :], nw_r[0:1, ci * 128:(ci + 1) * 128],
                vy_r[0:1, tb * 512:(tb + 1) * 512], start=True, stop=True))
            p.sop("vector", lambda e, ci=ci, tb=tb: e.tensor_mul(
                xr[:, ci * TH + tb * 512:ci * TH + tb * 512 + 512],
                xbuf[:, ci * 512:(ci + 1) * 512], psA[:, :]))
    NCO = DIP // 128 + 1
    for co in range(35):
        cod = 128 if co < 34 else 32
        for tb in range(TH // 512):
            for ci in range(NCH):
                p.sop("tensor", lambda e, co=co, ci=ci, tb=tb, cod=cod:
                      e.matmul(ps[0:cod, :],
                               w_sb[:, ci * DIP + co * 128:
                                    ci * DIP + co * 128 + cod],
                               xr[:, ci * TH + tb * 512:ci * TH + tb * 512 + 512],
                               start=(ci == 0), stop=(ci == NCH - 1)))
            if co < 16:
                p.sop("scalar", lambda e: e.activation(t2[:, :], ps[:, :],
                                                       AF.Copy))
                p.sop("sync", lambda e, co=co, tb=tb: e.dma_start(
                    out=zd[co * 128:(co + 1) * 128, tb * 512:tb * 512 + 512],
                    in_=t2[:, :]), dma=True)
            elif co < 34:
                p.sop("scalar", lambda e: e.activation(t2[:, :], ps[:, :],
                                                       AF.Copy))
                p.sop("sync", lambda e, co=co, tb=tb: e.dma_start(
                    out=xbcd[(co - 16) * 128:(co - 15) * 128,
                             tb * 512:tb * 512 + 512],
                    in_=t2[:, :]), dma=True)
            else:
                p.sop("scalar", lambda e: e.activation(
                    v1[:, :], ps[0:32, :], AF.Identity, bias=dtb_c[0:32, 0:1]))
                p.sop("scalar", lambda e: e.activation(v2[:, :], v1[:, :],
                                                       AF.Abs))
                p.sop("scalar", lambda e: e.activation(v2[:, :], v2[:, :],
                                                       AF.Exp, scale=-1.0))
                p.sop("scalar", lambda e: e.activation(
                    v2[:, :], v2[:, :], AF.Ln, bias=p._ones_f[0:32, 0:1]))
                p.sop("vector", lambda e: e.tensor_scalar_max(v3[:, :],
                                                              v1[:, :], 0.0))
                p.sop("vector", lambda e: e.tensor_add(v3[:, :], v3[:, :],
                                                       v2[:, :]))
                p.sop("sync", lambda e, tb=tb: e.dma_start(
                    out=dtd[0:32, tb * 512:tb * 512 + 512], in_=v3[:, :]),
                    dma=True)
                p.sop("vector", lambda e: e.tensor_scalar_mul(
                    v3[:, :], v3[:, :], negA_c[0:32, 0:1]))
                p.sop("sync", lambda e, tb=tb: e.dma_start(
                    out=lld[0:32, tb * 512:tb * 512 + 512], in_=v3[:, :]),
                    dma=True)
    p.wait("sync", "seq")
    return p.build()



def build_mconv():
    p = Prog()
    xbcd = p.dram("xbc", [2304, TH], BF16, kind="ExternalInput")
    halo = p.dram("halo", [2304, 3], BF16, kind="ExternalInput")
    convw = p.dram("conv_w", [2304, 4], F32, kind="ExternalInput")
    convb = p.dram("conv_b", [2304], F32, kind="ExternalInput")
    dtd = p.dram("dtr", [32, TH], F32, kind="ExternalInput")
    ident = p.dram("ident", [128, 128], F32, kind="ExternalInput")
    xsd = p.dram("xsf", [2048, TH], BF16, kind="ExternalOutput")
    dtxd = p.dram("dtx", [TH, 2048], BF16, kind="ExternalOutput")
    Bd = p.dram("Bf", [128, TH], BF16, kind="ExternalOutput")
    Cd = p.dram("Cf", [128, TH], BF16, kind="ExternalOutput")

    common_tiles(p)
    identf = p.sb("identf", [128, 128], F32)
    cw_c = p.sb("cw_c", [128, 18 * 4], F32)
    cb_c = p.sb("cb_c", [128, 18], F32)
    xext = p.sb("xext", [128, 515], BF16)
    acc = p.sb("acc", [128, 512], F32)
    sil = p.sb("sil", [128, 512], BF16)
    dts = p.sb("dts", [32, 512], F32)
    dtt = p.sb("dtt", [128, 4 * 32], F32)
    ttok = p.sb("ttok", [128, 128], BF16)
    dtx_t = p.sb("dtx_t", [128, 128], BF16)
    pst = p.ps("pst", [128, 128], BF16)
    pst2 = p.ps("pst2", [128, 32], F32)

    p.sop("sync", lambda e: e.dma_start(out=identf[:, :], in_=ident),
          dma=True)
    p.sop("gpsimd", lambda e: e.dma_start(
        out=cw_c[:, :].rearrange("p (c t) -> p c t", c=18),
        in_=convw.rearrange("(c p) t -> p c t", p=128)), dma=True)
    p.sop("gpsimd", lambda e: e.dma_start(
        out=cb_c[:, 0:18], in_=convb.rearrange("(c p) -> p c", p=128),
        allow_slow_non_contiguous=True), dma=True)
    identb2 = p._identb
    p.sop("vector", lambda e: e.tensor_copy(identb2[:, :], identf[:, :]))

    for tb in range(TH // 512):
        p.sop("sync", lambda e, tb=tb: e.dma_start(
            out=dts[:, :], in_=dtd[0:32, tb * 512:(tb + 1) * 512]), dma=True)
        for sub in range(4):
            p.sop("tensor", lambda e, sub=sub: e.transpose(
                pst2[:, 0:32], dts[:, sub * 128:(sub + 1) * 128],
                identf[0:32, 0:32]))
            p.sop("vector", lambda e, sub=sub: e.tensor_copy(
                dtt[:, sub * 32:(sub + 1) * 32], pst2[:, 0:32]))
        for c in range(18):
            if tb == 0:
                p.sop("sync", lambda e, c=c: e.dma_start(
                    out=xext[:, 0:3], in_=halo[c * 128:(c + 1) * 128, :]),
                    dma=True)
            else:
                p.sop("sync", lambda e, c=c, tb=tb: e.dma_start(
                    out=xext[:, 0:3],
                    in_=xbcd[c * 128:(c + 1) * 128,
                             tb * 512 - 3:tb * 512]), dma=True)
            p.sop("sync", lambda e, c=c, tb=tb: e.dma_start(
                out=xext[:, 3:515],
                in_=xbcd[c * 128:(c + 1) * 128, tb * 512:(tb + 1) * 512]),
                dma=True)
            p.sop("vector", lambda e, c=c: e.tensor_scalar_mul(
                acc[:, :], xext[:, 0:512], cw_c[:, 4 * c:4 * c + 1]))
            for d in range(1, 4):
                p.sop("vector", lambda e, c=c, d=d: e.scalar_tensor_tensor(
                    acc[:, :], xext[:, d:512 + d], cw_c[:, 4 * c + d:4 * c + d + 1],
                    acc[:, :], ALU.mult, ALU.add))
            p.sop("scalar", lambda e, c=c: e.activation(
                sil[:, :], acc[:, :], AF.Silu, bias=cb_c[:, c:c + 1]))
            if c < 16:
                p.sop("sync", lambda e, c=c, tb=tb: e.dma_start(
                    out=xsd[c * 128:(c + 1) * 128, tb * 512:(tb + 1) * 512],
                    in_=sil[:, :]), dma=True)
                for sub in range(4):
                    p.sop("tensor", lambda e, sub=sub: e.transpose(
                        pst[:, :], sil[:, sub * 128:(sub + 1) * 128],
                        identb2[:, :]))
                    p.sop("vector", lambda e: e.tensor_copy(ttok[:, :],
                                                            pst[:, :]))
                    p.sop("vector", lambda e, c=c, sub=sub: e.tensor_scalar(
                        dtx_t[:, 0:64], ttok[:, 0:64],
                        dtt[:, sub * 32 + 2 * c:sub * 32 + 2 * c + 1], None,
                        ALU.mult))
                    p.sop("vector", lambda e, c=c, sub=sub: e.tensor_scalar(
                        dtx_t[:, 64:128], ttok[:, 64:128],
                        dtt[:, sub * 32 + 2 * c + 1:sub * 32 + 2 * c + 2],
                        None, ALU.mult))
                    p.sop("sync", lambda e, c=c, tb=tb, sub=sub: e.dma_start(
                        out=dtxd[tb * 512 + sub * 128:tb * 512 + sub * 128 + 128,
                                 c * 128:(c + 1) * 128],
                        in_=dtx_t[:, :]), dma=True)
            elif c == 16:
                p.sop("sync", lambda e, tb=tb: e.dma_start(
                    out=Bd[:, tb * 512:(tb + 1) * 512], in_=sil[:, :]),
                    dma=True)
            else:
                p.sop("sync", lambda e, tb=tb: e.dma_start(
                    out=Cd[:, tb * 512:(tb + 1) * 512], in_=sil[:, :]),
                    dma=True)
    p.wait("sync", "seq")
    return p.build()



def build_mamba(NH=16, TT=T, CH=128):
    p = Prog()
    nchk = TT // CH
    dtxd = p.dram("dtx", [TT, NH * 64], BF16, kind="ExternalInput")
    Bd = p.dram("Bf", [128, TT], BF16, kind="ExternalInput")
    Cd = p.dram("Cf", [128, TT], BF16, kind="ExternalInput")
    lld = p.dram("llm", [NH, TT], F32, kind="ExternalInput")
    maskL = p.dram("maskL", [128, 128], F32, kind="ExternalInput")
    ident = p.dram("ident", [128, 128], F32, kind="ExternalInput")
    yd = p.dram("yssm", [NH * 64, TT], BF16, kind="ExternalOutput")

    common_tiles(p)
    identf = p.sb("identf", [128, 128], F32)
    mask_sb = p.sb("mask_sb", [128, 128], F32)
    Bsb = p.sb("Bsb", [128, TT], BF16)
    Csb = p.sb("Csb", [128, TT], BF16)
    llsb = p.sb("llsb", [NH, TT], F32)
    dtxc = p.sb("dtxc", [128, NH * 64], BF16)
    llc = p.sb("llc", [1, CH], F32)
    llcall = p.sb("llcall", [NH, CH], F32)
    ones16 = p.sb("ones16", [NH, CH], F32)
    ltT = p.sb("ltT", [128, NH], F32)
    nltT = p.sb("nltT", [128, NH], F32)
    lc128 = p.sb("lc128", [128, 1], F32)
    wcol = p.sb("wcol", [128, 1], F32)
    ecall = p.sb("ecall", [128, 1], F32)
    bcsb = p.sb("bcsb", [128, 128], F32)
    e1in = p.sb("e1in", [128, 128], F32)
    Esb = p.sb("Esb", [128, 128], BF16)
    CBs = p.sb("CBs", [128, 128], BF16)
    Btok = p.sb("Btok", [128, 128], BF16)
    Asb = p.sb("Asb", [128, 128], BF16)
    Cw = p.sb("Cw", [128, 128], BF16)
    expb = p.sb("expb", [128, 128], BF16)
    dtxw = p.sb("dtxw", [128, 64], BF16)
    hts = p.sb("hts", [128, NH * 64], F32)
    htb = p.sb("htb", [128, 64], BF16)
    oy = p.sb("oy", [64, 128], BF16)
    psCB = p.ps("psCB", [128, 128], F32)
    psBT = p.ps("psBT", [128, 128], BF16)
    psb = p.ps("psb", [128, 128], F32)
    ps2c = p.ps("ps2c", [128, 2], F32)
    psY = p.ps("psY", [64, 128], F32)
    psH = p.ps("psH", [128, 64], F32)
    psLR = p.ps("psLR", [1, 128], F32)
    psLT = p.ps("psLT", [128, 16], F32)

    p.sop("sync", lambda e: e.dma_start(out=identf[:, :], in_=ident),
          dma=True)
    p.sop("sync", lambda e: e.dma_start(out=mask_sb[:, :], in_=maskL),
          dma=True)
    p.sop("vector", lambda e: e.tensor_copy(p._identb[:, :], identf[:, :]))
    p.sop("sync", lambda e: e.dma_start(out=Bsb[:, :], in_=Bd), dma=True)
    p.sop("sync", lambda e: e.dma_start(out=Csb[:, :], in_=Cd), dma=True)
    p.sop("sync", lambda e: e.dma_start(out=llsb[:, :], in_=lld), dma=True)
    p.sop("vector", lambda e: e.memset(hts[:, :], 0.0))
    p.sop("vector", lambda e: e.memset(ones16[:, :], 1.0))

    for ch in range(nchk):
        p.sop("sync", lambda e, ch=ch: e.dma_start(
            out=dtxc[:, :], in_=dtxd[ch * 128:(ch + 1) * 128, :]), dma=True)
        p.sop("vector", lambda e, ch=ch: e.tensor_tensor_scan(
            llcall[0:NH, :], llsb[0:NH, ch * CH:(ch + 1) * CH],
            ones16[0:NH, :], 0.0, ALU.add, ALU.mult))
        p.sop("tensor", lambda e: e.transpose(
            psLT[:, 0:NH], llcall[0:NH, :], identf[0:NH, 0:NH]))
        p.sop("vector", lambda e: e.tensor_copy(ltT[:, :], psLT[:, 0:NH]))
        p.sop("vector", lambda e: e.tensor_scalar_mul(nltT[:, :], ltT[:, :],
                                                      -1.0))
        p.sop("tensor", lambda e, ch=ch: e.matmul(
            psCB[:, :], Bsb[:, ch * CH:(ch + 1) * CH],
            Csb[:, ch * CH:(ch + 1) * CH], start=True, stop=True))
        p.sop("vector", lambda e: e.tensor_copy(CBs[:, :], psCB[:, :]))
        p.sop("tensor", lambda e, ch=ch: e.transpose(
            psBT[:, :], Bsb[:, ch * CH:(ch + 1) * CH], p._identb[:, :]))
        p.sop("vector", lambda e: e.tensor_copy(Btok[:, :], psBT[:, :]))
        for h in range(NH):
            p.sop("tensor", lambda e, h=h: e.transpose(
                psLR[0:1, :], ltT[:, h:h + 1], identf[:, :]))
            p.sop("vector", lambda e: e.tensor_copy(llc[0:1, :],
                                                    psLR[0:1, :]))
            p.sop("tensor", lambda e: e.matmul(
                psb[:, :], p._onesrow_f[0:1, 0:128], llc[0:1, :],
                start=True, stop=True))
            p.sop("vector", lambda e: e.tensor_copy(bcsb[:, :], psb[:, :]))
            p.sop("tensor", lambda e: e.matmul(
                ps2c[:, 1:2], p._onesrow_f[0:1, 0:128], llc[0:1, 127:128],
                start=True, stop=True, skip_group_check=True))
            p.sop("vector", lambda e: e.tensor_copy(lc128[:, :],
                                                    ps2c[:, 1:2]))
            p.sop("scalar", lambda e, h=h: e.activation(
                wcol[:, :], ltT[:, h:h + 1], AF.Exp, bias=lc128[:, 0:1],
                scale=-1.0))
            p.sop("scalar", lambda e: e.activation(ecall[:, :], lc128[:, :],
                                                   AF.Exp))
            p.sop("vector", lambda e: e.tensor_add(e1in[:, :], bcsb[:, :],
                                                   mask_sb[:, :]))
            p.sop("scalar", lambda e, h=h: e.activation(
                Esb[:, :], e1in[:, :], AF.Exp, bias=nltT[:, h:h + 1]))
            p.sop("scalar", lambda e: e.activation(expb[:, :], bcsb[:, :],
                                                   AF.Exp))
            p.sop("vector", lambda e: e.tensor_mul(Asb[:, :], CBs[:, :],
                                                   Esb[:, :]))
            p.sop("vector", lambda e, ch=ch: e.tensor_mul(
                Cw[:, :], Csb[:, ch * CH:(ch + 1) * CH], expb[:, :]))
            p.sop("vector", lambda e, h=h: e.tensor_copy(
                htb[:, :], hts[:, h * 64:(h + 1) * 64]))
            p.sop("tensor", lambda e, h=h: e.matmul(
                psY[:, :], dtxc[:, h * 64:(h + 1) * 64], Asb[:, :],
                start=True, stop=False, skip_group_check=True))
            p.sop("tensor", lambda e: e.matmul(
                psY[:, :], htb[:, :], Cw[:, :],
                start=False, stop=True, skip_group_check=True))
            p.sop("vector", lambda e: e.tensor_copy(oy[:, :], psY[:, :]))
            p.sop("sync", lambda e, h=h, ch=ch: e.dma_start(
                out=yd[h * 64:(h + 1) * 64, ch * CH:(ch + 1) * CH],
                in_=oy[:, :]), dma=True)
            p.sop("vector", lambda e, h=h: e.tensor_scalar(
                dtxw[:, :], dtxc[:, h * 64:(h + 1) * 64], wcol[:, 0:1],
                None, ALU.mult))
            p.sop("tensor", lambda e: e.matmul(
                psH[:, :], Btok[:, :], dtxw[:, :], start=True, stop=True))
            p.sop("vector", lambda e, h=h: e.tensor_scalar(
                hts[:, h * 64:(h + 1) * 64], hts[:, h * 64:(h + 1) * 64],
                ecall[:, 0:1], None, ALU.mult))
            p.sop("vector", lambda e, h=h: e.tensor_add(
                hts[:, h * 64:(h + 1) * 64], hts[:, h * 64:(h + 1) * 64],
                psH[:, :]))
    p.wait("sync", "seq")
    return p.build()



def build_mtail():
    p = Prog()
    DI, NCI = 2048, 16
    yssm = p.dram("yssm", [DI, TH], BF16, kind="ExternalInput")
    xsf = p.dram("xsf", [DI, TH], BF16, kind="ExternalInput")
    zd = p.dram("z", [DI, TH], BF16, kind="ExternalInput")
    Drep = p.dram("Drep", [DI], F32, kind="ExternalInput")
    gnw = p.dram("gnorm_w", [DI], F32, kind="ExternalInput")
    owT = p.dram("out_wT", [DI, D], F32, kind="ExternalInput")
    s1T = p.dram("sc1T", [D, 256], F32, kind="ExternalInput")
    s2T = p.dram("sc2T", [256, 1], F32, kind="ExternalInput")
    smT = p.dram("summT", [D, 64], F32, kind="ExternalInput")
    qrT = p.dram("qrT", [D, 64], F32, kind="ExternalInput")
    y2tmp = p.dram("y2tmp", [DI, TH], BF16, kind="Internal")
    y1024d = p.dram("y1024", [D, TH], BF16, kind="ExternalOutput")
    scod = p.dram("scores", [1, TH], F32, kind="ExternalOutput")
    sumd = p.dram("summtok", [TH, 64], BF16, kind="ExternalOutput")
    qrd = p.dram("qrf", [64, TH], BF16, kind="ExternalOutput")

    common_tiles(p)
    ow_sb = p.sb("ow_sb", [128, NCI * D], BF16)
    s1_sb = p.sb("s1_sb", [128, NCH * 256], BF16)
    s2_sb = p.sb("s2_sb", [128, 2 * 1], BF16)
    sm_sb = p.sb("sm_sb", [128, NCH * 64], BF16)
    qr_sb = p.sb("qr_sb", [128, NCH * 64], BF16)
    D_c = p.sb("D_c", [128, NCI], F32)
    gn_c = p.sb("gn_c", [128, NCI], F32)
    by = p.sb("by", [128, NCI * 512], BF16)
    bx = p.sb("bx", [128, NCI * 512], BF16)
    bz = p.sb("bz", [128, NCI * 512], BF16)
    y1k = p.sb("y1k", [128, NCH * TH], BF16)
    hsc = p.sb("hsc", [128, 2 * 512], BF16)
    sq_row = p.sb("sq_row", [1, TH], F32)
    t1 = p.sb("t1", [128, 512], F32)
    t2 = p.sb("t2", [128, 512], BF16)
    sqt = p.sb("sqt", [128, 512], BF16)
    rowo = p.sb("rowo", [1, 512], F32)
    ps = p.ps("ps", [128, 512], F32)
    psA = p.ps("psA", [128, 512], F32)
    psS = p.ps("psS", [1, 512], F32)

    p.sop("gpsimd", lambda e: e.dma_start(
        out=ow_sb[:, :].rearrange("p (c n) -> p c n", c=NCI),
        in_=owT.rearrange("(c p) n -> p c n", p=128)), dma=True)
    p.sop("gpsimd", lambda e: e.dma_start(
        out=s1_sb[:, :].rearrange("p (c n) -> p c n", c=NCH),
        in_=s1T.rearrange("(c p) n -> p c n", p=128)), dma=True)
    p.sop("gpsimd", lambda e: e.dma_start(
        out=s2_sb[:, :].rearrange("p (c n) -> p c n", c=2),
        in_=s2T.rearrange("(c p) n -> p c n", p=128),
        allow_slow_non_contiguous=True), dma=True)
    p.sop("gpsimd", lambda e: e.dma_start(
        out=sm_sb[:, :].rearrange("p (c n) -> p c n", c=NCH),
        in_=smT.rearrange("(c p) n -> p c n", p=128)), dma=True)
    p.sop("gpsimd", lambda e: e.dma_start(
        out=qr_sb[:, :].rearrange("p (c n) -> p c n", c=NCH),
        in_=qrT.rearrange("(c p) n -> p c n", p=128)), dma=True)
    p.sop("gpsimd", lambda e: e.dma_start(
        out=D_c[:, 0:NCI], in_=Drep.rearrange("(c p) -> p c", p=128),
        allow_slow_non_contiguous=True), dma=True)
    p.sop("gpsimd", lambda e: e.dma_start(
        out=gn_c[:, 0:NCI], in_=gnw.rearrange("(c p) -> p c", p=128),
        allow_slow_non_contiguous=True), dma=True)

    for tb in range(TH // 512):
        for nm, src, buf in (("y", yssm, by), ("x", xsf, bx), ("z", zd, bz)):
            p.sop("sync", lambda e, src=src, buf=buf, tb=tb: e.dma_start(
                out=buf[:, :].rearrange("p (c n) -> p c n", c=NCI),
                in_=src.rearrange("(c p) n -> p c n", p=128)[
                    :, :, tb * 512:(tb + 1) * 512]), dma=True)
        for ci in range(NCI):
            p.sop("vector", lambda e, ci=ci: e.scalar_tensor_tensor(
                t1[:, :], bx[:, ci * 512:(ci + 1) * 512], D_c[:, ci:ci + 1],
                by[:, ci * 512:(ci + 1) * 512], ALU.mult, ALU.add))
            p.sop("scalar", lambda e, ci=ci: e.activation(
                t2[:, :], bz[:, ci * 512:(ci + 1) * 512], AF.Silu))
            p.sop("vector", lambda e: e.tensor_mul(t2[:, :], t1[:, :],
                                                   t2[:, :]))
            p.sop("sync", lambda e, ci=ci, tb=tb: e.dma_start(
                out=y2tmp[ci * 128:(ci + 1) * 128, tb * 512:(tb + 1) * 512],
                in_=t2[:, :]), dma=True)
            p.sop("scalar", lambda e: e.activation(sqt[:, :], t2[:, :],
                                                   AF.Square))
            p.sop("tensor", lambda e, ci=ci: e.matmul(
                psS[0:1, :], p._onescol_b[:, 0:1], sqt[:, :],
                start=(ci == 0), stop=(ci == NCI - 1)))
        p.sop("vector", lambda e, tb=tb: e.tensor_copy(
            sq_row[0:1, tb * 512:(tb + 1) * 512], psS[0:1, :]))
    p.sop("vector", lambda e: e.tensor_scalar_mul(sq_row[0:1, :],
                                                  sq_row[0:1, :], 1.0 / DI))
    p.sop("scalar", lambda e: e.activation(p._rtmp[0:1, :], sq_row[0:1, :],
                                           AF.Sqrt,
                                           bias=p._eps_col[0:1, 0:1]))
    p.sop("vector", lambda e: e.reciprocal(sq_row[0:1, :], p._rtmp[0:1, :]))
    for tb in range(TH // 512):
        p.sop("sync", lambda e, tb=tb: e.dma_start(
            out=by[:, :].rearrange("p (c n) -> p c n", c=NCI),
            in_=y2tmp.rearrange("(c p) n -> p c n", p=128)[
                :, :, tb * 512:(tb + 1) * 512]), dma=True)
        p.sop("tensor", lambda e, tb=tb: e.matmul(
            psA[:, :], p._onesrow_f[0:1, 0:128],
            sq_row[0:1, tb * 512:(tb + 1) * 512], start=True, stop=True))
        for ci in range(NCI):
            p.sop("vector", lambda e, ci=ci: e.tensor_mul(
                t1[:, :], by[:, ci * 512:(ci + 1) * 512], psA[:, :]))
            p.sop("vector", lambda e, ci=ci: e.tensor_scalar(
                by[:, ci * 512:(ci + 1) * 512], t1[:, :], gn_c[:, ci:ci + 1],
                None, ALU.mult))
        for co in range(NCH):
            for ci in range(NCI):
                p.sop("tensor", lambda e, co=co, ci=ci: e.matmul(
                    ps[:, :], ow_sb[:, ci * D + co * 128:ci * D + co * 128 + 128],
                    by[:, ci * 512:(ci + 1) * 512],
                    start=(ci == 0), stop=(ci == NCI - 1)))
            p.sop("scalar", lambda e: e.activation(t2[:, :], ps[:, :],
                                                   AF.Copy))
            p.sop("vector", lambda e, co=co, tb=tb: e.tensor_copy(
                y1k[:, co * TH + tb * 512:co * TH + tb * 512 + 512],
                t2[:, :]))
            p.sop("sync", lambda e, co=co, tb=tb: e.dma_start(
                out=y1024d[co * 128:(co + 1) * 128, tb * 512:(tb + 1) * 512],
                in_=t2[:, :]), dma=True)
    for tb in range(TH // 512):
        for co in range(2):
            for ci in range(NCH):
                p.sop("tensor", lambda e, co=co, ci=ci, tb=tb: e.matmul(
                    ps[:, :],
                    s1_sb[:, ci * 256 + co * 128:ci * 256 + co * 128 + 128],
                    y1k[:, ci * TH + tb * 512:ci * TH + tb * 512 + 512],
                    start=(ci == 0), stop=(ci == NCH - 1)))
            p.sop("scalar", lambda e, co=co: e.activation(
                hsc[:, co * 512:(co + 1) * 512], ps[:, :], AF.Relu))
        for ci in range(2):
            p.sop("tensor", lambda e, ci=ci: e.matmul(
                psS[0:1, :], s2_sb[:, ci:ci + 1],
                hsc[:, ci * 512:(ci + 1) * 512],
                start=(ci == 0), stop=(ci == 1)))
        p.sop("scalar", lambda e: e.activation(rowo[0:1, :], psS[0:1, :],
                                               AF.Sigmoid))
        p.sop("sync", lambda e, tb=tb: e.dma_start(
            out=scod[0:1, tb * 512:(tb + 1) * 512], in_=rowo[0:1, :]),
            dma=True)
    for tt in range(TH // 128):
        for ci in range(NCH):
            p.sop("tensor", lambda e, tt=tt, ci=ci: e.matmul(
                ps[0:128, 0:64],
                y1k[:, ci * TH + tt * 128:ci * TH + tt * 128 + 128],
                sm_sb[:, ci * 64:(ci + 1) * 64],
                start=(ci == 0), stop=(ci == NCH - 1)))
        p.sop("scalar", lambda e: e.activation(t2[:, 0:64], ps[:, 0:64],
                                               AF.Copy))
        p.sop("sync", lambda e, tt=tt: e.dma_start(
            out=sumd[tt * 128:(tt + 1) * 128, :], in_=t2[0:128, 0:64]),
            dma=True)
    for tb in range(TH // 512):
        for ci in range(NCH):
            p.sop("tensor", lambda e, ci=ci, tb=tb: e.matmul(
                ps[0:64, :], qr_sb[:, ci * 64:(ci + 1) * 64],
                y1k[:, ci * TH + tb * 512:ci * TH + tb * 512 + 512],
                start=(ci == 0), stop=(ci == NCH - 1)))
        p.sop("scalar", lambda e: e.activation(t2[0:64, :], ps[0:64, :],
                                               AF.Copy))
        p.sop("sync", lambda e, tb=tb: e.dma_start(
            out=qrd[0:64, tb * 512:(tb + 1) * 512], in_=t2[0:64, 0:512]),
            dma=True)
    p.wait("sync", "seq")
    return p.build()



def build_retr():
    p = Prog()
    scoF = p.dram("scoresF", [1, T], F32, kind="ExternalInput")
    sumF = p.dram("summF", [T, 64], BF16, kind="ExternalInput")
    y1024 = p.dram("y1024", [D, TH], BF16, kind="ExternalInput")
    qrd = p.dram("qrf", [64, TH], BF16, kind="ExternalInput")
    xn1 = p.dram("xn1", [D, TH], F32, kind="ExternalInput")
    kwT = p.dram("k_wT", [64, 64], F32, kind="ExternalInput")
    vwT = p.dram("v_wT", [64, D], F32, kind="ExternalInput")
    gwAT = p.dram("gwAT", [D, D], F32, kind="ExternalInput")
    gwBT = p.dram("gwBT", [D, D], F32, kind="ExternalInput")
    ident = p.dram("ident", [128, 128], F32, kind="ExternalInput")
    a1d = p.dram("a1", [D, TH], BF16, kind="ExternalOutput")

    common_tiles(p)
    identf = p.sb("identf", [128, 128], F32)
    kw_sb = p.sb("kw_sb", [64, 64], BF16)
    vw_sb = p.sb("vw_sb", [64, D], BF16)
    gA_sb = p.sb("gA_sb", [128, NCH * D], BF16)
    gB_sb = p.sb("gB_sb", [128, NCH * D], BF16)
    y1k = p.sb("y1k", [128, NCH * TH], BF16)
    qr_sb = p.sb("qr_sb", [64, TH], BF16)
    sco = p.sb("sco", [32, 256], F32)
    srtA = p.sb("srtA", [32, 256], F32)
    srtB = p.sb("srtB", [32, 256], F32)
    mxs = p.sb("mxs", [32, 56], F32)
    cand = p.sb("cand", [1, 1792], F32)
    cndB = p.sb("cndB", [1, 1792], F32)
    mx8 = p.sb("mx8", [1, 64], F32)
    r1 = p.sb("r1", [32, 256], F32)
    r2 = p.sb("r2", [32, 256], F32)
    rk = p.sb("rk", [32, 256], F32)
    on32 = p.sb("on32", [32, 256], F32)
    col32 = p.sb("col32", [32, 4], F32)
    segrow = p.sb("segrow", [1, 32], F32)
    sc1 = p.sb("sc1", [1, 8], F32)
    rkrow = p.sb("rkrow", [1, 128], F32)
    iota50 = p.sb("iota50", [128, 50], mybir.dt.int32)
    iota50f = p.sb("iota50f", [128, 50], F32)
    sumt = p.sb("sumt", [128, 64], BF16)
    rkcol = p.sb("rkcol", [128, 1], F32)
    oht = p.sb("oht", [128, 50], BF16)
    pool = p.sb("pool", [50, 64], BF16)
    poolT = p.sb("poolT", [64, 50], BF16)
    krT = p.sb("krT", [64, 50], BF16)
    vr = p.sb("vr", [50, D], BF16)
    maddb = p.sb("maddb", [128, 50], F32)
    lg = p.sb("lg", [128, 50], F32)
    mxc = p.sb("mxc", [128, 1], F32)
    dsum = p.sb("dsum", [128, 1], F32)
    esb = p.sb("esb", [128, 50], BF16)
    esT = p.sb("esT", [50, 128], BF16)
    retr = p.sb("retr", [128, NCH * 128], BF16)
    rdb = p.sb("rdb", [128, 128], F32)
    rfcol = p.sb("rfcol", [128, 1], F32)
    xb_ = p.sb("xb_", [128, NCH * 128], BF16)
    gateb = p.sb("gateb", [128, 128], BF16)
    t1 = p.sb("t1", [128, 128], F32)
    t2 = p.sb("t2", [128, 128], BF16)
    psP = p.ps("psP", [50, 64], F32)
    psL = p.ps("psL", [128, 50], F32)
    psT = p.ps("psT", [128, 128], BF16)
    psR = p.ps("psR", [128, 128], F32)
    psA2 = p.ps("psA2", [128, 128], F32)
    psB2 = p.ps("psB2", [128, 128], F32)
    psM = p.ps("psM", [128, 50], F32)
    psD2 = p.ps("psD2", [1, 128], F32)

    p.sop("sync", lambda e: e.dma_start(out=identf[:, :], in_=ident),
          dma=True)
    p.sop("vector", lambda e: e.tensor_copy(p._identb[:, :], identf[:, :]))
    p.sop("gpsimd", lambda e: e.dma_start(out=kw_sb[:, :], in_=kwT),
          dma=True)
    p.sop("gpsimd", lambda e: e.dma_start(out=vw_sb[:, :], in_=vwT),
          dma=True)
    p.sop("gpsimd", lambda e: e.dma_start(
        out=gA_sb[:, :].rearrange("p (c n) -> p c n", c=NCH),
        in_=gwAT.rearrange("(c p) n -> p c n", p=128)), dma=True)
    p.sop("gpsimd", lambda e: e.dma_start(
        out=gB_sb[:, :].rearrange("p (c n) -> p c n", c=NCH),
        in_=gwBT.rearrange("(c p) n -> p c n", p=128)), dma=True)
    p.sop("sync", lambda e: e.dma_start(
        out=y1k[:, :].rearrange("p (c n) -> p c n", c=NCH),
        in_=y1024.rearrange("(c p) n -> p c n", p=128)), dma=True)
    p.sop("sync", lambda e: e.dma_start(out=qr_sb[:, :], in_=qrd), dma=True)
    p.sop("gpsimd", lambda e: e.iota(iota50[:, :], pattern=[[1, 50]], base=1,
                                     channel_multiplier=0))
    p.sop("vector", lambda e: e.tensor_copy(iota50f[:, :], iota50[:, :]))

    candd = p.dram("candd", [32, 56], F32, kind="Internal")
    rkd = p.dram("rkd", [1, T], F32, kind="Internal")

    def bc32(src11, dstcol):
        p.sop("tensor", lambda e: e.matmul(
            psL[0:32, 0:1], p._onesrow_f[0:1, 0:32], src11,
            start=True, stop=True))
        p.sop("vector", lambda e: e.tensor_copy(dstcol, psL[0:32, 0:1]))

    def scan32(xin, out):
        p.sop("vector", lambda e: e.tensor_tensor_scan(
            out[0:32, :], xin[0:32, :], on32[0:32, :], 0.0, ALU.add,
            ALU.mult))
        p.sop("tensor", lambda e: e.transpose(
            psD2[0:1, 0:32], out[0:32, 255:256], identf[0:32, 0:32]))
        p.sop("vector", lambda e: e.tensor_copy(segrow[0:1, :],
                                                psD2[0:1, 0:32]))
        p.sop("vector", lambda e: e.tensor_tensor_scan(
            segrow[0:1, :], segrow[0:1, :], p._onesrow_f[0:1, 0:32], 0.0,
            ALU.add, ALU.mult))
        p.sop("tensor", lambda e: e.transpose(
            psL[0:32, 0:1], segrow[0:1, :], p._onesrow_f[0:1, 0:1]))
        p.sop("vector", lambda e: e.tensor_copy(col32[:, 0:1],
                                                psL[0:32, 0:1]))
        p.sop("vector", lambda e: e.tensor_sub(col32[:, 1:2], col32[:, 0:1],
                                               out[0:32, 255:256]))
        p.sop("vector", lambda e, out=out: e.tensor_scalar(
            out[0:32, :], out[0:32, :], col32[:, 1:2], None, ALU.add))

    p.sop("sync", lambda e: e.dma_start(
        out=sco[:, :], in_=scoF[0, :].rearrange("(p n) -> p n", p=32)),
        dma=True)
    p.sop("vector", lambda e: e.memset(on32[:, :], 1.0))
    p.sop("vector", lambda e: e.tensor_copy(srtA[:, :], sco[:, :]))
    cur, nxt = srtA, srtB
    for rnd in range(7):
        p.sop("vector", lambda e, cur=cur, rnd=rnd: e.max(
            out=mxs[:, rnd * 8:(rnd + 1) * 8], in_=cur[:, :]))
        p.sop("vector", lambda e, cur=cur, nxt=nxt, rnd=rnd: e.match_replace(
            out=nxt[:, :], in_to_replace=mxs[:, rnd * 8:(rnd + 1) * 8],
            in_values=cur[:, :], imm_value=-1e30))
        cur, nxt = nxt, cur
    p.sop("sync", lambda e: e.dma_start(out=candd, in_=mxs[:, :]), dma=True)
    p.sop("sync", lambda e: e.dma_start(
        out=cand[0:1, :], in_=candd.rearrange("p n -> (p n)")[None, :]),
        dma=True)
    for rnd in range(7):
        p.sop("vector", lambda e, cur2=(cand if rnd % 2 == 0 else cndB),
              rnd=rnd: e.max(out=mx8[0:1, rnd * 8:(rnd + 1) * 8],
                             in_=cur2[0:1, :]))
        p.sop("vector", lambda e, cur2=(cand if rnd % 2 == 0 else cndB),
              nxt2=(cndB if rnd % 2 == 0 else cand), rnd=rnd:
              e.match_replace(out=nxt2[0:1, :],
                              in_to_replace=mx8[0:1, rnd * 8:(rnd + 1) * 8],
                              in_values=cur2[0:1, :], imm_value=-1e30))
    p.sop("vector", lambda e: e.tensor_scalar_max(
        sc1[0:1, 0:1], mx8[0:1, 50:51], 0.5))
    bc32(sc1[0:1, 0:1], col32[:, 2:3])
    p.sop("vector", lambda e: e.tensor_scalar(
        r1[:, :], sco[:, :], col32[:, 2:3], None, ALU.is_gt))
    p.sop("vector", lambda e: e.tensor_scalar(
        r2[:, :], sco[:, :], 0.5, None, ALU.is_gt))
    p.sop("vector", lambda e: e.tensor_reduce(col32[:, 3:4], r2[:, :], AX.X,
                                              ALU.add))
    p.sop("tensor", lambda e: e.matmul(psD2[0:1, 0:1], col32[0:32, 3:4],
                                       p._ones_f[0:32, 0:1],
                                       start=True, stop=True))
    p.sop("vector", lambda e: e.tensor_copy(sc1[0:1, 1:2], psD2[0:1, 0:1]))
    p.sop("vector", lambda e: e.tensor_scalar_min(sc1[0:1, 1:2],
                                                  sc1[0:1, 1:2], 50.0))
    p.sop("vector", lambda e: e.tensor_reduce(col32[:, 3:4], r1[:, :], AX.X,
                                              ALU.add))
    p.sop("tensor", lambda e: e.matmul(psD2[0:1, 0:1], col32[0:32, 3:4],
                                       p._ones_f[0:32, 0:1],
                                       start=True, stop=True))
    p.sop("vector", lambda e: e.tensor_copy(sc1[0:1, 2:3], psD2[0:1, 0:1]))
    p.sop("vector", lambda e: e.tensor_sub(sc1[0:1, 3:4], sc1[0:1, 1:2],
                                           sc1[0:1, 2:3]))
    p.sop("vector", lambda e: e.tensor_scalar(
        r2[:, :], sco[:, :], col32[:, 2:3], None, ALU.is_equal))
    scan32(r2, rk)
    bc32(sc1[0:1, 3:4], col32[:, 2:3])
    p.sop("vector", lambda e: e.tensor_scalar(
        rk[:, :], rk[:, :], col32[:, 2:3], None, ALU.is_le))
    p.sop("vector", lambda e: e.tensor_mul(r2[:, :], r2[:, :], rk[:, :]))
    p.sop("vector", lambda e: e.tensor_add(r1[:, :], r1[:, :], r2[:, :]))
    p.sop("vector", lambda e: e.tensor_reduce(col32[:, 3:4], r1[:, :], AX.X,
                                              ALU.add))
    p.sop("tensor", lambda e: e.matmul(psD2[0:1, 0:1], col32[0:32, 3:4],
                                       p._ones_f[0:32, 0:1],
                                       start=True, stop=True))
    p.sop("vector", lambda e: e.tensor_copy(sc1[0:1, 4:5], psD2[0:1, 0:1]))
    scan32(r1, rk)
    p.sop("vector", lambda e: e.tensor_mul(rk[:, :], rk[:, :], r1[:, :]))
    p.sop("vector", lambda e: e.tensor_add(rk[:, :], rk[:, :], r1[:, :]))
    p.sop("vector", lambda e: e.tensor_scalar_add(rk[:, :], rk[:, :], -1.0))
    p.sop("sync", lambda e: e.dma_start(
        out=rkd[0, :].rearrange("(p n) -> p n", p=32), in_=rk[:, :]),
        dma=True)
    p.sop("vector", lambda e: e.tensor_reduce(col32[:, 3:4], sco[:, :], AX.X,
                                              ALU.add))
    p.sop("tensor", lambda e: e.matmul(psD2[0:1, 0:1], col32[0:32, 3:4],
                                       p._ones_f[0:32, 0:1],
                                       start=True, stop=True))
    p.sop("vector", lambda e: e.tensor_copy(sc1[0:1, 5:6], psD2[0:1, 0:1]))
    p.sop("vector", lambda e: e.tensor_scalar_mul(sc1[0:1, 5:6],
                                                  sc1[0:1, 5:6], 1.0 / T))
    p.sop("vector", lambda e: e.tensor_scalar(sc1[0:1, 5:6], sc1[0:1, 5:6],
                                              0.3, None, ALU.is_gt))
    p.sop("vector", lambda e: e.tensor_scalar(sc1[0:1, 6:7], sc1[0:1, 4:5],
                                              0.5, None, ALU.is_gt))
    p.sop("vector", lambda e: e.tensor_mul(sc1[0:1, 5:6], sc1[0:1, 5:6],
                                           sc1[0:1, 6:7]))
    p.sop("tensor", lambda e: e.matmul(
        psL[:, 0:1], p._onesrow_f[0:1, 0:128], sc1[0:1, 5:6],
        start=True, stop=True))
    p.sop("vector", lambda e: e.tensor_copy(rfcol[:, :], psL[:, 0:1]))
    for tt in range(T // 128):
        p.sop("sync", lambda e, tt=tt: e.dma_start(
            out=sumt[:, :], in_=sumF[tt * 128:(tt + 1) * 128, :]), dma=True)
        p.sop("sync", lambda e, tt=tt: e.dma_start(
            out=rkrow[0:1, :], in_=rkd[0:1, tt * 128:(tt + 1) * 128]),
            dma=True)
        p.sop("tensor", lambda e: e.transpose(
            psL[:, 0:1], rkrow[0:1, :], p._onesrow_f[0:1, 0:1]))
        p.sop("vector", lambda e: e.tensor_copy(rkcol[:, :], psL[:, 0:1]))
        p.sop("vector", lambda e: e.tensor_scalar(
            oht[:, :], iota50f[:, :], rkcol[:, 0:1], None, ALU.is_equal))
        p.sop("tensor", lambda e, tt=tt: e.matmul(
            psP[:, :], oht[:, :], sumt[:, :],
            start=(tt == 0), stop=(tt == T // 128 - 1)))
    p.sop("vector", lambda e: e.tensor_copy(pool[:, :], psP[:, :]))
    p.sop("tensor", lambda e: e.transpose(psT[0:64, 0:50], pool[:, :],
                                          p._identb[0:50, 0:50]))
    p.sop("vector", lambda e: e.tensor_copy(poolT[:, :], psT[0:64, 0:50]))
    p.sop("tensor", lambda e: e.matmul(psL[0:64, 0:50], kw_sb[:, :],
                                       poolT[:, :], start=True, stop=True))
    p.sop("vector", lambda e: e.tensor_copy(krT[:, :], psL[0:64, 0:50]))
    for ob in range(8):
        p.sop("tensor", lambda e, ob=ob: e.matmul(
            psR[0:50, 0:128], poolT[:, :],
            vw_sb[:, ob * 128:(ob + 1) * 128], start=True, stop=True))
        p.sop("vector", lambda e, ob=ob: e.tensor_copy(
            vr[:, ob * 128:(ob + 1) * 128], psR[0:50, 0:128]))
    madr = p.sb("madr", [1, 50], F32)
    rdrow = p.sb("rdrow", [1, 128], F32)
    p.sop("vector", lambda e: e.tensor_scalar(
        madr[0:1, :], iota50f[0:1, :], sc1[0:1, 4:5], None, ALU.is_le))
    p.sop("vector", lambda e: e.tensor_scalar(
        madr[0:1, :], madr[0:1, :], 1.0, 1e9, ALU.subtract, ALU.mult))
    p.sop("tensor", lambda e: e.matmul(
        psM[:, :], p._onesrow_f[0:1, 0:128], madr[0:1, :],
        start=True, stop=True))
    p.sop("vector", lambda e: e.tensor_copy(maddb[:, :], psM[:, :]))
    for tt in range(TH // 128):
        p.sop("gpsimd", lambda e, tt=tt: e.dma_start(
            out=xb_[:, :].rearrange("p (c n) -> p c n", c=NCH),
            in_=xn1.rearrange("(c p) n -> p c n", p=128)[
                :, :, tt * 128:(tt + 1) * 128]), dma=True)
        p.sop("tensor", lambda e, tt=tt: e.matmul(
            psL[:, :], qr_sb[:, tt * 128:(tt + 1) * 128], krT[:, :],
            start=True, stop=True))
        p.sop("vector", lambda e: e.scalar_tensor_tensor(
            lg[:, :], psL[:, :], 0.25, maddb[:, :], ALU.mult, ALU.add))
        p.sop("vector", lambda e: e.tensor_reduce(mxc[:, :], lg[:, :], AX.X,
                                                  ALU.max))
        p.sop("vector", lambda e: e.tensor_scalar_mul(mxc[:, :], mxc[:, :],
                                                      -1.0))
        p.sop("scalar", lambda e: e.activation(
            esb[:, :], lg[:, :], AF.Exp, bias=mxc[:, 0:1],
            accum_out=dsum[:, 0:1]))
        p.sop("vector", lambda e: e.reciprocal(dsum[:, :], dsum[:, :]))
        p.sop("tensor", lambda e: e.transpose(psT[0:50, 0:128], esb[:, :],
                                              p._identb[:, :]))
        p.sop("vector", lambda e: e.tensor_copy(esT[:, :], psT[0:50, 0:128]))
        for co in range(NCH):
            p.sop("tensor", lambda e, co=co: e.matmul(
                psR[:, :], vr[0:50, co * 128:(co + 1) * 128], esT[:, :],
                start=True, stop=True))
            p.sop("vector", lambda e, co=co: e.tensor_copy(
                retr[:, co * 128:(co + 1) * 128], psR[:, :]))
        p.sop("tensor", lambda e: e.transpose(psD2[0:1, :], dsum[:, :],
                                              identf[:, :]))
        p.sop("vector", lambda e: e.tensor_copy(rdrow[0:1, :], psD2[0:1, :]))
        p.sop("tensor", lambda e: e.matmul(
            psA2[:, :], p._onesrow_f[0:1, 0:128], rdrow[0:1, :],
            start=True, stop=True))
        p.sop("vector", lambda e: e.tensor_copy(rdb[:, :], psA2[:, :]))
        for co in range(NCH):
            for ci in range(NCH):
                p.sop("tensor", lambda e, co=co, ci=ci, tt=tt: e.matmul(
                    psA2[:, :],
                    gA_sb[:, ci * D + co * 128:ci * D + co * 128 + 128],
                    y1k[:, ci * TH + tt * 128:ci * TH + tt * 128 + 128],
                    start=(ci == 0), stop=(ci == NCH - 1)))
            for ci in range(NCH):
                p.sop("tensor", lambda e, co=co, ci=ci: e.matmul(
                    psB2[:, :],
                    gB_sb[:, ci * D + co * 128:ci * D + co * 128 + 128],
                    retr[:, ci * 128:(ci + 1) * 128],
                    start=(ci == 0), stop=(ci == NCH - 1)))
            p.sop("vector", lambda e: e.tensor_mul(t1[:, :], psB2[:, :],
                                                   rdb[:, :]))
            p.sop("vector", lambda e: e.tensor_add(t1[:, :], t1[:, :],
                                                   psA2[:, :]))
            p.sop("scalar", lambda e: e.activation(gateb[:, :], t1[:, :],
                                                   AF.Sigmoid))
            p.sop("vector", lambda e, co=co: e.tensor_mul(
                t2[:, :], gateb[:, :], retr[:, co * 128:(co + 1) * 128]))
            p.sop("vector", lambda e: e.tensor_mul(t1[:, :], t2[:, :],
                                                   rdb[:, :]))
            p.sop("vector", lambda e: e.tensor_scalar(
                t1[:, :], t1[:, :], rfcol[:, 0:1], None, ALU.mult))
            p.sop("vector", lambda e, co=co, tt=tt: e.tensor_add(
                t1[:, :], t1[:, :],
                y1k[:, co * TH + tt * 128:co * TH + tt * 128 + 128]))
            p.sop("vector", lambda e, co=co: e.tensor_add(
                t2[:, :], t1[:, :], xb_[:, co * 128:(co + 1) * 128]))
            p.sop("sync", lambda e, co=co, tt=tt: e.dma_start(
                out=a1d[co * 128:(co + 1) * 128, tt * 128:(tt + 1) * 128],
                in_=t2[:, :]), dma=True)
    p.wait("sync", "seq")
    return p.build()



def build_l1route():
    p = Prog()
    xn1 = p.dram("xn1", [D, TH], F32, kind="ExternalInput")
    vy1 = p.dram("vy1", [1, TH], F32, kind="ExternalInput")
    ctT = p.dram("centsT", [D, 8], F32, kind="ExternalInput")
    wrT = p.dram("w_routeT", [D, 8], F32, kind="ExternalInput")
    ident = p.dram("ident", [128, 128], F32, kind="ExternalInput")
    gd = p.dram("gatesF", [8, TH], F32, kind="ExternalOutput")

    common_tiles(p)
    identf = p.sb("identf", [128, 128], F32)
    ct_sb = p.sb("ct_sb", [128, NCH * 8], BF16)
    ctf_sb = p.sb("ctf_sb", [128, NCH * 8], F32)
    wr_sb = p.sb("wr_sb", [128, NCH * 8], BF16)
    s2r = p.sb("s2r", [1, TH], F32)
    cnrow = p.sb("cnrow", [1, 8], F32)
    cnb = p.sb("cnb", [128, 8], F32)
    xb_ = p.sb("xb_", [128, NCH * 128], BF16)
    lg = p.sb("lg", [128, 8], F32)
    scol = p.sb("scol", [128, 1], F32)
    m1c = p.sb("m1c", [128, 1], F32)
    oh = p.sb("oh", [128, 8], F32)
    cum = p.sb("cum", [128, 8], F32)
    oh1 = p.sb("oh1", [128, 8], F32)
    keep = p.sb("keep", [128, 8], F32)
    es = p.sb("es", [128, 8], F32)
    sumc = p.sb("sumc", [128, 1], F32)
    gt = p.sb("gt", [128, 8], F32)
    gtf = p.sb("gtf", [8, 128], F32)
    sqc = p.sb("sqc", [128, 8], F32)
    psA = p.ps("psA", [128, 8], F32)
    psB = p.ps("psB", [128, 8], F32)
    psr = p.ps("psr", [1, 8], F32)
    pst = p.ps("pst", [128, 128], F32)

    p.sop("sync", lambda e: e.dma_start(out=identf[:, :], in_=ident),
          dma=True)
    p.sop("gpsimd", lambda e: e.dma_start(
        out=ct_sb[:, :].rearrange("p (c n) -> p c n", c=NCH),
        in_=ctT.rearrange("(c p) n -> p c n", p=128)), dma=True)
    p.sop("sync", lambda e: e.dma_start(
        out=ctf_sb[:, :].rearrange("p (c n) -> p c n", c=NCH),
        in_=ctT.rearrange("(c p) n -> p c n", p=128)), dma=True)
    p.sop("gpsimd", lambda e: e.dma_start(
        out=wr_sb[:, :].rearrange("p (c n) -> p c n", c=NCH),
        in_=wrT.rearrange("(c p) n -> p c n", p=128)), dma=True)
    p.sop("sync", lambda e: e.dma_start(out=s2r[0:1, :], in_=vy1[0:1, :]),
          dma=True)
    p.sop("scalar", lambda e: e.activation(p._rtmp[0:1, :], s2r[0:1, :],
                                           AF.Sqrt, scale=float(D)))
    p.sop("vector", lambda e: e.tensor_scalar_max(p._rtmp[0:1, :],
                                                  p._rtmp[0:1, :], 1e-12))
    p.sop("vector", lambda e: e.reciprocal(s2r[0:1, :], p._rtmp[0:1, :]))
    for ci in range(NCH):
        p.sop("scalar", lambda e, ci=ci: e.activation(
            sqc[:, :], ctf_sb[:, ci * 8:(ci + 1) * 8], AF.Square))
        p.sop("tensor", lambda e, ci=ci: e.matmul(
            psr[0:1, :], p._ones_f[:, 0:1], sqc[:, :],
            start=(ci == 0), stop=(ci == NCH - 1)))
    p.sop("vector", lambda e: e.tensor_copy(cnrow[0:1, :], psr[0:1, :]))
    p.sop("scalar", lambda e: e.activation(cnrow[0:1, :], cnrow[0:1, :],
                                           AF.Sqrt))
    p.sop("vector", lambda e: e.tensor_scalar_max(cnrow[0:1, :],
                                                  cnrow[0:1, :], 1e-12))
    p.sop("vector", lambda e: e.reciprocal(cnrow[0:1, :], cnrow[0:1, :]))
    p.sop("tensor", lambda e: e.matmul(
        psA[:, :], p._onesrow_f[0:1, 0:128], cnrow[0:1, :],
        start=True, stop=True))
    p.sop("vector", lambda e: e.tensor_copy(cnb[:, :], psA[:, :]))
    for tt in range(TH // 128):
        p.sop("gpsimd", lambda e, tt=tt: e.dma_start(
            out=xb_[:, :].rearrange("p (c n) -> p c n", c=NCH),
            in_=xn1.rearrange("(c p) n -> p c n", p=128)[
                :, :, tt * 128:(tt + 1) * 128]), dma=True)
        for ci in range(NCH):
            p.sop("tensor", lambda e, ci=ci: e.matmul(
                psA[:, :], xb_[:, ci * 128:(ci + 1) * 128],
                ct_sb[:, ci * 8:(ci + 1) * 8],
                start=(ci == 0), stop=(ci == NCH - 1)))
            p.sop("tensor", lambda e, ci=ci: e.matmul(
                psB[:, :], xb_[:, ci * 128:(ci + 1) * 128],
                wr_sb[:, ci * 8:(ci + 1) * 8],
                start=(ci == 0), stop=(ci == NCH - 1)))
        p.sop("tensor", lambda e, tt=tt: e.transpose(
            pst[0:1, 0:128], s2r[0:1, tt * 128:(tt + 1) * 128],
            p._onesrow_f[0:1, 0:1])) if False else None
        p.sop("tensor", lambda e, tt=tt: e.transpose(
            pst[:, 0:1], s2r[0:1, tt * 128:(tt + 1) * 128],
            p._onesrow_f[0:1, 0:1]))
        p.sop("vector", lambda e: e.tensor_copy(scol[:, :], pst[:, 0:1]))
        p.sop("vector", lambda e: e.tensor_mul(lg[:, :], psA[:, :],
                                               cnb[:, :]))
        p.sop("vector", lambda e: e.tensor_scalar(lg[:, :], lg[:, :],
                                                  scol[:, 0:1], None,
                                                  ALU.mult))
        p.sop("vector", lambda e: e.tensor_add(lg[:, :], lg[:, :],
                                               psB[:, :]))
        p.sop("vector", lambda e: e.tensor_reduce(m1c[:, :], lg[:, :], AX.X,
                                                  ALU.max))
        p.sop("vector", lambda e: e.tensor_scalar(
            oh[:, :], lg[:, :], m1c[:, 0:1], None, ALU.is_ge))
        p.sop("vector", lambda e: e.tensor_tensor_scan(
            cum[:, :], oh[:, :], p._ones_f[:, 0:8], 0.0, ALU.add, ALU.mult))
        p.sop("vector", lambda e: e.tensor_scalar(
            oh1[:, :], cum[:, :], 1.0, None, ALU.is_equal))
        p.sop("vector", lambda e: e.tensor_mul(oh1[:, :], oh1[:, :],
                                               oh[:, :]))
        p.sop("vector", lambda e: e.scalar_tensor_tensor(
            es[:, :], oh1[:, :], -1e30, lg[:, :], ALU.mult, ALU.add))
        p.sop("vector", lambda e: e.tensor_reduce(sumc[:, :], es[:, :],
                                                  AX.X, ALU.max))
        p.sop("vector", lambda e: e.tensor_scalar(
            keep[:, :], lg[:, :], sumc[:, 0:1], None, ALU.is_ge))
        p.sop("vector", lambda e: e.tensor_scalar_mul(m1c[:, :], m1c[:, :],
                                                      -1.0))
        p.sop("scalar", lambda e: e.activation(es[:, :], lg[:, :], AF.Exp,
                                               bias=m1c[:, 0:1]))
        p.sop("vector", lambda e: e.tensor_mul(es[:, :], es[:, :],
                                               keep[:, :]))
        p.sop("vector", lambda e: e.tensor_reduce(sumc[:, :], es[:, :],
                                                  AX.X, ALU.add))
        p.sop("vector", lambda e: e.reciprocal(sumc[:, :], sumc[:, :]))
        p.sop("vector", lambda e: e.tensor_scalar(gt[:, :], es[:, :],
                                                  sumc[:, 0:1], None,
                                                  ALU.mult))
        p.sop("tensor", lambda e: e.transpose(pst[0:8, 0:128], gt[:, :],
                                              identf[:, :]))
        p.sop("vector", lambda e: e.tensor_copy(gtf[0:8, :], pst[0:8, 0:128]))
        p.sop("sync", lambda e, tt=tt: e.dma_start(
            out=gd[0:8, tt * 128:(tt + 1) * 128], in_=gtf[0:8, 0:128]),
            dma=True)
    p.wait("sync", "seq")
    return p.build()


def build_l1ffn(dff=4096):
    p = Prog()
    xn1 = p.dram("xn1", [D, TH], F32, kind="ExternalInput")
    w1c = p.dram("w1cat", [D, dff], F32, kind="ExternalInput")
    b1c = p.dram("b1cat", [dff], F32, kind="ExternalInput")
    w2c = p.dram("w2cat", [dff, D], F32, kind="ExternalInput")
    b2d = p.dram("b2", [8, D], F32, kind="ExternalInput")
    gd = p.dram("gatesF", [8, TH], F32, kind="ExternalInput")
    x1d = p.dram("x1", [D, TH], F32, kind="ExternalInput")
    a1d = p.dram("a1", [D, TH], BF16, kind="ExternalInput")
    outd = p.dram("out", [D, TH], F32, kind="ExternalOutput")

    nh = dff // 128
    NTB = TH // 512
    NG = nh + NCH
    common_tiles(p)
    w1_sb = p.sb("w1_sb", [128, NCH * dff], BF16)
    w2_sb = p.sb("w2_sb", [128, nh * D], BF16)
    b1_c = p.sb("b1_c", [128, nh], F32)
    b2_sb = p.sb("b2_sb", [8, D], BF16)
    g8 = p.sb("g8", [8, 512], BF16)
    ger = p.sb("ger", [1, 8 * 512], BF16)
    hb = p.sb("hb", [128, nh * 512], BF16)
    xb_ = p.sb("xb_", [128, NCH * 512], BF16)
    x1b = [p.sb(f"x1b{i}", [128, 512], F32) for i in range(2)]
    a1b = p.sb("a1b", [128, 512], BF16)
    t1 = p.sb("t1", [128, 512], F32)
    ps = [p.ps(f"ps{i}", [128, 512], F32) for i in range(2)]
    psG = [p.ps(f"psG{i}", [128, 512], F32) for i in range(2)]

    def W(*pairs):
        return [(n, v) for (n, v) in pairs if v > 0] or None

    p.op("gpsimd", lambda e: e.dma_start(
        out=w1_sb[:, :].rearrange("p (c n) -> p c n", c=NCH),
        in_=w1c.rearrange("(c p) n -> p c n", p=128)), inc="wl", incn=16)
    p.op("gpsimd", lambda e: e.dma_start(
        out=w2_sb[:, :].rearrange("p (c n) -> p c n", c=nh),
        in_=w2c.rearrange("(c p) n -> p c n", p=128)), inc="wl", incn=16)
    p.op("gpsimd", lambda e: e.dma_start(
        out=b1_c[:, 0:nh], in_=b1c.rearrange("(c p) -> p c", p=128),
        allow_slow_non_contiguous=True), inc="wl", incn=16)
    p.op("gpsimd", lambda e: e.dma_start(out=b2_sb[:, :], in_=b2d),
         inc="wl", incn=16)
    def emit_ldtb(tb):
        p.op("gpsimd", lambda e, tb=tb: e.dma_start(
            out=xb_[:, :].rearrange("p (c n) -> p c n", c=NCH),
            in_=xn1.rearrange("(c p) n -> p c n", p=128)[
                :, :, tb * 512:(tb + 1) * 512]), inc="xl", incn=16,
            wait=W(("mg", (tb - 1) * NG + nh)) if tb >= 1 else None)
        p.op("gpsimd", lambda e, tb=tb: e.dma_start(
            out=g8[:, :], in_=gd[0:8, tb * 512:(tb + 1) * 512]),
            inc="xg", incn=16,
            wait=W(("mg", tb * NG)) if tb >= 1 else None)
        p.op("gpsimd", lambda e, tb=tb: e.dma_start(
            out=ger[0:1, :].rearrange("o (e n) -> o e n", e=8),
            in_=gd[0:8, tb * 512:(tb + 1) * 512].rearrange(
                "(o e) n -> o e n", o=1)), inc="gl", incn=16,
            wait=W(("gb", 8 * tb)) if tb >= 1 else None)

    for tb in range(NTB):
        for o in range(NCH):
            oo = tb * NCH + o
            p.op("sync", lambda e, o=o, tb=tb, oo=oo: e.dma_start(
                out=x1b[oo % 2][:, :],
                in_=x1d[o * 128:(o + 1) * 128, tb * 512:(tb + 1) * 512]),
                inc="xd%d" % (oo % 2), incn=16,
                wait=W(("va", oo - 1)) if oo >= 2 else None)
            p.op("sync", lambda e, o=o, tb=tb, oo=oo: e.dma_start(
                out=a1b[:, :],
                in_=a1d[o * 128:(o + 1) * 128, tb * 512:(tb + 1) * 512]),
                inc="ad", incn=16,
                wait=W(("va", oo)) if oo >= 1 else None)
    emit_ldtb(0)
    for tb in range(NTB):
        base = tb * NG
        for co in range(nh):
            g = base + co
            for ci in range(NCH):
                w_ = [("ae", g - 1)]
                if co == 0 and ci == 0:
                    w_.append(("xl", 16 * (tb + 1)))
                    if tb == 0:
                        w_.append(("wl", 64))
                p.op("tensor", lambda e, co=co, ci=ci, g=g: e.matmul(
                    ps[g % 2][:, :],
                    w1_sb[:, ci * dff + co * 128:ci * dff + co * 128 + 128],
                    xb_[:, ci * 512:(ci + 1) * 512],
                    start=(ci == 0), stop=(ci == NCH - 1),
                    skip_group_check=True),
                    inc="mg" if ci == NCH - 1 else None,
                    wait=W(*w_) if ci == 0 else None)
            p.op("scalar", lambda e, co=co, g=g: e.activation(
                hb[:, co * 512:(co + 1) * 512], ps[g % 2][:, :], AF.Gelu,
                bias=b1_c[:, co:co + 1]), inc="ae", wait=W(("mg", g + 1)))
            if co % 4 == 3:
                ex = co // 4
                gg = tb * 8 + ex
                p.op("tensor", lambda e, gg=gg, ex=ex: e.matmul(
                    psG[gg % 2][:, :], p._ones_b[0:1, 0:128],
                    ger[0:1, ex * 512:(ex + 1) * 512],
                    start=True, stop=True, skip_group_check=True),
                    inc="gb",
                    wait=W(("gl", 16 * (tb + 1)), ("hs", 4 * (gg - 1))))
                for cc in range(4):
                    p.op("vector", lambda e, ex=ex, cc=cc, gg=gg:
                         e.tensor_mul(
                             hb[:, (4 * ex + cc) * 512:
                                (4 * ex + cc + 1) * 512],
                             hb[:, (4 * ex + cc) * 512:
                                (4 * ex + cc + 1) * 512],
                             psG[gg % 2][:, :]), inc="hs",
                         wait=W(("gb", gg + 1), ("ae", base + 4 * ex + 4))
                         if cc == 0 else None)
        for co in range(NCH):
            g = base + nh + co
            oo = tb * NCH + co
            for ci in range(nh):
                w_ = [("ae", g if co == 0 else g - 1)]
                if co == 0 and ci == 0:
                    w_.append(("hs", 32 * (tb + 1)))
                    w_.append(("xg", 16 * (tb + 1)))
                p.op("tensor", lambda e, co=co, ci=ci, g=g: e.matmul(
                    ps[g % 2][:, :],
                    w2_sb[:, ci * D + co * 128:ci * D + co * 128 + 128],
                    hb[:, ci * 512:(ci + 1) * 512],
                    start=(ci == 0), stop=False, skip_group_check=True),
                    wait=W(*w_) if ci == 0 else None)
            p.op("tensor", lambda e, co=co, g=g: e.matmul(
                ps[g % 2][:, :], b2_sb[0:8, co * 128:(co + 1) * 128],
                g8[:, :], start=False, stop=True, skip_group_check=True),
                inc="mg")
            p.op("vector", lambda e, co=co, g=g, oo=oo: e.tensor_add(
                t1[:, :], ps[g % 2][:, :], x1b[oo % 2][:, :]),
                inc="ae",
                wait=W(("mg", g + 1), ("xd%d" % (oo % 2), 16 * (oo // 2 + 1)),
                       ("st", 16 * oo)))
            p.op("vector", lambda e, oo=oo: e.tensor_add(
                t1[:, :], t1[:, :], a1b[:, :]),
                inc="va", wait=W(("ad", 16 * (oo + 1))))
            p.op("gpsimd", lambda e, co=co, tb=tb, oo=oo: e.dma_start(
                out=outd[co * 128:(co + 1) * 128, tb * 512:(tb + 1) * 512],
                in_=t1[:, :]), inc="st", incn=16,
                wait=W(("va", oo + 1)))
        if tb + 1 < NTB:
            emit_ldtb(tb + 1)
    p.wait("gpsimd", "st", 16 * NTB * NCH)
    return p.build()


MASKL = None
IDENT = None


def _consts():
    global MASKL, IDENT
    if MASKL is None:
        MASKL = np.where(np.arange(128)[None, :] >= np.arange(128)[:, None],
                         0.0, -1e4).astype(np.float32)
        IDENT = np.eye(128, dtype=np.float32)
    return MASKL, IDENT


def chain_l0attn(inputs, upto="a0"):
    maskL, ident = _consts()
    x = inputs["L0_attn_x"]
    g = lambda k: np.ascontiguousarray(inputs[k]).astype(np.float32)
    st = {}
    nc = build_pre()
    xT = shard_rows(x)
    r1 = run_launch(nc, [{"xT": s_} for s_ in xT])
    st["xn0"] = [r["xn0"] for r in r1]
    st["vy"] = [r["vy"] for r in r1]
    if upto == "pre":
        return st
    nc = build_aqkv()
    ins = [{"xn0": st["xn0"][c], "vy": st["vy"][c],
            "q_w": g("L0_attn_q_w"), "k_w": g("L0_attn_k_w"),
            "v_w": g("L0_attn_v_w"), "dec_w": g("L0_attn_dec_w"),
            "q_b": g("L0_attn_q_b"), "k_b": g("L0_attn_k_b"),
            "v_b": g("L0_attn_v_b"), "dec_b": g("L0_attn_dec_b"),
            "ln1_w": g("L0_attn_ln1_w"), "ln1_b": g("L0_attn_ln1_b")}
           for c in range(NCORE)]
    r2 = run_launch(nc, ins)
    nc = build_lion()
    ins = []
    for c in range(NCORE):
        b, j = c // 2, c % 2
        heads = range(8 * j, 8 * j + 8)
        pq = np.concatenate([
            np.concatenate([r2[2 * b]["phiq"][h * 64:(h + 1) * 64, :],
                            r2[2 * b + 1]["phiq"][h * 64:(h + 1) * 64, :]],
                           axis=1) for h in heads], axis=0)
        pk = np.concatenate([
            np.concatenate([r2[2 * b]["phik"][h * 64:(h + 1) * 64, :],
                            r2[2 * b + 1]["phik"][h * 64:(h + 1) * 64, :]],
                           axis=1) for h in heads], axis=0)
        vt = np.concatenate([
            np.concatenate([r2[2 * b]["vtok"][:, h * 64:(h + 1) * 64],
                            r2[2 * b + 1]["vtok"][:, h * 64:(h + 1) * 64]],
                           axis=0) for h in heads], axis=0)
        ll = np.stack([np.concatenate([r2[2 * b]["loglam"][h],
                                       r2[2 * b + 1]["loglam"][h]])
                       for h in heads], axis=0)
        ins.append({"phiq": np.ascontiguousarray(pq),
                    "phik": np.ascontiguousarray(pk),
                    "vtok": np.ascontiguousarray(vt),
                    "loglam": np.ascontiguousarray(ll),
                    "maskL": maskL, "ident": ident})
    r3 = run_launch(nc, ins)
    if upto == "lion":
        st["r3"] = r3
        return st
    attn_half = []
    for c in range(NCORE):
        b, j = c // 2, c % 2
        rows = []
        for h in range(16):
            lc = 2 * b + h // 8
            hr = h % 8
            rows.append(r3[lc]["attn"][hr * 64:(hr + 1) * 64,
                                       j * TH:(j + 1) * TH])
        attn_half.append(np.ascontiguousarray(np.concatenate(rows, axis=0)))
    nc = build_aoln()
    ins = [{"attn": attn_half[c], "xn0": st["xn0"][c],
            "o_w": g("L0_attn_o_w"), "o_b": g("L0_attn_o_b"),
            "ln2_w": g("L0_attn_ln2_w"), "ln2_b": g("L0_attn_ln2_b")}
           for c in range(NCORE)]
    r4 = run_launch(nc, ins)
    nc = build_affn()
    ins = [{"h2a": r4[c]["h2a"], "xa": r4[c]["xa"],
            "f1_w": g("L0_attn_f1_w"), "f1_b": g("L0_attn_f1_b"),
            "f2_w": g("L0_attn_f2_w"), "f2_b": g("L0_attn_f2_b")}
           for c in range(NCORE)]
    r5 = run_launch(nc, ins)
    st["a0"] = [r["a0"] for r in r5]
    return st


def chain_l0ffn(inputs, st, upto="route"):
    maskL, ident = _consts()
    g = lambda k: np.ascontiguousarray(inputs[k]).astype(np.float32)
    nc = build_fqkv()
    ins = [{"xn0": st["xn0"][c], "vy": st["vy"][c],
            "wq": g("L0_ffn_wq"), "wk": g("L0_ffn_wk"), "wv": g("L0_ffn_wv"),
            "bq": g("L0_ffn_bq"), "bk": g("L0_ffn_bk"), "bv": g("L0_ffn_bv"),
            "ln1_g": g("L0_ffn_ln1_g"), "ln1_b": g("L0_ffn_ln1_b")}
           for c in range(NCORE)]
    r6 = run_launch(nc, ins)
    nc = build_fattn()
    ins = []
    for c in range(NCORE):
        b, j = c // 2, c % 2
        heads = range(8 * j, 8 * j + 8)
        qf = np.concatenate([
            np.concatenate([r6[2 * b]["qF"][h * 64:(h + 1) * 64, :],
                            r6[2 * b + 1]["qF"][h * 64:(h + 1) * 64, :]],
                           axis=1) for h in heads], axis=0)
        kf = np.concatenate([
            np.concatenate([r6[2 * b]["kF"][h * 64:(h + 1) * 64, :],
                            r6[2 * b + 1]["kF"][h * 64:(h + 1) * 64, :]],
                           axis=1) for h in heads], axis=0)
        vf = np.concatenate([
            np.concatenate([r6[2 * b]["vF"][:, h * 64:(h + 1) * 64],
                            r6[2 * b + 1]["vF"][:, h * 64:(h + 1) * 64]],
                           axis=0) for h in heads], axis=0)
        ins.append({"qF": np.ascontiguousarray(qf),
                    "kF": np.ascontiguousarray(kf),
                    "vF": np.ascontiguousarray(vf)})
    r7 = run_launch(nc, ins)
    attn_half = []
    for c in range(NCORE):
        b, j = c // 2, c % 2
        rows = []
        for h in range(16):
            lc = 2 * b + h // 8
            hr = h % 8
            rows.append(r7[lc]["attnF"][hr * 64:(hr + 1) * 64,
                                        j * TH:(j + 1) * TH])
        attn_half.append(np.ascontiguousarray(np.concatenate(rows, axis=0)))
    nc = build_aoln()
    ins = [{"attn": attn_half[c], "xn0": st["xn0"][c],
            "o_w": g("L0_ffn_wo"), "o_b": g("L0_ffn_bo"),
            "ln2_w": g("L0_ffn_ln2_g"), "ln2_b": g("L0_ffn_ln2_b")}
           for c in range(NCORE)]
    r8 = run_launch(nc, ins)
    st["xf"] = [r["xa"] for r in r8]
    st["h2f"] = [r["h2a"] for r in r8]
    if upto == "oln":
        return st
    nc = build_route()
    ins = [{"h2f": st["h2f"][c], "wr": g("L0_ffn_wr"), "ident": ident}
           for c in range(NCORE)]
    r8b = run_launch(nc, ins)
    for k in ("p1", "p2", "e1", "e2", "s1", "s2", "cnt"):
        st[k] = [r[k] for r in r8b]
    return st


def chain_moe(inputs, st, xT_shards):
    import ml_dtypes
    g = lambda k: np.ascontiguousarray(inputs[k]).astype(np.float32)
    CAP = 8192
    cnt_all = np.zeros((16, 8), np.float32)
    for c in range(NCORE):
        cnt_all[c] = st["cnt"][c][0]
        cnt_all[8 + c] = st["cnt"][c][1]
    nc = build_moegates()
    ins = [{"cnt_all": cnt_all,
            "pref": (np.arange(8) < c).astype(np.float32),
            "p1": st["p1"][c], "p2": st["p2"][c], "e1": st["e1"][c],
            "e2": st["e2"][c], "s1": st["s1"][c], "s2": st["s2"][c]}
           for c in range(NCORE)]
    r9 = run_launch(nc, ins)
    Xe = [np.zeros((D, CAP), ml_dtypes.bfloat16) for _ in range(8)]
    idxmaps = []
    for c in range(NCORE):
        h2 = st["h2f"][c]
        e1 = st["e1"][c][0].astype(np.int64)
        e2 = st["e2"][c][0].astype(np.int64)
        l1 = r9[c]["loc1"][0].astype(np.int64)
        l2 = r9[c]["loc2"][0].astype(np.int64)
        v1 = (l1 >= 0) & (l1 < CAP)
        v2 = (l2 >= 0) & (l2 < CAP)
        for e in range(8):
            m = (e1 == e) & v1
            Xe[e][:, l1[m]] = h2[:, m]
            m2 = (e2 == e) & v2
            Xe[e][:, l2[m2]] = h2[:, m2]
        idxmaps.append((e1, l1, v1, e2, l2, v2))
    nc = build_expert()
    ins = [{"Xe": Xe[c], "w1": g("L0_ffn_w1")[c], "b1": g("L0_ffn_b1")[c],
            "w2": g("L0_ffn_w2")[c], "b2": g("L0_ffn_b2")[c]}
           for c in range(8)]
    r10 = run_launch(nc, ins)
    Ye = [r["Ye"] for r in r10]
    nc = build_combine()
    ins = []
    for c in range(NCORE):
        e1, l1, v1, e2, l2, v2 = idxmaps[c]
        y1 = np.zeros((D, TH), ml_dtypes.bfloat16)
        y2 = np.zeros((D, TH), ml_dtypes.bfloat16)
        for e in range(8):
            m = (e1 == e) & v1
            y1[:, m] = Ye[e][:, l1[m]]
            m2 = (e2 == e) & v2
            y2[:, m2] = Ye[e][:, l2[m2]]
        ins.append({"xT": xT_shards[c], "a0": st["a0"][c], "xf": st["xf"][c],
                    "y1": y1, "y2": y2, "g1": r9[c]["g1"], "g2": r9[c]["g2"]})
    r11 = run_launch(nc, ins)
    st["x1"] = [r["x1"] for r in r11]
    st["xn1"] = [r["xn1"] for r in r11]
    st["vy1"] = [r["vy1"] for r in r11]
    return st


def chain_mamba(inputs, st):
    maskL, ident = _consts()
    import ml_dtypes
    g = lambda k: np.ascontiguousarray(inputs[k]).astype(np.float32)
    nc = build_mpre()
    inwT = np.ascontiguousarray(g("L1_attn_in_w").T)
    ins = [{"xn1": st["xn1"][c], "vy1": st["vy1"][c], "in_wT": inwT,
            "norm_w": g("L1_attn_norm_w"), "dt_bias": g("L1_attn_dt_bias"),
            "A_log": g("L1_attn_A_log")} for c in range(NCORE)]
    r12 = run_launch(nc, ins)
    nc = build_mconv()
    ins = []
    for c in range(NCORE):
        b, j = c // 2, c % 2
        if j == 1:
            halo = np.ascontiguousarray(r12[2 * b]["xbc"][:, TH - 3:TH])
        else:
            halo = np.zeros((2304, 3), ml_dtypes.bfloat16)
        ins.append({"xbc": r12[c]["xbc"], "halo": halo,
                    "conv_w": g("L1_attn_conv_w"),
                    "conv_b": g("L1_attn_conv_b"),
                    "dtr": r12[c]["dtr"], "ident": ident})
    r12b = run_launch(nc, ins)
    st["z"] = [r["z"] for r in r12]
    st["xsf"] = [r["xsf"] for r in r12b]
    nc = build_mamba()
    ins = []
    for c in range(NCORE):
        b, j = c // 2, c % 2
        dtx = np.concatenate([r12b[2 * b]["dtx"], r12b[2 * b + 1]["dtx"]],
                             axis=0)[:, j * 1024:(j + 1) * 1024]
        Bf = np.concatenate([r12b[2 * b]["Bf"], r12b[2 * b + 1]["Bf"]],
                            axis=1)
        Cf = np.concatenate([r12b[2 * b]["Cf"], r12b[2 * b + 1]["Cf"]],
                            axis=1)
        llm = np.concatenate([r12[2 * b]["llm"], r12[2 * b + 1]["llm"]],
                             axis=1)[16 * j:16 * j + 16, :]
        ins.append({"dtx": np.ascontiguousarray(dtx),
                    "Bf": np.ascontiguousarray(Bf),
                    "Cf": np.ascontiguousarray(Cf),
                    "llm": np.ascontiguousarray(llm),
                    "maskL": maskL, "ident": ident})
    r13 = run_launch(nc, ins)
    st["r13"] = r13
    st["yssm"] = []
    for c in range(NCORE):
        b, j = c // 2, c % 2
        rows = []
        for h in range(32):
            lc = 2 * b + h // 16
            hr = h % 16
            rows.append(r13[lc]["yssm"][hr * 64:(hr + 1) * 64,
                                        j * TH:(j + 1) * TH])
        st["yssm"].append(np.ascontiguousarray(np.concatenate(rows, axis=0)))
    return st


def chain_tail(inputs, st):
    maskL, ident = _consts()
    g = lambda k: np.ascontiguousarray(inputs[k]).astype(np.float32)
    ct = np.ascontiguousarray
    nc = build_mtail()
    Drep = np.repeat(g("L1_attn_D_param"), 64)
    ins = [{"yssm": st["yssm"][c], "xsf": st["xsf"][c], "z": st["z"][c],
            "Drep": Drep, "gnorm_w": g("L1_attn_gnorm_w"),
            "out_wT": ct(g("L1_attn_out_w").T),
            "sc1T": ct(g("L1_attn_scorer_w1").T),
            "sc2T": ct(g("L1_attn_scorer_w2").T),
            "summT": ct(g("L1_attn_summ_w").T),
            "qrT": ct(g("L1_attn_q_w").T)} for c in range(NCORE)]
    r14 = run_launch(nc, ins)
    nc = build_retr()
    gw = g("L1_attn_gate_w")
    ins = []
    for c in range(NCORE):
        b = c // 2
        scoresF = np.concatenate([r14[2 * b]["scores"],
                                  r14[2 * b + 1]["scores"]], axis=1)
        summF = np.concatenate([r14[2 * b]["summtok"],
                                r14[2 * b + 1]["summtok"]], axis=0)
        ins.append({"scoresF": ct(scoresF), "summF": ct(summF),
                    "y1024": r14[c]["y1024"], "qrf": r14[c]["qrf"],
                    "xn1": st["xn1"][c],
                    "k_wT": ct(g("L1_attn_k_w").T),
                    "v_wT": ct(g("L1_attn_v_w").T),
                    "gwAT": ct(gw[:, 0:D].T), "gwBT": ct(gw[:, D:2 * D].T),
                    "ident": ident})
    r15 = run_launch(nc, ins)
    nc = build_l1route()
    ins = [{"xn1": st["xn1"][c], "vy1": st["vy1"][c],
            "centsT": ct(g("L1_ffn_centroids").T),
            "w_routeT": ct(g("L1_ffn_w_route").T), "ident": ident}
           for c in range(NCORE)]
    r16a = run_launch(nc, ins)
    nc = build_l1ffn()
    w1 = g("L1_ffn_w1")
    w2 = g("L1_ffn_w2")
    w1cat = ct(np.concatenate([w1[e] for e in range(8)], axis=1))
    b1cat = ct(np.concatenate([g("L1_ffn_b1")[e] for e in range(8)]))
    w2cat = ct(np.concatenate([w2[e] for e in range(8)], axis=0))
    ins = [{"xn1": st["xn1"][c], "w1cat": w1cat, "b1cat": b1cat,
            "w2cat": w2cat, "b2": g("L1_ffn_b2"), "gatesF": r16a[c]["gatesF"],
            "x1": st["x1"][c], "a1": r15[c]["a1"]} for c in range(NCORE)]
    r16 = run_launch(nc, ins)
    st["out"] = [r["out"] for r in r16]
    return st


def kernel(**inputs):
    st = chain_l0attn(inputs, upto="a0")
    st = chain_l0ffn(inputs, st, upto="route")
    xT = shard_rows(inputs["L0_attn_x"])
    st = chain_moe(inputs, st, xT)
    st = chain_mamba(inputs, st)
    st = chain_tail(inputs, st)
    out = np.zeros((B, T, D), np.float32)
    for c in range(NCORE):
        b, j = c // 2, c % 2
        out[b, j * TH:(j + 1) * TH, :] = st["out"][c].T
    return out

